# version 04
# baseline (speedup 1.0000x reference)
import contextlib
import os
import numpy as np
import ml_dtypes
import concourse.bass as bass
import concourse.mybir as mybir
from concourse.bass_utils import run_bass_kernel_spmd

DEBUG = int(os.environ.get("KDBG", "0"))
F32 = mybir.dt.float32
BF16 = mybir.dt.bfloat16
AF = mybir.ActivationFunctionType
ALU = mybir.AluOpType
AX = mybir.AxisListType
bf = ml_dtypes.bfloat16

D = 2048
NH = 16
HD = 128
SQ = 1024
SK = 2048
NCF = 16
NTT = 8
NKT = 16
EXP_SHIFT = 4.0
ATT_SCALE = float(HD) ** -0.5
LN_EPS = 1e-5


class K:

    def __init__(self, nc, ctx, block):
        self.nc = nc
        self.ctx = ctx
        self.block = block
        self.sems = {}
        self.cnt = {}
        for name in ("dma", "dk", "pe", "ve", "sc", "gp", "cc"):
            self.sems[name] = ctx.enter_context(nc.semaphore(name))
            self.cnt[name] = 0

    def inc(self, instr, sem, k=1):
        instr.then_inc(self.sems[sem], k)
        self.cnt[sem] += k
        return self.cnt[sem]

    def wait(self, eng, sem, val=None):
        v = self.cnt[sem] if val is None else val
        if v > 0:
            eng.wait_ge(self.sems[sem], v)

    def new_slots(self, n):
        return [{"w": None, "r": []} for _ in range(n)]

    def slot_write_waits(self, eng, slot):
        for s, c in slot["r"]:
            eng.wait_ge(self.sems[s], c)

    def slot_wrote(self, slot, sem="pe"):
        slot["w"] = (sem, self.cnt[sem])
        slot["r"] = []

    def slot_read_waits(self, eng, slot):
        if slot["w"] is not None:
            s, c = slot["w"]
            eng.wait_ge(self.sems[s], c)

    def slot_did_read(self, slot, sem):
        slot["r"].append((sem, self.cnt[sem]))

    def dma_flush(self, s):
        s.wait_ge(self.sems["dma"], self.cnt["dma"])
        self.inc(s.nop(), "dk")

    def fence(self, eng, sems=("pe", "ve", "sc")):
        for s in sems:
            self.wait(eng, s)

    def sb(self, name, shape, dt):
        return self.ctx.enter_context(self.nc.sbuf_tensor(name, shape, dt))

    def ps(self, name, shape, dt=F32):
        return self.ctx.enter_context(self.nc.psum_tensor(name, shape, dt))

    def sync(self, fn):
        self.block.sync(fn)

    def tensor(self, fn):
        self.block.tensor(fn)

    def vector(self, fn):
        self.block.vector(fn)

    def scalar(self, fn):
        self.block.scalar(fn)

    def gpsimd(self, fn):
        self.block.gpsimd(fn)


def _fm(dram2d):
    return dram2d.rearrange("(c p) m -> p c m", p=128)


def emit_load_cast_transpose(k, dram_rows, n_tiles, stage, tokbf, ident_sb, pT, dst_fm,
                             dst_col0=0):
    cast_done = [0] * n_tiles
    tpg = [0] * (n_tiles * 4)
    evg = [0] * (n_tiles * 4)
    entry_ve = k.cnt["ve"]
    entry_pe = k.cnt["pe"]
    for t in range(n_tiles):
        def s_body(s, t=t):
            if t >= 2:
                s.wait_ge(k.sems["ve"], cast_done[t - 2])
            elif entry_ve > 0:
                s.wait_ge(k.sems["ve"], entry_ve)
            k.inc(s.dma_start(out=stage[:, t % 2, :],
                              in_=dram_rows[t * 128:(t + 1) * 128, :]), "dma", 16)
            k.dma_flush(s)
        k.sync(s_body)
        dk_now = k.cnt["dk"]

        def v_cast(v, t=t, dk_now=dk_now):
            v.wait_ge(k.sems["dk"], dk_now)
            if t >= 2:
                v.wait_ge(k.sems["pe"], tpg[(t - 2) * 4 + 3])
            elif entry_pe > 0:
                v.wait_ge(k.sems["pe"], entry_pe)
            cast_done[t] = k.inc(
                v.tensor_copy(out=tokbf[:, t % 2, :], in_=stage[:, t % 2, :]), "ve")
        k.vector(v_cast)

        for g in range(4):
            def t_body(tn, t=t, g=g):
                tn.wait_ge(k.sems["ve"], cast_done[t])
                gi = t * 4 + g
                if gi >= 2:
                    tn.wait_ge(k.sems["ve"], evg[gi - 2])
                elif entry_ve > 0:
                    tn.wait_ge(k.sems["ve"], entry_ve)
                last = None
                for i in range(4):
                    c = g * 4 + i
                    last = tn.transpose(pT[:, gi % 2, i * 128:(i + 1) * 128],
                                        tokbf[:, t % 2, c * 128:(c + 1) * 128],
                                        ident_sb[:, :])
                tpg[gi] = k.inc(last, "pe")
            k.tensor(t_body)

            def v_evac(v, t=t, g=g):
                gi = t * 4 + g
                v.wait_ge(k.sems["pe"], tpg[gi])
                evg[gi] = k.inc(v.tensor_copy(
                    out=dst_fm[:, g * 4:(g + 1) * 4,
                               dst_col0 + t * 128:dst_col0 + (t + 1) * 128],
                    in_=pT[:, gi % 2, 0:512].rearrange("p (i m) -> p i m", i=4)), "ve")
            k.vector(v_evac)

    def v_end(v):
        v.drain()
    k.vector(v_end)


def emit_ln_and_xfm(k, stream, x_fm, ln_sc, ident_sb, pT, xtokbf):
    stats, mv8, veps8, rstd8, negmr8 = (ln_sc["stats"], ln_sc["mv8"], ln_sc["veps8"],
                                        ln_sc["rstd8"], ln_sc["negmr8"])

    def v1(v):
        k.wait(v, "pe")
        for t in range(NTT):
            for g in range(4):
                v.bn_stats(out=stats[:, t, g * 6:(g + 1) * 6],
                           in_=stream[:, t, g * 512:(g + 1) * 512])
        v.drain()
        for t in range(NTT):
            v.bn_aggr(out=mv8[:, t, :], in_=stats[:, t, :])
        v.drain()
        last = None
        for t in range(NTT):
            last = v.tensor_scalar(out=veps8[:, t:t + 1], in0=mv8[:, t, 1:2],
                                   scalar1=LN_EPS, scalar2=None, op0=ALU.add)
        k.inc(last, "ve")
    k.vector(v1)
    ve_ready = k.cnt["ve"]

    def s1(sc):
        sc.wait_ge(k.sems["ve"], ve_ready)
        k.inc(sc.activation(out=rstd8[:, :], in_=veps8[:, :], func=AF.Sqrt), "sc")
    k.scalar(s1)
    sc_ready = k.cnt["sc"]

    def v2(v):
        v.wait_ge(k.sems["sc"], sc_ready)
        v.reciprocal(out=rstd8[:, :], in_=rstd8[:, :])
        v.drain()
        v.scalar_tensor_tensor(out=negmr8[:, :],
                               in0=mv8.rearrange("p t s -> p (t s)")[:, 0::2],
                               scalar=-1.0, in1=rstd8[:, :],
                               op0=ALU.mult, op1=ALU.mult)
        v.drain()
        for t in range(NTT):
            v.tensor_scalar(out=stream[:, t, :], in0=stream[:, t, :],
                            scalar1=rstd8[:, t:t + 1], scalar2=negmr8[:, t:t + 1],
                            op0=ALU.mult, op1=ALU.add)
        v.drain()
        for t in range(NTT):
            ins = v.tensor_copy(out=xtokbf[:, t, :], in_=stream[:, t, :])
            k.inc(ins, "ve")
    k.vector(v2)
    cast_base = k.cnt["ve"] - NTT

    tpg = [0] * (NTT * 4)
    evg = [0] * (NTT * 4)
    for t in range(NTT):
        for g in range(4):
            def t_body(tn, t=t, g=g):
                tn.wait_ge(k.sems["ve"], cast_base + t + 1)
                gi = t * 4 + g
                if gi >= 2:
                    tn.wait_ge(k.sems["ve"], evg[gi - 2])
                last = None
                for i in range(4):
                    c = g * 4 + i
                    last = tn.transpose(pT[:, gi % 2, i * 128:(i + 1) * 128],
                                        xtokbf[:, t, c * 128:(c + 1) * 128],
                                        ident_sb[:, :])
                tpg[gi] = k.inc(last, "pe")
            k.tensor(t_body)

            def v_evac(v, t=t, g=g):
                gi = t * 4 + g
                v.wait_ge(k.sems["pe"], tpg[gi])
                evg[gi] = k.inc(v.tensor_copy(
                    out=x_fm[:, g * 4:(g + 1) * 4, t * 128:(t + 1) * 128],
                    in_=pT[:, gi % 2, 0:512].rearrange("p (i m) -> p i m", i=4)), "ve")
            k.vector(v_evac)

    def v_end(v):
        v.drain()
    k.vector(v_end)


def build_l0_attn(nc, ctx, k, io, persist):
    ident_sb, onesr_sb, onesc_sb, actc_sb, pT = (
        persist["ident_sb"],
        persist["onesr_sb"], persist["onesc_sb"], persist["actc_sb"], persist["pT"])

    sub0 = contextlib.ExitStack()
    ao_fm = sub0.enter_context(nc.sbuf_tensor("a_aofm", [128, NCF, SQ], BF16))
    pS = sub0.enter_context(nc.psum_tensor("a_pS", [128, 2, 512], F32))
    pS_sl = k.new_slots(2)

    with contextlib.ExitStack() as sub:
        sbt = lambda n, s, d: sub.enter_context(nc.sbuf_tensor(n, s, d))
        xk_fm = sbt("a_xkfm", [128, NCF, SK], BF16)
        masks_sb = sbt("a_masks", [128, 32, 512], BF16)
        zrow = sbt("a_zrow", [1, 512], F32)
        zrec = sbt("a_zrec", [1, 512], F32)
        zbf = sbt("a_zbf", [1, 512], BF16)
        rzb = sbt("a_rzb", [128, 512], F32)
        xq_fm = xk_fm

        pZ = sub.enter_context(nc.psum_tensor("a_pZ", [128, 512], F32))
        pO = sub.enter_context(nc.psum_tensor("a_pO", [128, 512], F32))
        pC = sub.enter_context(nc.psum_tensor("a_pC", [128, 512], F32))

        def c_sync(s):
            k.inc(s.dma_start(out=ident_sb[:, :], in_=io["ident"][:, :]), "dma", 16)
            k.inc(s.dma_start(out=onesr_sb[:, :], in_=io["onesr"][:, :]), "dma", 16)
            k.inc(s.dma_start(out=onesc_sb[:, :], in_=io["onesc"][:, :]), "dma", 16)
            k.inc(s.dma_start(out=actc_sb[:, :], in_=io["actc"][:, :]), "dma", 16)
            k.inc(s.dma_start(out=masks_sb[:, :, :],
                              in_=io["masks"].rearrange("(m p) j -> p m j", p=128)),
                  "dma", 16)
            k.dma_flush(s)
        k.sync(c_sync)
        dk_consts = k.cnt["dk"]

        with contextlib.ExitStack() as sub_in:
            stage = sub_in.enter_context(nc.sbuf_tensor("a_stage", [128, 2, 2048], F32))
            tokbf = sub_in.enter_context(nc.sbuf_tensor("a_tokbf", [128, 2, 2048], BF16))
            emit_load_cast_transpose(k, io["xk"], NKT, stage, tokbf, ident_sb, pT, xk_fm)

        input_pe = k.cnt["pe"]
        input_ve = k.cnt["ve"]
        stage_w1 = sbt("a_stw1", [128, 2, NCF, 128], F32)
        wT3 = sbt("a_wt3", [128, 2, 3, NCF, 128], BF16)
        q_h = sbt("a_qh", [128, SQ], BF16)
        k_h = sbt("a_kh", [128, SK], BF16)
        v_h = sbt("a_vh", [128, NKT, HD], BF16)
        exp_buf = sbt("a_exp", [128, NKT, 512], BF16)

        sl_stw = k.new_slots(2)
        sl_wt3 = k.new_slots(2)
        head_pe_end = [0] * (NH + 1)
        for h in range(NH):
            wb = h % 2
            for wi, wname in enumerate(("wqT", "wkT", "wvT")):
                ssl = (h * 3 + wi) % 2

                def w_sync(s, h=h, wi=wi, wname=wname, ssl=ssl):
                    k.slot_write_waits(s, sl_stw[ssl])
                    k.inc(s.dma_start(
                        out=stage_w1[:, ssl, :, :],
                        in_=io[wname][:, h * 128:(h + 1) * 128]
                            .rearrange("(c p) m -> p c m", p=128)), "dma", 16)
                    k.dma_flush(s)
                k.sync(w_sync)
                k.slot_wrote(sl_stw[ssl], "dma")
                dk_w = k.cnt["dk"]

                def w_cast(v, h=h, wi=wi, wb=wb, ssl=ssl, dk_w=dk_w):
                    v.wait_ge(k.sems["dk"], dk_w)
                    if wi == 0:
                        k.slot_write_waits(v, sl_wt3[wb])
                    k.inc(v.tensor_copy(
                        out=wT3[:, wb, wi, :, :].rearrange("p c m -> p (c m)"),
                        in_=stage_w1[:, ssl, :, :].rearrange("p c m -> p (c m)")), "ve")
                k.vector(w_cast)
                k.slot_did_read(sl_stw[ssl], "ve")
            k.slot_wrote(sl_wt3[wb], "ve")
            sc_w = k.cnt["ve"]

            proj = ([("q", blk) for blk in range(2)] + [("kk", blk) for blk in range(4)]
                    + [("v", tk) for tk in range(NKT)])
            for kind, blk in proj:
                si = blk % 2

                def t_p(tn, h=h, kind=kind, blk=blk, si=si, wb=wb, sc_w=sc_w):
                    tn.wait_ge(k.sems["ve"], sc_w)
                    if h == 0:
                        tn.wait_ge(k.sems["dk"], dk_consts)
                    k.slot_write_waits(tn, pS_sl[si])
                    ins = None
                    for c in range(NCF):
                        if kind == "q":
                            ins = tn.matmul(pS[:, si, :], wT3[:, wb, 0, c, :],
                                            xq_fm[:, c, blk * 512:(blk + 1) * 512],
                                            start=(c == 0), stop=(c == NCF - 1))
                        elif kind == "kk":
                            ins = tn.matmul(pS[:, si, :], wT3[:, wb, 1, c, :],
                                            xk_fm[:, c, blk * 512:(blk + 1) * 512],
                                            start=(c == 0), stop=(c == NCF - 1))
                        else:
                            ins = tn.matmul(pS[:, si, 0:128],
                                            xk_fm[:, c, blk * 128:(blk + 1) * 128],
                                            wT3[:, wb, 2, c, :],
                                            start=(c == 0), stop=(c == NCF - 1))
                    k.inc(ins, "pe")
                k.tensor(t_p)
                k.slot_wrote(pS_sl[si])

                def v_p(v, kind=kind, blk=blk, si=si):
                    k.slot_read_waits(v, pS_sl[si])
                    if kind == "q":
                        ins = v.tensor_scalar(out=q_h[:, blk * 512:(blk + 1) * 512],
                                              in0=pS[:, si, :], scalar1=ATT_SCALE,
                                              scalar2=None, op0=ALU.mult)
                    elif kind == "kk":
                        ins = v.tensor_copy(out=k_h[:, blk * 512:(blk + 1) * 512],
                                            in_=pS[:, si, :])
                    else:
                        ins = v.tensor_copy(out=v_h[:, blk, :], in_=pS[:, si, 0:128])
                    k.inc(ins, "ve")
                k.vector(v_p)
                k.slot_did_read(pS_sl[si], "ve")
            k.slot_did_read(sl_wt3[wb], "pe")

            qkv_ve = k.cnt["ve"]

            for tqb in range(2):
                mask_ve = [0] * NKT

                for tk in range(NKT):
                    si = tk % 2

                    def t_s(tn, tk=tk, tqb=tqb, si=si, qkv_ve=qkv_ve, h=h):
                        if tk == 0 and tqb == 0:
                            tn.wait_ge(k.sems["ve"], qkv_ve)
                        k.slot_write_waits(tn, pS_sl[si])
                        k.inc(tn.matmul(pS[:, si, :], k_h[:, tk * 128:(tk + 1) * 128],
                                        q_h[:, tqb * 512:(tqb + 1) * 512],
                                        start=True, stop=True), "pe")
                    k.tensor(t_s)
                    k.slot_wrote(pS_sl[si])

                    def s_exp(sc, tk=tk, tqb=tqb, si=si, h=h):
                        k.slot_read_waits(sc, pS_sl[si])
                        if h == 0 and tqb == 0 and tk == 0:
                            sc.wait_ge(k.sems["dk"], dk_consts)
                        k.inc(sc.activation(
                            out=exp_buf[:, tk, :], in_=pS[:, si, :], func=AF.Exp), "sc")
                    k.scalar(s_exp)
                    k.slot_did_read(pS_sl[si], "sc")
                    sc_now = k.cnt["sc"]

                    def v_mask(v, tk=tk, tqb=tqb, sc_now=sc_now, h=h):
                        v.wait_ge(k.sems["sc"], sc_now)
                        if h == 0 and tqb == 0 and tk == 0:
                            v.wait_ge(k.sems["dk"], dk_consts)
                        mask_ve[tk] = k.inc(v.tensor_tensor(
                            out=exp_buf[:, tk, :], in0=exp_buf[:, tk, :],
                            in1=masks_sb[:, tqb * 16 + tk, :], op=ALU.mult), "ve")
                    k.vector(v_mask)

                def t_zav(tn, tqb=tqb, h=h, mask_all=None):
                    tn.wait_ge(k.sems["ve"], k.cnt["ve"])
                    for tk in range(NKT):
                        ins = tn.matmul(pZ[0:1, :], onesc_sb[:, :], exp_buf[:, tk, :],
                                        start=(tk == 0), stop=(tk == NKT - 1))
                    k.inc(ins, "pe")
                    for tk in range(NKT):
                        ins = tn.matmul(pO[:, :], v_h[:, tk, :], exp_buf[:, tk, :],
                                        start=(tk == 0), stop=(tk == NKT - 1))
                    k.inc(ins, "pe")
                k.tensor(t_zav)
                pe_zav = k.cnt["pe"]

                def v_z(v, pe_zav=pe_zav):
                    v.wait_ge(k.sems["pe"], pe_zav - 1)
                    v.tensor_copy(out=zrow[:, :], in_=pZ[0:1, :])
                    v.drain()
                    v.reciprocal(out=zrec[:, :], in_=zrow[:, :])
                    v.drain()
                    k.inc(v.tensor_copy(out=zbf[:, :], in_=zrec[:, :]), "ve")
                k.vector(v_z)
                ve_z = k.cnt["ve"]

                def t_bc(tn, ve_z=ve_z):
                    tn.wait_ge(k.sems["ve"], ve_z)
                    k.inc(tn.matmul(pC[:, :], onesr_sb[:, :], zbf[:, :],
                                    start=True, stop=True), "pe")
                k.tensor(t_bc)
                pe_bc = k.cnt["pe"]

                def v_norm(v, tqb=tqb, h=h, pe_bc=pe_bc):
                    v.wait_ge(k.sems["pe"], pe_bc)
                    v.tensor_copy(out=rzb[:, :], in_=pC[:, :])
                    v.drain()
                    k.inc(v.tensor_tensor(
                        out=ao_fm[:, h, tqb * 512:(tqb + 1) * 512],
                        in0=pO[:, :], in1=rzb[:, :], op=ALU.mult), "ve")
                k.vector(v_norm)

        heads_ve = k.cnt["ve"]

        if DEBUG in (4, 5):
            def dbg_sync2(s):
                k.wait(s, "ve")
                k.wait(s, "sc")
                if DEBUG == 4:
                    k.wait(s, "dma")
            k.sync(dbg_sync2)

        if DEBUG in (1, 2):
            def dbg_sync(s):
                k.wait(s, "ve")
                k.wait(s, "sc")
                k.inc(s.dma_start(out=io["dbg_q"][:, :], in_=q_h[:, :]), "dma", 16)
                k.inc(s.dma_start(out=io["dbg_k"][:, :], in_=k_h[:, :]), "dma", 16)
                k.inc(s.dma_start(out=io["dbg_v"][:, :],
                                  in_=v_h.rearrange("p a b -> p (a b)")), "dma", 16)
                k.inc(s.dma_start(out=io["dbg_exp"][:, :],
                                  in_=exp_buf.rearrange("p a b -> p (a b)")), "dma", 16)
                k.inc(s.dma_start(out=io["dbg_z"][0:1, :], in_=zrow[:, :]), "dma", 16)
                k.inc(s.dma_start(out=io["dbg_z"][1:2, :], in_=zrec[:, :]), "dma", 16)
                k.inc(s.dma_start(out=io["dbg_ao"][:, :],
                                  in_=ao_fm.rearrange("p a b -> p (a b)")), "dma", 16)
                s.wait_ge(k.sems["dma"], k.cnt["dma"])
            k.sync(dbg_sync)

    stream = ctx.enter_context(nc.sbuf_tensor("stream", [128, NTT, D], F32, side="right"))
    persist["stream"] = stream

    if True:
        def r_sync(s):
            k.wait(s, "pe")
            k.wait(s, "ve")
            k.wait(s, "sc")
            for t in range(NTT):
                k.inc(s.dma_start(out=stream[:, t, :],
                                  in_=io["xk"][t * 128:(t + 1) * 128, :]), "dma", 16)
            k.dma_flush(s)
        k.sync(r_sync)
        dk_res = k.cnt["dk"]

        with contextlib.ExitStack() as sub2:
            wo_st = sub2.enter_context(nc.sbuf_tensor("a_wost", [128, NCF, 512], F32))
            wo_bf = sub2.enter_context(nc.sbuf_tensor("a_wobf", [128, NCF, 512], BF16))
            for jb in range(4):
                pe_before_w = k.cnt["pe"]

                def wo_sync(s, jb=jb, pe_before_w=pe_before_w):
                    if jb == 0:
                        k.wait(s, "pe")
                        k.wait(s, "ve")
                        k.wait(s, "sc")
                    if jb > 0:
                        s.wait_ge(k.sems["pe"], pe_before_w)
                    k.inc(s.dma_start(
                        out=wo_st[:, :, :],
                        in_=_fm(io["woT"])[:, :, jb * 512:(jb + 1) * 512]), "dma", 16)
                    k.dma_flush(s)
                k.sync(wo_sync)
                dk_now = k.cnt["dk"]

                def wo_cast(sc, jb=jb, dk_now=dk_now, pe_before_w=pe_before_w):
                    sc.wait_ge(k.sems["dk"], dk_now)
                    if jb > 0:
                        sc.wait_ge(k.sems["pe"], pe_before_w)
                    k.inc(sc.copy(out=wo_bf.rearrange("p c m -> p (c m)"),
                                  in_=wo_st.rearrange("p c m -> p (c m)")), "sc")
                k.scalar(wo_cast)
                sc_now = k.cnt["sc"]

                for t in range(NTT):
                    si = t % 2

                    def t_wo(tn, t=t, jb=jb, si=si, sc_now=sc_now, heads_ve=heads_ve):
                        tn.wait_ge(k.sems["sc"], sc_now)
                        if t == 0 and jb == 0:
                            tn.wait_ge(k.sems["ve"], heads_ve)
                        k.slot_write_waits(tn, pS_sl[si])
                        ins = None
                        for c in range(NCF):
                            ins = tn.matmul(pS[:, si, :], ao_fm[:, c, t * 128:(t + 1) * 128],
                                            wo_bf[:, c, :],
                                            start=(c == 0), stop=(c == NCF - 1))
                        k.inc(ins, "pe")
                    k.tensor(t_wo)
                    k.slot_wrote(pS_sl[si])

                    def v_wo(v, t=t, jb=jb, si=si, dk_res=dk_res):
                        k.slot_read_waits(v, pS_sl[si])
                        v.wait_ge(k.sems["dk"], dk_res)
                        k.inc(v.tensor_tensor(
                            out=stream[:, t, jb * 512:(jb + 1) * 512],
                            in0=stream[:, t, jb * 512:(jb + 1) * 512],
                            in1=pS[:, si, :], op=ALU.add), "ve")
                    k.vector(v_wo)
                    k.slot_did_read(pS_sl[si], "ve")

            def v_fin(v):
                v.drain()
            k.vector(v_fin)

    if DEBUG in (1, 3):
        def dbg_res_sync(s):
            k.wait(s, "ve")
            for t in range(NTT):
                k.inc(s.dma_start(out=io["dbg_res"][t * 128:(t + 1) * 128, :],
                                  in_=stream[:, t, :]), "dma", 16)
            s.wait_ge(k.sems["dma"], k.cnt["dma"])
        k.sync(dbg_res_sync)

    sub0.close()

    x_fm = ctx.enter_context(nc.sbuf_tensor("x_fm", [128, NCF, SQ], BF16, side="right"))
    persist["x_fm"] = x_fm
    if True:
        with contextlib.ExitStack() as sub3:
            ln_sc = {
                "stats": sub3.enter_context(nc.sbuf_tensor("ln_stats", [128, NTT, 24], F32)),
                "mv8": sub3.enter_context(nc.sbuf_tensor("ln_mv8", [128, NTT, 2], F32)),
                "veps8": sub3.enter_context(nc.sbuf_tensor("ln_veps", [128, NTT], F32)),
                "rstd8": sub3.enter_context(nc.sbuf_tensor("ln_rstd", [128, NTT], F32)),
                "negmr8": sub3.enter_context(nc.sbuf_tensor("ln_negmr", [128, NTT], F32)),
            }
            xtokbf = sub3.enter_context(nc.sbuf_tensor("ln_tokbf", [128, NTT, 2048], BF16))
            emit_ln_and_xfm(k, stream, x_fm, ln_sc, ident_sb, pT, xtokbf)


def emit_standalone_entry(k, nc, ctx, io, persist, xname):
    stream = ctx.enter_context(nc.sbuf_tensor("stream", [128, NTT, D], F32, side="right"))
    x_fm = ctx.enter_context(nc.sbuf_tensor("x_fm", [128, NCF, SQ], BF16, side="right"))
    persist["stream"] = stream
    persist["x_fm"] = x_fm
    ident_sb, onesr_sb, onesc_sb, actc_sb = (persist["ident_sb"], persist["onesr_sb"],
                                             persist["onesc_sb"], persist["actc_sb"])

    def c_sync(s):
        k.inc(s.dma_start(out=ident_sb[:, :], in_=io["ident"][:, :]), "dma", 16)
        k.inc(s.dma_start(out=onesr_sb[:, :], in_=io["onesr"][:, :]), "dma", 16)
        k.inc(s.dma_start(out=onesc_sb[:, :], in_=io["onesc"][:, :]), "dma", 16)
        k.inc(s.dma_start(out=actc_sb[:, :], in_=io["actc"][:, :]), "dma", 16)
        for t in range(NTT):
            k.inc(s.dma_start(out=stream[:, t, :],
                              in_=io[xname][t * 128:(t + 1) * 128, :]), "dma", 16)
        k.dma_flush(s)
    k.sync(c_sync)

    with contextlib.ExitStack() as sub_in:
        stage = sub_in.enter_context(nc.sbuf_tensor("e_stage", [128, 2, 2048], F32))
        tokbf = sub_in.enter_context(nc.sbuf_tensor("e_tokbf", [128, 2, 2048], BF16))
        emit_load_cast_transpose(k, io[xname], NTT, stage, tokbf, ident_sb,
                                 persist["pT"], x_fm)


def build_moe(nc, ctx, k, io, persist):
    stream, x_fm, ident_sb, pT = (persist["stream"], persist["x_fm"],
                                  persist["ident_sb"], persist["pT"])
    NE, NI, NIT = 32, 768, 6
    NSC, NSI = 16, 128

    wg_v = io["Wg"].rearrange("e (c p) i -> e p c i", p=128)
    wu_v = io["Wu"].rearrange("e (c p) i -> e p c i", p=128)
    wd_v = io["Wd"].rearrange("e (ic p) h -> e p ic h", p=128)
    cwg_v = io["cWg"].rearrange("e (c p) i -> e p c i", p=128)
    cwu_v = io["cWu"].rearrange("e (c p) i -> e p c i", p=128)
    cwd_v = io["cWd"].rearrange("e (ic p) h -> e p ic h", p=128)

    with contextlib.ExitStack() as sub:
        sbt = lambda n, s, d: sub.enter_context(nc.sbuf_tensor(n, s, d))
        lg = sbt("m_lg", [128, NTT, 32], F32)
        ew = sbt("m_ew", [128, NTT, 32], F32)
        acc = sbt("m_acc", [128, NTT, 32], F32)
        work = sbt("m_work", [128, NTT, 32], F32)
        eqb = sbt("m_eq", [128, NTT, 32], F32)
        cw = sbt("m_cw", [128, NTT, 32], F32)
        cws = sbt("m_cws", [128, NTT, 16], F32)
        sc8 = sbt("m_sc8", [128, 5, NTT], F32)

        pG = sub.enter_context(nc.psum_tensor("m_pG", [128, 2, 512], F32))
        pU = sub.enter_context(nc.psum_tensor("m_pU", [128, 2, 512], F32))
        pY = sub.enter_context(nc.psum_tensor("m_pY", [128, 2, 512], F32))

        sl_wst = k.new_slots(2)
        sl_gu = k.new_slots(2)
        sl_y = k.new_slots(2)
        sl_a = k.new_slots(1)
        sl_silu = k.new_slots(2)
        sl_w = {"wg": k.new_slots(NIT), "wu": k.new_slots(NIT), "wd": k.new_slots(8)}

        sub_rt = contextlib.ExitStack()
        gw_st = sub_rt.enter_context(nc.sbuf_tensor("m_gwst", [128, NCF, 32], F32))
        gw_bf = sub_rt.enter_context(nc.sbuf_tensor("m_gwbf", [128, NCF, 32], BF16))

        def rt_sync(s):
            k.fence(s)
            k.inc(s.dma_start(out=gw_st[:, :, :],
                              in_=io["gate_wT"].rearrange("(c p) m -> p c m", p=128)),
                  "dma", 16)
            k.dma_flush(s)
        k.sync(rt_sync)
        dk_now = k.cnt["dk"]

        def rt_cast(sc, dk_now=dk_now):
            sc.wait_ge(k.sems["dk"], dk_now)
            k.fence(sc, sems=("pe", "ve"))
            k.inc(sc.copy(out=gw_bf.rearrange("p c m -> p (c m)"),
                          in_=gw_st.rearrange("p c m -> p (c m)")), "sc")
        k.scalar(rt_cast)
        sc_gw = k.cnt["sc"]

        for t in range(NTT):
            si = t % 2

            def rt_mm(tn, t=t, si=si, sc_gw=sc_gw):
                tn.wait_ge(k.sems["sc"], sc_gw)
                k.slot_write_waits(tn, sl_y[si])
                ins = None
                for c in range(NCF):
                    ins = tn.matmul(pY[:, si, 0:32], x_fm[:, c, t * 128:(t + 1) * 128],
                                    gw_bf[:, c, :], start=(c == 0), stop=(c == NCF - 1))
                k.inc(ins, "pe")
            k.tensor(rt_mm)
            k.slot_wrote(pY if False else sl_y[si])

            def rt_ev(v, t=t, si=si):
                k.slot_read_waits(v, sl_y[si])
                k.inc(v.tensor_copy(out=lg[:, t, :], in_=pY[:, si, 0:32]), "ve")
            k.vector(rt_ev)
            k.slot_did_read(sl_y[si], "ve")

        def rt_max(v):
            v.drain()
            last = None
            for t in range(NTT):
                last = v.reduce_max(out=sc8[:, 0, t:t + 1], in_=lg[:, t, :],
                                    axis=AX.X, negate=True)
            k.inc(last, "ve")
        k.vector(rt_max)
        ve_nm = k.cnt["ve"]

        def rt_exp(sc, ve_nm=ve_nm):
            sc.wait_ge(k.sems["ve"], ve_nm)
            last = None
            for t in range(NTT):
                last = sc.activation(out=ew[:, t, :], in_=lg[:, t, :], func=AF.Exp,
                                     bias=sc8[:, 0, t:t + 1], scale=1.0)
            k.inc(last, "sc")
        k.scalar(rt_exp)
        sc_ew = k.cnt["sc"]

        def rt_topk(v, sc_ew=sc_ew):
            for t in range(NTT):
                v.tensor_copy(out=work[:, t, :], in_=lg[:, t, :])
            v.memset(acc.rearrange("p a b -> p (a b)"), 0.0)
            v.wait_ge(k.sems["sc"], sc_ew)
            v.drain()
            for kk in range(4):
                for t in range(NTT):
                    v.reduce_max(out=sc8[:, 1, t:t + 1], in_=work[:, t, :], axis=AX.X)
                v.drain()
                for t in range(NTT):
                    v.tensor_scalar(out=eqb[:, t, :], in0=work[:, t, :],
                                    scalar1=sc8[:, 1, t:t + 1], scalar2=None,
                                    op0=ALU.is_equal)
                v.drain()
                for t in range(NTT):
                    v.scalar_tensor_tensor(out=lg[:, t, :], in0=ew[:, t, :],
                                           scalar=1.0, in1=eqb[:, t, :],
                                           op0=ALU.mult, op1=ALU.mult,
                                           accum_out=sc8[:, 2, t:t + 1])
                v.drain()
                for t in range(NTT):
                    v.scalar_tensor_tensor(out=acc[:, t, :], in0=eqb[:, t, :],
                                           scalar=sc8[:, 2, t:t + 1], in1=acc[:, t, :],
                                           op0=ALU.mult, op1=ALU.add)
                for t in range(NTT):
                    v.scalar_tensor_tensor(out=work[:, t, :], in0=eqb[:, t, :],
                                           scalar=-1e30, in1=work[:, t, :],
                                           op0=ALU.mult, op1=ALU.add)
                v.drain()
            for t in range(NTT):
                v.reduce_sum(out=sc8[:, 3, t:t + 1], in_=acc[:, t, :], axis=AX.X)
            v.drain()
            v.reciprocal(out=sc8[:, 4, :], in_=sc8[:, 3, :])
            v.drain()
            for t in range(NTT):
                v.tensor_scalar(out=cw[:, t, :], in0=acc[:, t, :],
                                scalar1=sc8[:, 4, t:t + 1], scalar2=None, op0=ALU.mult)
            v.drain()
            cwv = cw.rearrange("p t (e two) -> p t e two", two=2)
            for t in range(NTT):
                v.tensor_tensor(out=cws[:, t, :], in0=cwv[:, t, :, 0],
                                in1=cwv[:, t, :, 1], op=ALU.add)
            v.drain()
            last = v.tensor_scalar(out=cws.rearrange("p a b -> p (a b)"),
                                   in0=cws.rearrange("p a b -> p (a b)"),
                                   scalar1=0.05, scalar2=None, op0=ALU.mult)
            k.inc(last, "ve")
            v.drain()
        k.vector(rt_topk)
        ve_cw = k.cnt["ve"]
        sub_rt.close()

        wst = sbt("m_wst", [128, 2, 2048], F32)
        wg_bf = sbt("m_wg", [128, NCF, NI], BF16)
        wu_bf = sbt("m_wu", [128, NCF, NI], BF16)
        wd_bf = sbt("m_wd", [128, NIT, 2048], BF16)
        a_bf = sbt("m_a", [128, 1, NIT, SQ], BF16)
        silu_sb = sbt("m_silu", [128, 2, 512], F32)

        def expert_pass(e, small):
            n_it = 1 if small else NIT
            sa = 0
            g_view = cwg_v if small else wg_v
            u_view = cwu_v if small else wu_v
            d_view = cwd_v if small else wd_v

            marks = {}
            grains = []
            if small:
                grains.append(("wg", 0, g_view[e, :, :, :], (NCF, NSI)))
                grains.append(("wu", 0, u_view[e, :, :, :], (NCF, NSI)))
                grains.append(("wd", 0, d_view[e, :, :, :], (1, 2048)))
            else:
                for gi in range(NIT):
                    grains.append(("wg", gi, wg_v[e, :, :, gi * 128:(gi + 1) * 128],
                                   (NCF, 128)))
                for gi in range(NIT):
                    grains.append(("wu", gi, wu_v[e, :, :, gi * 128:(gi + 1) * 128],
                                   (NCF, 128)))
                for gi in range(8):
                    grains.append(("wd", gi, wd_v[e, :, :, gi * 256:(gi + 1) * 256],
                                   (NIT, 256)))
            for gn, (wname, gi, view, shp) in enumerate(grains):
                ssl = gn % 2
                nel = shp[0] * shp[1]
                first_all = (e == 0 and gn == 0 and not small)

                def g_sync(s, view=view, ssl=ssl, shp=shp, nel=nel, first_all=first_all):
                    if first_all:
                        k.fence(s)
                    k.slot_write_waits(s, sl_wst[ssl])
                    k.inc(s.dma_start(
                        out=wst[:, ssl, 0:nel].rearrange("p (a b) -> p a b", a=shp[0]),
                        in_=view), "dma", 16)
                    k.dma_flush(s)
                k.sync(g_sync)
                k.slot_wrote(sl_wst[ssl], "dma")
                dk_g = k.cnt["dk"]

                def g_cast(sc, wname=wname, gi=gi, ssl=ssl, shp=shp, nel=nel,
                           dk_g=dk_g, small=small):
                    sc.wait_ge(k.sems["dk"], dk_g)
                    if small and wname == "wd":
                        for sl in sl_w["wd"]:
                            k.slot_write_waits(sc, sl)
                    else:
                        k.slot_write_waits(sc, sl_w[wname][gi])
                    src = wst[:, ssl, 0:nel].rearrange("p (a b) -> p a b", a=shp[0])
                    if wname == "wg":
                        dst = wg_bf[:, :, gi * 128:gi * 128 + shp[1]]
                    elif wname == "wu":
                        dst = wu_bf[:, :, gi * 128:gi * 128 + shp[1]]
                    else:
                        dst = wd_bf[:, 0:shp[0], gi * 256:gi * 256 + shp[1]] if not small \
                            else wd_bf[:, 0:shp[0], :]
                    k.inc(sc.copy(out=dst, in_=src), "sc")
                k.scalar(g_cast)
                marks[(wname, gi)] = k.cnt["sc"]
                k.slot_did_read(sl_wst[ssl], "sc")
                k.slot_wrote(sl_w[wname][gi], "sc")

            for it in range(n_it):
                for th in range(2):
                    gu = (it * 2 + th) % 2

                    def t_gu(tn, it=it, th=th, gu=gu,
                             mg=marks[("wg", 0 if small else it)],
                             mu=marks[("wu", 0 if small else it)]):
                        tn.wait_ge(k.sems["sc"], max(mg, mu))
                        k.slot_write_waits(tn, sl_gu[gu])
                        for c in range(NCF):
                            tn.matmul(pG[:, gu, :], wg_bf[:, c, it * 128:(it + 1) * 128],
                                      x_fm[:, c, th * 512:(th + 1) * 512],
                                      start=(c == 0), stop=(c == NCF - 1))
                        ins = None
                        for c in range(NCF):
                            ins = tn.matmul(pU[:, gu, :], wu_bf[:, c, it * 128:(it + 1) * 128],
                                            x_fm[:, c, th * 512:(th + 1) * 512],
                                            start=(c == 0), stop=(c == NCF - 1))
                        k.inc(ins, "pe")
                    k.tensor(t_gu)
                    k.slot_wrote(sl_gu[gu])
                    k.slot_did_read(sl_w["wg"][0 if small else it], "pe")
                    k.slot_did_read(sl_w["wu"][0 if small else it], "pe")

                    def s_silu(sc, gu=gu):
                        k.slot_read_waits(sc, sl_gu[gu])
                        k.slot_write_waits(sc, sl_silu[gu])
                        k.inc(sc.activation(out=silu_sb[:, gu, :], in_=pG[:, gu, :],
                                            func=AF.Silu), "sc")
                    k.scalar(s_silu)
                    k.slot_did_read(sl_gu[gu], "sc")
                    k.slot_wrote(sl_silu[gu], "sc")

                    def v_a(v, it=it, th=th, gu=gu, sa=sa, first=(it == 0 and th == 0)):
                        k.slot_read_waits(v, sl_silu[gu])
                        if first:
                            v.wait_ge(k.sems["ve"], ve_cw)
                            k.slot_write_waits(v, sl_a[sa])
                        k.inc(v.tensor_tensor(out=a_bf[:, sa, it, th * 512:(th + 1) * 512],
                                              in0=silu_sb[:, gu, :], in1=pU[:, gu, :],
                                              op=ALU.mult), "ve")
                    k.vector(v_a)
                    k.slot_did_read(sl_silu[gu], "ve")
                    k.slot_did_read(sl_gu[gu], "ve")
            if small:
                k.slot_wrote(sl_a[sa], "ve")
            else:
                k.slot_wrote(sl_a[sa], "ve")
            a_done = k.cnt["ve"]

            for t in range(NTT):
                for hb in range(4):
                    y = (t * 4 + hb) % 2

                    def t_dn(tn, t=t, hb=hb, y=y, sa=sa, a_done=a_done,
                             md=marks[("wd", 0 if small else 2 * hb + 1)], n_it=n_it):
                        tn.wait_ge(k.sems["sc"], md)
                        tn.wait_ge(k.sems["ve"], a_done)
                        k.slot_write_waits(tn, sl_y[y])
                        ins = None
                        for ic in range(n_it):
                            ins = tn.matmul(pY[:, y, :], a_bf[:, sa, ic, t * 128:(t + 1) * 128],
                                            wd_bf[:, ic, hb * 512:(hb + 1) * 512],
                                            start=(ic == 0), stop=(ic == n_it - 1))
                        k.inc(ins, "pe")
                    k.tensor(t_dn)
                    k.slot_wrote(sl_y[y])
                    if small:
                        k.slot_did_read(sl_w["wd"][0], "pe")
                    else:
                        k.slot_did_read(sl_w["wd"][2 * hb], "pe")
                        k.slot_did_read(sl_w["wd"][2 * hb + 1], "pe")

                    def v_dn(v, t=t, hb=hb, y=y, e=e, small=small):
                        k.slot_read_waits(v, sl_y[y])
                        if e == 0 and not small and t == 0 and hb == 0:
                            k.wait(v, "dk")
                        col = cws[:, t, e:e + 1] if small else cw[:, t, e:e + 1]
                        k.inc(v.scalar_tensor_tensor(
                            out=stream[:, t, hb * 512:(hb + 1) * 512],
                            in0=pY[:, y, :], scalar=col,
                            in1=stream[:, t, hb * 512:(hb + 1) * 512],
                            op0=ALU.mult, op1=ALU.add), "ve")
                    k.vector(v_dn)
                    k.slot_did_read(sl_y[y], "ve")

            def v_edrain(v):
                v.drain()
            k.vector(v_edrain)
            k.slot_did_read(sl_a[sa], "pe")

        for e in range(NE):
            expert_pass(e, small=False)
        for e in range(NSC):
            expert_pass(e, small=True)

    with contextlib.ExitStack() as sub3:
        ln_sc = {
            "stats": sub3.enter_context(nc.sbuf_tensor("ln2_stats", [128, NTT, 24], F32)),
            "mv8": sub3.enter_context(nc.sbuf_tensor("ln2_mv8", [128, NTT, 2], F32)),
            "veps8": sub3.enter_context(nc.sbuf_tensor("ln2_veps", [128, NTT], F32)),
            "rstd8": sub3.enter_context(nc.sbuf_tensor("ln2_rstd", [128, NTT], F32)),
            "negmr8": sub3.enter_context(nc.sbuf_tensor("ln2_negmr", [128, NTT], F32)),
        }
        xtokbf = sub3.enter_context(nc.sbuf_tensor("ln2_tokbf", [128, NTT, 2048], BF16))
        emit_ln_and_xfm(k, stream, x_fm, ln_sc, ident_sb, pT, xtokbf)


def build_l1attn(nc, ctx, k, io, persist):
    stream, x_fm, ident_sb, pT = (persist["stream"], persist["x_fm"],
                                  persist["ident_sb"], persist["pT"])
    IB = 4

    with contextlib.ExitStack() as sub:
        sbt = lambda n, s, d: sub.enter_context(nc.sbuf_tensor(n, s, d))
        x2_all = sbt("s_x2all", [128, NKT, 2048], BF16)
        gcol = sbt("s_gcol", [128, 1], F32)

        def v_own(v):
            k.wait(v, "pe")
            last = None
            for t in range(NTT):
                last = v.tensor_copy(out=x2_all[:, t, :], in_=stream[:, t, :])
            k.inc(last, "ve")
            v.drain()
        k.vector(v_own)

        part_cast = [0] * NTT
        if "x2o" in io:
            with contextlib.ExitStack() as sub_in:
                stage = sub_in.enter_context(nc.sbuf_tensor("s_stage", [128, 2, 2048], F32))
                tokbf = sub_in.enter_context(nc.sbuf_tensor("s_tokbf", [128, 2, 2048], BF16))
                for t in range(NTT):
                    def s_ld(s, t=t):
                        if t < 2:
                            k.fence(s)
                        else:
                            s.wait_ge(k.sems["ve"], part_cast[t - 2])
                        k.inc(s.dma_start(out=stage[:, t % 2, :],
                                          in_=io["x2o"][t * 128:(t + 1) * 128, :]),
                              "dma", 16)
                        k.dma_flush(s)
                    k.sync(s_ld)
                    dk_now = k.cnt["dk"]

                    def v_pc(v, t=t, dk_now=dk_now):
                        v.wait_ge(k.sems["dk"], dk_now)
                        part_cast[t] = k.inc(v.tensor_copy(
                            out=x2_all[:, NTT + t, :], in_=stage[:, t % 2, :]), "ve")
                    k.vector(v_pc)
        else:
            cc_in = io["cc_in"]
            cc_out = io["cc_out"]

            def s_ccout(s):
                k.wait(s, "ve")
                for t in range(NTT):
                    k.inc(s.dma_start(out=cc_in[t * 128:(t + 1) * 128, :],
                                      in_=x2_all[:, t, :]), "dma", 16)
                k.dma_flush(s)
            k.sync(s_ccout)
            dk_cc = k.cnt["dk"]

            def g_cc(g, dk_cc=dk_cc):
                g.wait_ge(k.sems["dk"], dk_cc)
                k.inc(g.collective_compute(
                    "AllGather", ALU.bypass,
                    replica_groups=[[0, 1], [2, 3], [4, 5], [6, 7]],
                    ins=[cc_in.tensor.ap().opt()],
                    outs=[cc_out.tensor.ap().opt()]), "cc")
            k.gpsimd(g_cc)
            cc_done = k.cnt["cc"]

            def s_ccin(s, cc_done=cc_done):
                s.wait_ge(k.sems["cc"], cc_done)
                last = None
                for t in range(2 * NTT):
                    last = k.inc(s.dma_start(out=x2_all[:, t, :],
                                             in_=cc_out[t * 128:(t + 1) * 128, :]),
                                 "dma", 16)
                k.dma_flush(s)
            k.sync(s_ccin)

        def s_gamma(s):
            k.inc(s.dma_start(out=gcol[:, :], in_=io["gammac"][:, :]), "dma", 16)
            k.dma_flush(s)
        k.sync(s_gamma)
        x2_ready_ve = k.cnt["ve"]
        x2_ready_dma = k.cnt["dk"]

        def v_double(v):
            v.wait_ge(k.sems["ve"], x2_ready_ve)
            last = None
            for t in range(NTT):
                last = v.tensor_scalar(out=stream[:, t, :], in0=stream[:, t, :],
                                       scalar1=2.0, scalar2=None, op0=ALU.mult)
            k.inc(last, "ve")
            v.drain()
        k.vector(v_double)

        with contextlib.ExitStack() as sub2:
            erow = sub2.enter_context(nc.sbuf_tensor("s_erow", [128, 1, 2048], F32))
            attn_blk = sub2.enter_context(nc.sbuf_tensor("s_attn", [128, 4, 2048], BF16))
            attnT = sub2.enter_context(nc.sbuf_tensor("s_attnT", [128, NCF, 512], BF16))
            zsc = sub2.enter_context(nc.sbuf_tensor("s_zsc", [128, 3], F32))
            pE = sub2.enter_context(nc.psum_tensor("s_pE", [128, 2, 512], F32))
            pO2 = sub2.enter_context(nc.psum_tensor("s_pO2", [128, 2, 512], F32))
            sl_e = k.new_slots(2)
            sl_o = k.new_slots(2)
            sl_er = k.new_slots(1)
            sl_attnT = k.new_slots(1)[0]
            sl_ab = k.new_slots(1)[0]

            for ib in range(IB):
                for itl in range(4):
                    i0 = ib * 512 + itl * 128
                    esl = 0
                    for jb in range(4):
                        ps = (itl * 4 + jb) % 2

                        def t_e(tn, i0=i0, jb=jb, ps=ps, dma0=x2_ready_dma,
                                ve0=x2_ready_ve, first=(ib == 0 and itl == 0 and jb == 0)):
                            if first:
                                k.fence(tn)
                            tn.wait_ge(k.sems["ve"], ve0)
                            tn.wait_ge(k.sems["dk"], dma0)
                            k.slot_write_waits(tn, sl_e[ps])
                            ins = None
                            for sc_ in range(NKT):
                                ins = tn.matmul(pE[:, ps, :],
                                                x2_all[:, sc_, i0:i0 + 128],
                                                x2_all[:, sc_, jb * 512:(jb + 1) * 512],
                                                start=(sc_ == 0), stop=(sc_ == NKT - 1))
                            k.inc(ins, "pe")
                        k.tensor(t_e)
                        k.slot_wrote(sl_e[ps])

                        def v_ee(v, itl=itl, jb=jb, ps=ps, esl=esl):
                            k.slot_read_waits(v, sl_e[ps])
                            if jb == 0:
                                k.slot_write_waits(v, sl_er[esl])
                            k.inc(v.tensor_copy(out=erow[:, esl, jb * 512:(jb + 1) * 512],
                                                in_=pE[:, ps, :]), "ve")
                        k.vector(v_ee)
                        k.slot_did_read(sl_e[ps], "ve")
                    k.slot_wrote(sl_er[esl], "ve")

                    def v_mx(v, esl=esl):
                        k.slot_read_waits(v, sl_er[esl])
                        k.wait(v, "sc")
                        v.drain()
                        k.inc(v.reduce_max(out=zsc[:, 0:1], in_=erow[:, esl, :],
                                           axis=AX.X, negate=True), "ve")
                    k.vector(v_mx)
                    ve_mx = k.cnt["ve"]

                    def s_ex(sc, esl=esl, ve_mx=ve_mx):
                        sc.wait_ge(k.sems["ve"], ve_mx)
                        k.inc(sc.activation(out=erow[:, esl, :], in_=erow[:, esl, :],
                                            func=AF.Exp, bias=zsc[:, 0:1], scale=1.0,
                                            accum_out=zsc[:, 1:2]), "sc")
                    k.scalar(s_ex)
                    sc_ex = k.cnt["sc"]

                    def v_at(v, itl=itl, esl=esl, sc_ex=sc_ex):
                        v.wait_ge(k.sems["sc"], sc_ex)
                        v.reciprocal(out=zsc[:, 2:3], in_=zsc[:, 1:2])
                        v.drain()
                        if itl == 0:
                            k.slot_write_waits(v, sl_ab)
                        ins = v.tensor_scalar(out=attn_blk[:, itl, :],
                                              in0=erow[:, esl, :],
                                              scalar1=zsc[:, 2:3], scalar2=None,
                                              op0=ALU.mult)
                        k.inc(ins, "ve")
                    k.vector(v_at)
                    k.slot_did_read(sl_er[esl], "ve")
                    k.slot_did_read(sl_er[esl], "sc")
                k.slot_wrote(sl_ab, "ve")
                attn_ve = k.cnt["ve"]

                tpg = [0] * NCF
                evg = [0] * NCF
                for jt in range(NKT):
                    def t_tp(tn, jt=jt, attn_ve=attn_ve):
                        if jt == 0:
                            tn.wait_ge(k.sems["ve"], attn_ve)
                        if jt >= 2:
                            tn.wait_ge(k.sems["ve"], evg[jt - 2])
                        last = None
                        for itl in range(4):
                            last = tn.transpose(pT[:, jt % 2, itl * 128:(itl + 1) * 128],
                                                attn_blk[:, itl, jt * 128:(jt + 1) * 128],
                                                ident_sb[:, :])
                        tpg[jt] = k.inc(last, "pe")
                    k.tensor(t_tp)

                    def v_tp(v, jt=jt):
                        v.wait_ge(k.sems["pe"], tpg[jt])
                        if jt == 0:
                            k.slot_write_waits(v, sl_attnT)
                        evg[jt] = k.inc(v.tensor_copy(
                            out=attnT[:, jt, :],
                            in_=pT[:, jt % 2, 0:512].rearrange("p (i m) -> p i m", i=4)
                                .rearrange("p i m -> p (i m)")), "ve")
                    k.vector(v_tp)
                k.slot_wrote(sl_attnT, "ve")
                k.slot_did_read(sl_ab, "pe")
                tp_ve = k.cnt["ve"]

                for t in range(NTT):
                    osl = t % 2

                    def t_o(tn, t=t, ib=ib, osl=osl, tp_ve=tp_ve):
                        if t == 0:
                            tn.wait_ge(k.sems["ve"], tp_ve)
                        k.slot_write_waits(tn, sl_o[osl])
                        ins = None
                        for jc in range(NCF):
                            ins = tn.matmul(pO2[:, osl, :], x_fm[:, jc, t * 128:(t + 1) * 128],
                                            attnT[:, jc, :],
                                            start=(jc == 0), stop=(jc == NCF - 1))
                        k.inc(ins, "pe")
                    k.tensor(t_o)
                    k.slot_wrote(sl_o[osl])

                    def v_o(v, t=t, ib=ib, osl=osl, dma0=x2_ready_dma):
                        k.slot_read_waits(v, sl_o[osl])
                        if t == 0 and ib == 0:
                            v.wait_ge(k.sems["dk"], dma0)
                        k.inc(v.scalar_tensor_tensor(
                            out=stream[:, t, ib * 512:(ib + 1) * 512],
                            in0=pO2[:, osl, :], scalar=gcol[:, :],
                            in1=stream[:, t, ib * 512:(ib + 1) * 512],
                            op0=ALU.mult, op1=ALU.add), "ve")
                    k.vector(v_o)
                    k.slot_did_read(sl_o[osl], "ve")
                k.slot_did_read(sl_attnT, "pe")
                def v_bdrain(v):
                    v.drain()
                k.vector(v_bdrain)

    with contextlib.ExitStack() as sub3:
        ln_sc = {
            "stats": sub3.enter_context(nc.sbuf_tensor("ln3_stats", [128, NTT, 24], F32)),
            "mv8": sub3.enter_context(nc.sbuf_tensor("ln3_mv8", [128, NTT, 2], F32)),
            "veps8": sub3.enter_context(nc.sbuf_tensor("ln3_veps", [128, NTT], F32)),
            "rstd8": sub3.enter_context(nc.sbuf_tensor("ln3_rstd", [128, NTT], F32)),
            "negmr8": sub3.enter_context(nc.sbuf_tensor("ln3_negmr", [128, NTT], F32)),
        }
        xtokbf = sub3.enter_context(nc.sbuf_tensor("ln3_tokbf", [128, NTT, 2048], BF16))
        emit_ln_and_xfm(k, stream, x_fm, ln_sc, ident_sb, pT, xtokbf)


def build_l1ffn(nc, ctx, k, io, persist):
    stream, x_fm, ident_sb, onesr_sb, pT = (persist["stream"], persist["x_fm"],
                                            persist["ident_sb"], persist["onesr_sb"],
                                            persist["pT"])
    NAR = 1024
    b2_v = io["B2"].rearrange("(ac p) j -> p ac j", p=128)
    wt_v = io["WT"].rearrange("(c p) j -> p c j", p=128)

    with contextlib.ExitStack() as sub:
        sbt = lambda n, s, d: sub.enter_context(nc.sbuf_tensor(n, s, d))
        c_fm = sbt("f_cfm", [128, 8, SQ], BF16)
        pV = sub.enter_context(nc.psum_tensor("f_pV", [128, 2, 512], F32))
        sl_v = k.new_slots(2)
        subx = contextlib.ExitStack()
        vecs_sb = subx.enter_context(nc.sbuf_tensor("f_vecs", [128, NTT, NAR], F32))

        with contextlib.ExitStack() as suba:
            a2t_st = suba.enter_context(nc.sbuf_tensor("f_a2st", [128, NCF, 256], F32))
            a2t_bf = suba.enter_context(nc.sbuf_tensor("f_a2bf", [128, NCF, NAR], BF16))
            sl_a2st = k.new_slots(1)
            cast_marks = []
            for gi in range(4):
                def s_a2(s, gi=gi):
                    if gi == 0:
                        k.fence(s)
                    k.slot_write_waits(s, sl_a2st[0])
                    k.inc(s.dma_start(
                        out=a2t_st[:, :, :],
                        in_=_fm(io["A2T"])[:, :, gi * 256:(gi + 1) * 256]), "dma", 16)
                    k.dma_flush(s)
                k.sync(s_a2)
                k.slot_wrote(sl_a2st[0], "dma")
                dk_g = k.cnt["dk"]

                def sc_a2(sc, gi=gi, dk_g=dk_g):
                    sc.wait_ge(k.sems["dk"], dk_g)
                    if gi == 0:
                        k.fence(sc, sems=("pe", "ve"))
                    k.inc(sc.copy(out=a2t_bf[:, :, gi * 256:(gi + 1) * 256],
                                  in_=a2t_st[:, :, :]), "sc")
                k.scalar(sc_a2)
                cast_marks.append(k.cnt["sc"])
                k.slot_did_read(sl_a2st[0], "sc")

            for t in range(NTT):
                for vb in range(2):
                    ps = (t * 2 + vb) % 2

                    def t_v(tn, t=t, vb=vb, ps=ps, m=cast_marks[-1]):
                        tn.wait_ge(k.sems["sc"], m)
                        k.slot_write_waits(tn, sl_v[ps])
                        ins = None
                        for c in range(NCF):
                            ins = tn.matmul(pV[:, ps, :], x_fm[:, c, t * 128:(t + 1) * 128],
                                            a2t_bf[:, c, vb * 512:(vb + 1) * 512],
                                            start=(c == 0), stop=(c == NCF - 1))
                        k.inc(ins, "pe")
                    k.tensor(t_v)
                    k.slot_wrote(sl_v[ps])

                    def v_ve(v, t=t, vb=vb, ps=ps):
                        k.slot_read_waits(v, sl_v[ps])
                        k.inc(v.tensor_copy(out=vecs_sb[:, t, vb * 512:(vb + 1) * 512],
                                            in_=pV[:, ps, :]), "ve")
                    k.vector(v_ve)
                    k.slot_did_read(sl_v[ps], "ve")
        vecs_ve = k.cnt["ve"]

        subr = contextlib.ExitStack()
        sq = subr.enter_context(nc.sbuf_tensor("f_sq", [128, NTT, NAR], F32))
        arr = subr.enter_context(nc.sbuf_tensor("f_arr", [128, NTT, 64], F32))
        routes = subr.enter_context(nc.sbuf_tensor("f_routes", [128, NTT, 64], F32))
        eqb = subr.enter_context(nc.sbuf_tensor("f_eq", [128, NTT, 64], F32))
        v16 = subr.enter_context(nc.sbuf_tensor("f_v16", [128, NTT, 16], F32))
        wk8 = subr.enter_context(nc.sbuf_tensor("f_wk8", [128, NTT], F32))
        c_bf = subr.enter_context(nc.sbuf_tensor("f_cbf", [128, NTT, NAR], BF16))

        def sc_sq(sc, vecs_ve=vecs_ve):
            sc.wait_ge(k.sems["ve"], vecs_ve)
            k.fence(sc, sems=("pe", "ve"))
            last = None
            for t in range(NTT):
                last = sc.activation(out=sq[:, t, :], in_=vecs_sb[:, t, :],
                                     func=AF.Square)
            k.inc(last, "sc")
        k.scalar(sc_sq)
        sc_sq_c = k.cnt["sc"]

        def v_route(v, sc_sq_c=sc_sq_c):
            v.wait_ge(k.sems["sc"], sc_sq_c)
            sqv = sq.rearrange("p t (a r) -> p t a r", r=16)
            for width in (8, 4, 2, 1):
                for t in range(NTT):
                    v.tensor_tensor(out=sqv[:, t, :, 0:width],
                                    in0=sqv[:, t, :, 0:width],
                                    in1=sqv[:, t, :, width:2 * width], op=ALU.add)
                v.drain()
            for t in range(NTT):
                v.tensor_copy(out=arr[:, t, :], in_=sqv[:, t, :, 0])
            v.memset(routes.rearrange("p a b -> p (a b)"), 0.0)
            v.drain()
            for kk in range(4):
                for t in range(NTT):
                    v.reduce_max(out=wk8[:, t:t + 1], in_=arr[:, t, :], axis=AX.X)
                v.drain()
                for t in range(NTT):
                    v.tensor_scalar(out=eqb[:, t, :], in0=arr[:, t, :],
                                    scalar1=wk8[:, t:t + 1], scalar2=None,
                                    op0=ALU.is_equal)
                v.drain()
                for t in range(NTT):
                    v.tensor_tensor(out=routes[:, t, :], in0=routes[:, t, :],
                                    in1=eqb[:, t, :], op=ALU.add)
                for t in range(NTT):
                    v.scalar_tensor_tensor(out=arr[:, t, :], in0=eqb[:, t, :],
                                           scalar=-1e30, in1=arr[:, t, :],
                                           op0=ALU.mult, op1=ALU.add)
                v.drain()
            last = v.tensor_scalar(out=routes.rearrange("p a b -> p (a b)"),
                                   in0=routes.rearrange("p a b -> p (a b)"),
                                   scalar1=0.25, scalar2=None, op0=ALU.mult)
            v.drain()
            for t in range(NTT):
                v.tensor_tensor(out=sq[:, t, :], in0=vecs_sb[:, t, :],
                                in1=routes[:, t, :].unsqueeze(2)
                                    .broadcast_to([128, 64, 16]), op=ALU.mult)
            v.drain()
            for width in (512, 256, 128, 64, 32, 16):
                for t in range(NTT):
                    v.tensor_tensor(out=sq[:, t, 0:width], in0=sq[:, t, 0:width],
                                    in1=sq[:, t, width:2 * width], op=ALU.add)
                v.drain()
            for t in range(NTT):
                v.tensor_copy(out=v16[:, t, :], in_=sq[:, t, 0:16])
            v.drain()
            last = None
            for t in range(NTT):
                last = v.tensor_tensor(
                    out=c_bf[:, t, :],
                    in0=routes[:, t, :].unsqueeze(2).broadcast_to([128, 64, 16]),
                    in1=v16[:, t, :].unsqueeze(1).broadcast_to([128, 64, 16]),
                    op=ALU.mult)
            k.inc(last, "ve")
            v.drain()
        k.vector(v_route)
        c_ve = k.cnt["ve"]

        tpg = [0] * (NTT * 2)
        evg = [0] * (NTT * 2)
        for t in range(NTT):
            for g in range(2):
                gi = t * 2 + g

                def t_c(tn, t=t, g=g, gi=gi, c_ve=c_ve):
                    if gi == 0:
                        tn.wait_ge(k.sems["ve"], c_ve)
                    if gi >= 2:
                        tn.wait_ge(k.sems["ve"], evg[gi - 2])
                    last = None
                    for i in range(4):
                        ac = g * 4 + i
                        last = tn.transpose(pT[:, gi % 2, i * 128:(i + 1) * 128],
                                            c_bf[:, t, ac * 128:(ac + 1) * 128],
                                            ident_sb[:, :])
                    tpg[gi] = k.inc(last, "pe")
                k.tensor(t_c)

                def v_c(v, t=t, g=g, gi=gi):
                    v.wait_ge(k.sems["pe"], tpg[gi])
                    evg[gi] = k.inc(v.tensor_copy(
                        out=c_fm[:, g * 4:(g + 1) * 4, t * 128:(t + 1) * 128],
                        in_=pT[:, gi % 2, 0:512].rearrange("p (i m) -> p i m", i=4)), "ve")
                k.vector(v_c)
        cfm_ve = k.cnt["ve"]
        subr.close()
        subx.close()

        with contextlib.ExitStack() as subo:
            w_st = subo.enter_context(nc.sbuf_tensor("f_wst", [128, NCF, 512], F32))
            w_bf = subo.enter_context(nc.sbuf_tensor("f_wbf", [128, NCF, 512], BF16))
            b2_st = subo.enter_context(nc.sbuf_tensor("f_b2st", [128, 8, 512], F32))
            b2_bf = subo.enter_context(nc.sbuf_tensor("f_b2bf", [128, 8, 512], BF16))
            biasrow = subo.enter_context(nc.sbuf_tensor("f_bias", [1, 512], F32))
            biasbf = subo.enter_context(nc.sbuf_tensor("f_biasbf", [1, 512], BF16))
            sl_wb = k.new_slots(1)
            for jb in range(4):
                pe_prev = k.cnt["pe"]

                def s_w(s, jb=jb):
                    if jb == 0:
                        k.fence(s)
                    k.slot_write_waits(s, sl_wb[0])
                    k.wait(s, "sc")
                    k.inc(s.dma_start(out=w_st[:, :, :],
                                      in_=wt_v[:, :, jb * 512:(jb + 1) * 512]), "dma", 16)
                    k.inc(s.dma_start(out=b2_st[:, :, :],
                                      in_=b2_v[:, :, jb * 512:(jb + 1) * 512]), "dma", 16)
                    k.inc(s.dma_start(out=biasrow[:, :],
                                      in_=io["bias"][:, jb * 512:(jb + 1) * 512]), "dma", 16)
                    k.dma_flush(s)
                k.sync(s_w)
                dk_w = k.cnt["dk"]

                def sc_w(sc, jb=jb, dk_w=dk_w, pe_prev=pe_prev):
                    sc.wait_ge(k.sems["dk"], dk_w)
                    if jb > 0:
                        sc.wait_ge(k.sems["pe"], pe_prev)
                    sc.copy(out=w_bf.rearrange("p c m -> p (c m)"),
                            in_=w_st.rearrange("p c m -> p (c m)"))
                    sc.copy(out=b2_bf.rearrange("p c m -> p (c m)"),
                            in_=b2_st.rearrange("p c m -> p (c m)"))
                    k.inc(sc.copy(out=biasbf[:, :], in_=biasrow[:, :]), "sc")
                k.scalar(sc_w)
                sc_wm = k.cnt["sc"]

                for t in range(NTT):
                    ps = t % 2

                    def t_out(tn, t=t, jb=jb, ps=ps, sc_wm=sc_wm, cfm_ve=cfm_ve):
                        tn.wait_ge(k.sems["sc"], sc_wm)
                        if t == 0 and jb == 0:
                            tn.wait_ge(k.sems["ve"], cfm_ve)
                        k.slot_write_waits(tn, sl_v[ps])
                        for c in range(NCF):
                            tn.matmul(pV[:, ps, :], x_fm[:, c, t * 128:(t + 1) * 128],
                                      w_bf[:, c, :], start=(c == 0), stop=False)
                        for ac in range(8):
                            tn.matmul(pV[:, ps, :], c_fm[:, ac, t * 128:(t + 1) * 128],
                                      b2_bf[:, ac, :], start=False, stop=False)
                        ins = tn.matmul(pV[:, ps, :], onesr_sb[:, :],
                                        biasbf[:, :],
                                        start=False, stop=True)
                        k.inc(ins, "pe")
                    k.tensor(t_out)
                    k.slot_wrote(sl_v[ps])

                    def v_out(v, t=t, jb=jb, ps=ps):
                        k.slot_read_waits(v, sl_v[ps])
                        k.inc(v.tensor_tensor(
                            out=stream[:, t, jb * 512:(jb + 1) * 512],
                            in0=stream[:, t, jb * 512:(jb + 1) * 512],
                            in1=pV[:, ps, :], op=ALU.add), "ve")
                    k.vector(v_out)
                    k.slot_did_read(sl_v[ps], "ve")
                k.slot_did_read(sl_wb[0], "pe")
            def v_fdrain(v):
                v.drain()
            k.vector(v_fdrain)

    with contextlib.ExitStack() as sub3:
        ln_sc = {
            "stats": sub3.enter_context(nc.sbuf_tensor("ln4_stats", [128, NTT, 24], F32)),
            "mv8": sub3.enter_context(nc.sbuf_tensor("ln4_mv8", [128, NTT, 2], F32)),
            "veps8": sub3.enter_context(nc.sbuf_tensor("ln4_veps", [128, NTT], F32)),
            "rstd8": sub3.enter_context(nc.sbuf_tensor("ln4_rstd", [128, NTT], F32)),
            "negmr8": sub3.enter_context(nc.sbuf_tensor("ln4_negmr", [128, NTT], F32)),
        }
        emit_ln_apply_only(k, stream, ln_sc)


def emit_ln_apply_only(k, stream, ln_sc):
    stats, mv8, veps8, rstd8, negmr8 = (ln_sc["stats"], ln_sc["mv8"], ln_sc["veps8"],
                                        ln_sc["rstd8"], ln_sc["negmr8"])

    def v1(v):
        k.wait(v, "pe")
        for t in range(NTT):
            for g in range(4):
                v.bn_stats(out=stats[:, t, g * 6:(g + 1) * 6],
                           in_=stream[:, t, g * 512:(g + 1) * 512])
        v.drain()
        for t in range(NTT):
            v.bn_aggr(out=mv8[:, t, :], in_=stats[:, t, :])
        v.drain()
        last = None
        for t in range(NTT):
            last = v.tensor_scalar(out=veps8[:, t:t + 1], in0=mv8[:, t, 1:2],
                                   scalar1=LN_EPS, scalar2=None, op0=ALU.add)
        k.inc(last, "ve")
    k.vector(v1)
    ve_ready = k.cnt["ve"]

    def s1(sc):
        sc.wait_ge(k.sems["ve"], ve_ready)
        k.inc(sc.activation(out=rstd8[:, :], in_=veps8[:, :], func=AF.Sqrt), "sc")
    k.scalar(s1)
    sc_ready = k.cnt["sc"]

    def v2(v):
        v.wait_ge(k.sems["sc"], sc_ready)
        v.reciprocal(out=rstd8[:, :], in_=rstd8[:, :])
        v.drain()
        v.scalar_tensor_tensor(out=negmr8[:, :],
                               in0=mv8.rearrange("p t s -> p (t s)")[:, 0::2],
                               scalar=-1.0, in1=rstd8[:, :],
                               op0=ALU.mult, op1=ALU.mult)
        v.drain()
        last = None
        for t in range(NTT):
            last = v.tensor_scalar(out=stream[:, t, :], in0=stream[:, t, :],
                                   scalar1=rstd8[:, t:t + 1], scalar2=negmr8[:, t:t + 1],
                                   op0=ALU.mult, op1=ALU.add)
        k.inc(last, "ve")
        v.drain()
    k.vector(v2)


def emit_store_stream(k, stream, out_dram):
    def s_out(s):
        k.wait(s, "ve")
        for t in range(NTT):
            k.inc(s.dma_start(out=out_dram[t * 128:(t + 1) * 128, :],
                              in_=stream[:, t, :]), "dma", 16)
        k.wait(s, "dma")
    k.sync(s_out)


_PROG_CACHE = {}


def build_program(blocks):
    nc = bass.Bass("TRN2", target_bir_lowering=False)
    io = {}
    if "l0attn" in blocks:
        io["xk"] = nc.dram_tensor("xk", [SK, D], F32, kind="ExternalInput").ap()
        io["wqT"] = nc.dram_tensor("wqT", [D, D], F32, kind="ExternalInput").ap()
        io["wkT"] = nc.dram_tensor("wkT", [D, D], F32, kind="ExternalInput").ap()
        io["wvT"] = nc.dram_tensor("wvT", [D, D], F32, kind="ExternalInput").ap()
        io["woT"] = nc.dram_tensor("woT", [D, D], F32, kind="ExternalInput").ap()
        io["masks"] = nc.dram_tensor("masks", [32 * 128, 512], BF16, kind="ExternalInput").ap()
    if "moe" in blocks:
        if "l0attn" not in blocks:
            io["x1"] = nc.dram_tensor("x1", [SQ, D], F32, kind="ExternalInput").ap()
        io["gate_wT"] = nc.dram_tensor("gate_wT", [D, 32], F32, kind="ExternalInput").ap()
        io["Wg"] = nc.dram_tensor("Wg", [32, D, 768], F32, kind="ExternalInput").ap()
        io["Wu"] = nc.dram_tensor("Wu", [32, D, 768], F32, kind="ExternalInput").ap()
        io["Wd"] = nc.dram_tensor("Wd", [32, 768, D], F32, kind="ExternalInput").ap()
        io["cWg"] = nc.dram_tensor("cWg", [16, D, 128], F32, kind="ExternalInput").ap()
        io["cWu"] = nc.dram_tensor("cWu", [16, D, 128], F32, kind="ExternalInput").ap()
        io["cWd"] = nc.dram_tensor("cWd", [16, 128, D], F32, kind="ExternalInput").ap()
    if "l1attn" in blocks:
        if blocks[0] == "l1attn":
            io["x2"] = nc.dram_tensor("x2", [SQ, D], F32, kind="ExternalInput").ap()
            io["x2o"] = nc.dram_tensor("x2o", [SQ, D], F32, kind="ExternalInput").ap()
        else:
            io["cc_in"] = nc.dram_tensor("cc_in", [SQ, D], BF16).ap()
            io["cc_out"] = nc.dram_tensor("cc_out", [SK, D], BF16).ap()
        io["gammac"] = nc.dram_tensor("gammac", [128, 1], F32, kind="ExternalInput").ap()
    if "l1ffn" in blocks:
        if blocks[0] == "l1ffn":
            io["x3"] = nc.dram_tensor("x3", [SQ, D], F32, kind="ExternalInput").ap()
        io["A2T"] = nc.dram_tensor("A2T", [D, 1024], F32, kind="ExternalInput").ap()
        io["B2"] = nc.dram_tensor("B2", [1024, D], F32, kind="ExternalInput").ap()
        io["WT"] = nc.dram_tensor("WT", [D, D], F32, kind="ExternalInput").ap()
        io["bias"] = nc.dram_tensor("bias", [1, D], F32, kind="ExternalInput").ap()
    io["ident"] = nc.dram_tensor("ident", [128, 128], BF16, kind="ExternalInput").ap()
    io["actc"] = nc.dram_tensor("actc", [128, 2], F32, kind="ExternalInput").ap()
    io["onesr"] = nc.dram_tensor("onesr", [1, 128], BF16, kind="ExternalInput").ap()
    io["onesc"] = nc.dram_tensor("onesc", [128, 1], BF16, kind="ExternalInput").ap()
    out = nc.dram_tensor("out", [SQ, D], F32, kind="ExternalOutput").ap()
    if DEBUG:
        io["dbg_q"] = nc.dram_tensor("dbg_q", [128, SQ], BF16, kind="ExternalOutput").ap()
        io["dbg_k"] = nc.dram_tensor("dbg_k", [128, SK], BF16, kind="ExternalOutput").ap()
        io["dbg_v"] = nc.dram_tensor("dbg_v", [128, NKT * HD], BF16, kind="ExternalOutput").ap()
        io["dbg_exp"] = nc.dram_tensor("dbg_exp", [128, NKT * 512], BF16, kind="ExternalOutput").ap()
        io["dbg_z"] = nc.dram_tensor("dbg_z", [2, 512], F32, kind="ExternalOutput").ap()
        io["dbg_ao"] = nc.dram_tensor("dbg_ao", [128, NCF * SQ], BF16, kind="ExternalOutput").ap()
        io["dbg_res"] = nc.dram_tensor("dbg_res", [SQ, D], F32, kind="ExternalOutput").ap()
        io["dbg_xfm"] = nc.dram_tensor("dbg_xfm", [128, 2048], BF16, kind="ExternalOutput").ap()

    with contextlib.ExitStack() as ctx:
        block = ctx.enter_context(nc.Block())
        k = K(nc, ctx, block)
        persist = {
            "ident_sb": k.sb("ident_sb", [128, 128], BF16),
            "onesr_sb": k.sb("onesr_sb", [1, 128], BF16),
            "onesc_sb": k.sb("onesc_sb", [128, 1], BF16),
            "actc_sb": k.sb("actc_sb", [128, 2], F32),
            "pT": k.ps("pT", [128, 2, 1024], BF16),
        }
        for bi, b in enumerate(blocks):
            if b == "l0attn":
                build_l0_attn(nc, ctx, k, io, persist)
            elif b == "moe":
                if bi == 0:
                    emit_standalone_entry(k, nc, ctx, io, persist, "x1")
                build_moe(nc, ctx, k, io, persist)
            elif b == "l1attn":
                if bi == 0:
                    emit_standalone_entry(k, nc, ctx, io, persist, "x2")
                build_l1attn(nc, ctx, k, io, persist)
            elif b == "l1ffn":
                if bi == 0:
                    emit_standalone_entry(k, nc, ctx, io, persist, "x3")
                build_l1ffn(nc, ctx, k, io, persist)
        emit_store_stream(k, persist["stream"], out)
    return nc


def _prep_l0_inputs(inputs, core):
    b, half = core // 2, core % 2
    T0 = half * SQ
    x = np.asarray(inputs["L0_attn_x"])
    d = {}
    perm = np.concatenate([np.arange(T0, T0 + SQ), np.arange((1 - half) * SQ, (2 - half) * SQ)])
    d["xk"] = np.ascontiguousarray(x[b][perm])
    d["wqT"] = np.ascontiguousarray(np.asarray(inputs["L0_attn_wq"]).T)
    d["wkT"] = np.ascontiguousarray(np.asarray(inputs["L0_attn_wk"]).T)
    d["wvT"] = np.ascontiguousarray(np.asarray(inputs["L0_attn_wv"]).T)
    d["woT"] = np.ascontiguousarray(np.asarray(inputs["L0_attn_wo"]).T)
    kg = perm.reshape(NKT, 128)
    qj = (np.arange(2)[:, None] * 512 + np.arange(512)[None, :] + T0)
    m = (kg[None, :, :, None] - qj[:, None, None, :]) <= 128
    d["masks"] = np.ascontiguousarray(
        m.astype(bf).reshape(2 * NKT * 128, 512))
    d.update(_prep_consts())
    return d


def _prep_consts():
    return {
        "ident": np.eye(128, dtype=bf),
        "actc": np.tile(np.array([[ATT_SCALE, -EXP_SHIFT]], np.float32), (128, 1)),
        "onesr": np.ones((1, 128), dtype=bf),
        "onesc": np.ones((128, 1), dtype=bf),
    }


def _prep_l1attn_inputs(inputs, core, x2=None):
    d = {}
    b, half = core // 2, core % 2
    if x2 is not None:
        d["x2"] = np.ascontiguousarray(x2[b, half * SQ:(half + 1) * SQ])
        d["x2o"] = np.ascontiguousarray(x2[b, (1 - half) * SQ:(2 - half) * SQ])
    g = float(np.asarray(inputs["L1_attn_gamma"]).reshape(-1)[0])
    d["gammac"] = np.full((128, 1), g, np.float32)
    d.update(_prep_consts())
    return d


def _prep_l1ffn_inputs(inputs, core, x3=None):
    d = {}
    b, half = core // 2, core % 2
    if x3 is not None:
        d["x3"] = np.ascontiguousarray(x3[b, half * SQ:(half + 1) * SQ])
    A = np.asarray(inputs["L1_ffn_A"])
    d["A2T"] = np.ascontiguousarray(A.reshape(1024, 2048).T)
    B = np.asarray(inputs["L1_ffn_B"])
    d["B2"] = np.ascontiguousarray(B.transpose(0, 2, 1).reshape(1024, 2048))
    d["WT"] = np.ascontiguousarray(np.asarray(inputs["L1_ffn_W"]).T)
    d["bias"] = np.ascontiguousarray(np.asarray(inputs["L1_ffn_bias"]).reshape(1, -1))
    d.update(_prep_consts())
    return d


def _prep_moe_inputs(inputs, core, x1=None):
    d = {}
    if x1 is not None:
        b, half = core // 2, core % 2
        d["x1"] = np.ascontiguousarray(x1[b, half * SQ:(half + 1) * SQ])
    d["gate_wT"] = np.ascontiguousarray(np.asarray(inputs["L0_ffn_gate_w"]).T)
    d["Wg"] = np.ascontiguousarray(np.asarray(inputs["L0_ffn_Wg"]))
    d["Wu"] = np.ascontiguousarray(np.asarray(inputs["L0_ffn_Wu"]))
    d["Wd"] = np.ascontiguousarray(np.asarray(inputs["L0_ffn_Wd"]))
    d["cWg"] = np.ascontiguousarray(np.asarray(inputs["L0_ffn_cWg"]))
    d["cWu"] = np.ascontiguousarray(np.asarray(inputs["L0_ffn_cWu"]))
    d["cWd"] = np.ascontiguousarray(np.asarray(inputs["L0_ffn_cWd"]))
    d.update(_prep_consts())
    return d


FUSED = int(os.environ.get("KFUSED", "1"))
ALL_BLOCKS = ("l0attn", "moe", "l1attn", "l1ffn")
LAUNCHES = (("l0attn",), ("moe",), ("l1attn",), ("l1ffn",))


def _run_launch(blocks, in_maps):
    if blocks not in _PROG_CACHE:
        _PROG_CACHE[blocks] = build_program(blocks)
    res = run_bass_kernel_spmd(_PROG_CACHE[blocks], in_maps, core_ids=list(range(8)))
    out = np.empty((4, SK, D), dtype=np.float32)
    for core in range(8):
        b, half = core // 2, core % 2
        out[b, half * SQ:(half + 1) * SQ] = res.results[core]["out"]
    return out


def kernel(**inputs):
    if FUSED:
        in_maps = []
        for core in range(8):
            d = _prep_l0_inputs(inputs, core)
            d.update(_prep_moe_inputs(inputs, core))
            d.update(_prep_l1attn_inputs(inputs, core))
            d.update(_prep_l1ffn_inputs(inputs, core))
            in_maps.append(d)
        return _run_launch(ALL_BLOCKS, in_maps)
    x = None
    for blocks in LAUNCHES:
        if blocks == ("l0attn",):
            in_maps = [_prep_l0_inputs(inputs, core) for core in range(8)]
        elif blocks == ("moe",):
            in_maps = [_prep_moe_inputs(inputs, core, x1=x) for core in range(8)]
        elif blocks == ("l1attn",):
            in_maps = [_prep_l1attn_inputs(inputs, core, x2=x) for core in range(8)]
        elif blocks == ("l1ffn",):
            in_maps = [_prep_l1ffn_inputs(inputs, core, x3=x) for core in range(8)]
        x = _run_launch(blocks, in_maps)
    return x


# version 05
# speedup vs baseline: 1.0829x; 1.0829x over previous
import contextlib
import os
import numpy as np
import ml_dtypes
import concourse.bass as bass
import concourse.mybir as mybir
from concourse.bass_utils import run_bass_kernel_spmd

DEBUG = int(os.environ.get("KDBG", "0"))
F32 = mybir.dt.float32
BF16 = mybir.dt.bfloat16
AF = mybir.ActivationFunctionType
ALU = mybir.AluOpType
AX = mybir.AxisListType
bf = ml_dtypes.bfloat16

D = 2048
NH = 16
HD = 128
SQ = 1024
SK = 2048
NCF = 16
NTT = 8
NKT = 16
EXP_SHIFT = 4.0
ATT_SCALE = float(HD) ** -0.5
LN_EPS = 1e-5


class K:

    def __init__(self, nc, ctx, block):
        self.nc = nc
        self.ctx = ctx
        self.block = block
        self.sems = {}
        self.cnt = {}
        for name in ("dma", "dk", "dw0", "dw1", "pe", "ve", "sc", "gp", "cc"):
            self.sems[name] = ctx.enter_context(nc.semaphore(name))
            self.cnt[name] = 0

    def inc(self, instr, sem, k=1):
        instr.then_inc(self.sems[sem], k)
        self.cnt[sem] += k
        return self.cnt[sem]

    def wait(self, eng, sem, val=None):
        v = self.cnt[sem] if val is None else val
        if v > 0:
            eng.wait_ge(self.sems[sem], v)

    def new_slots(self, n):
        return [{"w": None, "r": []} for _ in range(n)]

    def slot_write_waits(self, eng, slot):
        for s, c in slot["r"]:
            eng.wait_ge(self.sems[s], c)

    def slot_wrote(self, slot, sem="pe"):
        slot["w"] = (sem, self.cnt[sem])
        slot["r"] = []

    def slot_read_waits(self, eng, slot):
        if slot["w"] is not None:
            s, c = slot["w"]
            eng.wait_ge(self.sems[s], c)

    def slot_did_read(self, slot, sem):
        slot["r"].append((sem, self.cnt[sem]))

    def dma_flush(self, s):
        s.wait_ge(self.sems["dma"], self.cnt["dma"])
        self.inc(s.nop(), "dk")

    def fence(self, eng, sems=("pe", "ve", "sc")):
        for s in sems:
            self.wait(eng, s)

    def sb(self, name, shape, dt):
        return self.ctx.enter_context(self.nc.sbuf_tensor(name, shape, dt))

    def ps(self, name, shape, dt=F32):
        return self.ctx.enter_context(self.nc.psum_tensor(name, shape, dt))

    def sync(self, fn):
        self.block.sync(fn)

    def tensor(self, fn):
        self.block.tensor(fn)

    def vector(self, fn):
        self.block.vector(fn)

    def scalar(self, fn):
        self.block.scalar(fn)

    def gpsimd(self, fn):
        self.block.gpsimd(fn)


def _fm(dram2d):
    return dram2d.rearrange("(c p) m -> p c m", p=128)


def emit_load_cast_transpose(k, dram_rows, n_tiles, stage, tokbf, ident_sb, pT, dst_fm,
                             dst_col0=0):
    cast_done = [0] * n_tiles
    tpg = [0] * (n_tiles * 4)
    evg = [0] * (n_tiles * 4)
    entry_ve = k.cnt["ve"]
    entry_pe = k.cnt["pe"]
    for t in range(n_tiles):
        def s_body(s, t=t):
            if t >= 2:
                s.wait_ge(k.sems["ve"], cast_done[t - 2])
            elif entry_ve > 0:
                s.wait_ge(k.sems["ve"], entry_ve)
            k.inc(s.dma_start(out=stage[:, t % 2, :],
                              in_=dram_rows[t * 128:(t + 1) * 128, :]), "dma", 16)
            k.dma_flush(s)
        k.sync(s_body)
        dk_now = k.cnt["dk"]

        def v_cast(v, t=t, dk_now=dk_now):
            v.wait_ge(k.sems["dk"], dk_now)
            if t >= 2:
                v.wait_ge(k.sems["pe"], tpg[(t - 2) * 4 + 3])
            elif entry_pe > 0:
                v.wait_ge(k.sems["pe"], entry_pe)
            cast_done[t] = k.inc(
                v.tensor_copy(out=tokbf[:, t % 2, :], in_=stage[:, t % 2, :]), "ve")
        k.vector(v_cast)

        for g in range(4):
            def t_body(tn, t=t, g=g):
                tn.wait_ge(k.sems["ve"], cast_done[t])
                gi = t * 4 + g
                if gi >= 2:
                    tn.wait_ge(k.sems["ve"], evg[gi - 2])
                elif entry_ve > 0:
                    tn.wait_ge(k.sems["ve"], entry_ve)
                last = None
                for i in range(4):
                    c = g * 4 + i
                    last = tn.transpose(pT[:, gi % 2, i * 128:(i + 1) * 128],
                                        tokbf[:, t % 2, c * 128:(c + 1) * 128],
                                        ident_sb[:, :])
                tpg[gi] = k.inc(last, "pe")
            k.tensor(t_body)

            def v_evac(v, t=t, g=g):
                gi = t * 4 + g
                v.wait_ge(k.sems["pe"], tpg[gi])
                evg[gi] = k.inc(v.tensor_copy(
                    out=dst_fm[:, g * 4:(g + 1) * 4,
                               dst_col0 + t * 128:dst_col0 + (t + 1) * 128],
                    in_=pT[:, gi % 2, 0:512].rearrange("p (i m) -> p i m", i=4)), "ve")
            k.vector(v_evac)

    def v_end(v):
        v.drain()
    k.vector(v_end)


def emit_ln_and_xfm(k, stream, x_fm, ln_sc, ident_sb, pT, xtokbf):
    stats, mv8, veps8, rstd8, negmr8 = (ln_sc["stats"], ln_sc["mv8"], ln_sc["veps8"],
                                        ln_sc["rstd8"], ln_sc["negmr8"])

    def v1(v):
        k.wait(v, "pe")
        for t in range(NTT):
            for g in range(4):
                v.bn_stats(out=stats[:, t, g * 6:(g + 1) * 6],
                           in_=stream[:, t, g * 512:(g + 1) * 512])
        v.drain()
        for t in range(NTT):
            v.bn_aggr(out=mv8[:, t, :], in_=stats[:, t, :])
        v.drain()
        last = None
        for t in range(NTT):
            last = v.tensor_scalar(out=veps8[:, t:t + 1], in0=mv8[:, t, 1:2],
                                   scalar1=LN_EPS, scalar2=None, op0=ALU.add)
        k.inc(last, "ve")
    k.vector(v1)
    ve_ready = k.cnt["ve"]

    def s1(sc):
        sc.wait_ge(k.sems["ve"], ve_ready)
        k.inc(sc.activation(out=rstd8[:, :], in_=veps8[:, :], func=AF.Sqrt), "sc")
    k.scalar(s1)
    sc_ready = k.cnt["sc"]

    def v2(v):
        v.wait_ge(k.sems["sc"], sc_ready)
        v.reciprocal(out=rstd8[:, :], in_=rstd8[:, :])
        v.drain()
        v.scalar_tensor_tensor(out=negmr8[:, :],
                               in0=mv8.rearrange("p t s -> p (t s)")[:, 0::2],
                               scalar=-1.0, in1=rstd8[:, :],
                               op0=ALU.mult, op1=ALU.mult)
        v.drain()
        for t in range(NTT):
            v.tensor_scalar(out=stream[:, t, :], in0=stream[:, t, :],
                            scalar1=rstd8[:, t:t + 1], scalar2=negmr8[:, t:t + 1],
                            op0=ALU.mult, op1=ALU.add)
        v.drain()
        for t in range(NTT):
            ins = v.tensor_copy(out=xtokbf[:, t, :], in_=stream[:, t, :])
            k.inc(ins, "ve")
    k.vector(v2)
    cast_base = k.cnt["ve"] - NTT

    tpg = [0] * (NTT * 4)
    evg = [0] * (NTT * 4)
    for t in range(NTT):
        for g in range(4):
            def t_body(tn, t=t, g=g):
                tn.wait_ge(k.sems["ve"], cast_base + t + 1)
                gi = t * 4 + g
                if gi >= 2:
                    tn.wait_ge(k.sems["ve"], evg[gi - 2])
                last = None
                for i in range(4):
                    c = g * 4 + i
                    last = tn.transpose(pT[:, gi % 2, i * 128:(i + 1) * 128],
                                        xtokbf[:, t, c * 128:(c + 1) * 128],
                                        ident_sb[:, :])
                tpg[gi] = k.inc(last, "pe")
            k.tensor(t_body)

            def v_evac(v, t=t, g=g):
                gi = t * 4 + g
                v.wait_ge(k.sems["pe"], tpg[gi])
                evg[gi] = k.inc(v.tensor_copy(
                    out=x_fm[:, g * 4:(g + 1) * 4, t * 128:(t + 1) * 128],
                    in_=pT[:, gi % 2, 0:512].rearrange("p (i m) -> p i m", i=4)), "ve")
            k.vector(v_evac)

    def v_end(v):
        v.drain()
    k.vector(v_end)


def build_l0_attn(nc, ctx, k, io, persist):
    ident_sb, onesr_sb, onesc_sb, actc_sb, pT = (
        persist["ident_sb"],
        persist["onesr_sb"], persist["onesc_sb"], persist["actc_sb"], persist["pT"])

    sub0 = contextlib.ExitStack()
    ao_fm = sub0.enter_context(nc.sbuf_tensor("a_aofm", [128, NCF, SQ], BF16))
    pS = sub0.enter_context(nc.psum_tensor("a_pS", [128, 2, 512], F32))
    pS_sl = k.new_slots(2)

    with contextlib.ExitStack() as sub:
        sbt = lambda n, s, d: sub.enter_context(nc.sbuf_tensor(n, s, d))
        xk_fm = sbt("a_xkfm", [128, NCF, SK], BF16)
        masks_sb = sbt("a_masks", [128, 32, 512], BF16)
        zrow = sbt("a_zrow", [1, 512], F32)
        zrec = sbt("a_zrec", [1, 512], F32)
        zbf = sbt("a_zbf", [1, 512], BF16)
        rzb = sbt("a_rzb", [128, 512], F32)
        xq_fm = xk_fm

        pZ = sub.enter_context(nc.psum_tensor("a_pZ", [128, 512], F32))
        pO = sub.enter_context(nc.psum_tensor("a_pO", [128, 512], F32))
        pC = sub.enter_context(nc.psum_tensor("a_pC", [128, 512], F32))

        def c_sync(s):
            k.inc(s.dma_start(out=ident_sb[:, :], in_=io["ident"][:, :]), "dma", 16)
            k.inc(s.dma_start(out=onesr_sb[:, :], in_=io["onesr"][:, :]), "dma", 16)
            k.inc(s.dma_start(out=onesc_sb[:, :], in_=io["onesc"][:, :]), "dma", 16)
            k.inc(s.dma_start(out=actc_sb[:, :], in_=io["actc"][:, :]), "dma", 16)
            k.inc(s.dma_start(out=masks_sb[:, :, :],
                              in_=io["masks"].rearrange("(m p) j -> p m j", p=128)),
                  "dma", 16)
            k.dma_flush(s)
        k.sync(c_sync)
        dk_consts = k.cnt["dk"]

        with contextlib.ExitStack() as sub_in:
            stage = sub_in.enter_context(nc.sbuf_tensor("a_stage", [128, 2, 2048], F32))
            tokbf = sub_in.enter_context(nc.sbuf_tensor("a_tokbf", [128, 2, 2048], BF16))
            emit_load_cast_transpose(k, io["xk"], NKT, stage, tokbf, ident_sb, pT, xk_fm)

        input_pe = k.cnt["pe"]
        input_ve = k.cnt["ve"]
        stage_w1 = sbt("a_stw1", [128, 2, NCF, 128], F32)
        wT3 = sbt("a_wt3", [128, 2, 3, NCF, 128], BF16)
        q_h = sbt("a_qh", [128, SQ], BF16)
        k_h = sbt("a_kh", [128, SK], BF16)
        v_h = sbt("a_vh", [128, NKT, HD], BF16)
        exp_buf = sbt("a_exp", [128, NKT, 512], BF16)

        sl_stw = k.new_slots(2)
        sl_wt3 = k.new_slots(2)
        head_pe_end = [0] * (NH + 1)
        for h in range(NH):
            wb = h % 2
            for wi, wname in enumerate(("wqT", "wkT", "wvT")):
                ssl = (h * 3 + wi) % 2

                dwn = "dw%d" % ssl

                def w_sync(s, h=h, wi=wi, wname=wname, ssl=ssl, dwn=dwn):
                    k.slot_write_waits(s, sl_stw[ssl])
                    k.inc(s.dma_start(
                        out=stage_w1[:, ssl, :, :],
                        in_=io[wname][:, h * 128:(h + 1) * 128]
                            .rearrange("(c p) m -> p c m", p=128)), dwn, 16)
                k.sync(w_sync)
                k.slot_wrote(sl_stw[ssl], "dma")
                dw_w = k.cnt[dwn]

                def w_cast(v, h=h, wi=wi, wb=wb, ssl=ssl, dw_w=dw_w, dwn=dwn):
                    v.wait_ge(k.sems[dwn], dw_w)
                    if wi == 0:
                        k.slot_write_waits(v, sl_wt3[wb])
                    k.inc(v.tensor_copy(
                        out=wT3[:, wb, wi, :, :].rearrange("p c m -> p (c m)"),
                        in_=stage_w1[:, ssl, :, :].rearrange("p c m -> p (c m)")), "ve")
                k.vector(w_cast)
                k.slot_did_read(sl_stw[ssl], "ve")
            k.slot_wrote(sl_wt3[wb], "ve")
            sc_w = k.cnt["ve"]

            proj = ([("q", blk) for blk in range(2)] + [("kk", blk) for blk in range(4)]
                    + [("v", tk) for tk in range(NKT)])
            for kind, blk in proj:
                si = blk % 2

                def t_p(tn, h=h, kind=kind, blk=blk, si=si, wb=wb, sc_w=sc_w):
                    tn.wait_ge(k.sems["ve"], sc_w)
                    if h == 0:
                        tn.wait_ge(k.sems["dk"], dk_consts)
                    k.slot_write_waits(tn, pS_sl[si])
                    ins = None
                    for c in range(NCF):
                        if kind == "q":
                            ins = tn.matmul(pS[:, si, :], wT3[:, wb, 0, c, :],
                                            xq_fm[:, c, blk * 512:(blk + 1) * 512],
                                            start=(c == 0), stop=(c == NCF - 1))
                        elif kind == "kk":
                            ins = tn.matmul(pS[:, si, :], wT3[:, wb, 1, c, :],
                                            xk_fm[:, c, blk * 512:(blk + 1) * 512],
                                            start=(c == 0), stop=(c == NCF - 1))
                        else:
                            ins = tn.matmul(pS[:, si, 0:128],
                                            xk_fm[:, c, blk * 128:(blk + 1) * 128],
                                            wT3[:, wb, 2, c, :],
                                            start=(c == 0), stop=(c == NCF - 1))
                    k.inc(ins, "pe")
                k.tensor(t_p)
                k.slot_wrote(pS_sl[si])

                def v_p(v, kind=kind, blk=blk, si=si):
                    k.slot_read_waits(v, pS_sl[si])
                    if kind == "q":
                        ins = v.tensor_scalar(out=q_h[:, blk * 512:(blk + 1) * 512],
                                              in0=pS[:, si, :], scalar1=ATT_SCALE,
                                              scalar2=None, op0=ALU.mult)
                    elif kind == "kk":
                        ins = v.tensor_copy(out=k_h[:, blk * 512:(blk + 1) * 512],
                                            in_=pS[:, si, :])
                    else:
                        ins = v.tensor_copy(out=v_h[:, blk, :], in_=pS[:, si, 0:128])
                    k.inc(ins, "ve")
                k.vector(v_p)
                k.slot_did_read(pS_sl[si], "ve")
            k.slot_did_read(sl_wt3[wb], "pe")

            qkv_ve = k.cnt["ve"]

            for tqb in range(2):
                mask_ve = [0] * NKT

                for tk in range(NKT):
                    si = tk % 2

                    def t_s(tn, tk=tk, tqb=tqb, si=si, qkv_ve=qkv_ve, h=h):
                        if tk == 0 and tqb == 0:
                            tn.wait_ge(k.sems["ve"], qkv_ve)
                        k.slot_write_waits(tn, pS_sl[si])
                        k.inc(tn.matmul(pS[:, si, :], k_h[:, tk * 128:(tk + 1) * 128],
                                        q_h[:, tqb * 512:(tqb + 1) * 512],
                                        start=True, stop=True), "pe")
                    k.tensor(t_s)
                    k.slot_wrote(pS_sl[si])

                    def s_exp(sc, tk=tk, tqb=tqb, si=si, h=h):
                        k.slot_read_waits(sc, pS_sl[si])
                        if h == 0 and tqb == 0 and tk == 0:
                            sc.wait_ge(k.sems["dk"], dk_consts)
                        k.inc(sc.activation(
                            out=exp_buf[:, tk, :], in_=pS[:, si, :], func=AF.Exp), "sc")
                    k.scalar(s_exp)
                    k.slot_did_read(pS_sl[si], "sc")
                    sc_now = k.cnt["sc"]

                    def v_mask(v, tk=tk, tqb=tqb, sc_now=sc_now, h=h):
                        v.wait_ge(k.sems["sc"], sc_now)
                        if h == 0 and tqb == 0 and tk == 0:
                            v.wait_ge(k.sems["dk"], dk_consts)
                        mask_ve[tk] = k.inc(v.tensor_tensor(
                            out=exp_buf[:, tk, :], in0=exp_buf[:, tk, :],
                            in1=masks_sb[:, tqb * 16 + tk, :], op=ALU.mult), "ve")
                    k.vector(v_mask)

                def t_zav(tn, tqb=tqb, h=h, mask_all=None):
                    tn.wait_ge(k.sems["ve"], k.cnt["ve"])
                    for tk in range(NKT):
                        ins = tn.matmul(pZ[0:1, :], onesc_sb[:, :], exp_buf[:, tk, :],
                                        start=(tk == 0), stop=(tk == NKT - 1))
                    k.inc(ins, "pe")
                    for tk in range(NKT):
                        ins = tn.matmul(pO[:, :], v_h[:, tk, :], exp_buf[:, tk, :],
                                        start=(tk == 0), stop=(tk == NKT - 1))
                    k.inc(ins, "pe")
                k.tensor(t_zav)
                pe_zav = k.cnt["pe"]

                def v_z(v, pe_zav=pe_zav):
                    v.wait_ge(k.sems["pe"], pe_zav - 1)
                    v.tensor_copy(out=zrow[:, :], in_=pZ[0:1, :])
                    v.drain()
                    v.reciprocal(out=zrec[:, :], in_=zrow[:, :])
                    v.drain()
                    k.inc(v.tensor_copy(out=zbf[:, :], in_=zrec[:, :]), "ve")
                k.vector(v_z)
                ve_z = k.cnt["ve"]

                def t_bc(tn, ve_z=ve_z):
                    tn.wait_ge(k.sems["ve"], ve_z)
                    k.inc(tn.matmul(pC[:, :], onesr_sb[:, :], zbf[:, :],
                                    start=True, stop=True), "pe")
                k.tensor(t_bc)
                pe_bc = k.cnt["pe"]

                def v_norm(v, tqb=tqb, h=h, pe_bc=pe_bc):
                    v.wait_ge(k.sems["pe"], pe_bc)
                    v.tensor_copy(out=rzb[:, :], in_=pC[:, :])
                    v.drain()
                    k.inc(v.tensor_tensor(
                        out=ao_fm[:, h, tqb * 512:(tqb + 1) * 512],
                        in0=pO[:, :], in1=rzb[:, :], op=ALU.mult), "ve")
                k.vector(v_norm)

        heads_ve = k.cnt["ve"]

        if DEBUG in (4, 5):
            def dbg_sync2(s):
                k.wait(s, "ve")
                k.wait(s, "sc")
                if DEBUG == 4:
                    k.wait(s, "dma")
            k.sync(dbg_sync2)

        if DEBUG in (1, 2):
            def dbg_sync(s):
                k.wait(s, "ve")
                k.wait(s, "sc")
                k.inc(s.dma_start(out=io["dbg_q"][:, :], in_=q_h[:, :]), "dma", 16)
                k.inc(s.dma_start(out=io["dbg_k"][:, :], in_=k_h[:, :]), "dma", 16)
                k.inc(s.dma_start(out=io["dbg_v"][:, :],
                                  in_=v_h.rearrange("p a b -> p (a b)")), "dma", 16)
                k.inc(s.dma_start(out=io["dbg_exp"][:, :],
                                  in_=exp_buf.rearrange("p a b -> p (a b)")), "dma", 16)
                k.inc(s.dma_start(out=io["dbg_z"][0:1, :], in_=zrow[:, :]), "dma", 16)
                k.inc(s.dma_start(out=io["dbg_z"][1:2, :], in_=zrec[:, :]), "dma", 16)
                k.inc(s.dma_start(out=io["dbg_ao"][:, :],
                                  in_=ao_fm.rearrange("p a b -> p (a b)")), "dma", 16)
                s.wait_ge(k.sems["dma"], k.cnt["dma"])
            k.sync(dbg_sync)

    stream = ctx.enter_context(nc.sbuf_tensor("stream", [128, NTT, D], F32, side="right"))
    persist["stream"] = stream

    if True:
        def r_sync(s):
            k.wait(s, "pe")
            k.wait(s, "ve")
            k.wait(s, "sc")
            for t in range(NTT):
                k.inc(s.dma_start(out=stream[:, t, :],
                                  in_=io["xk"][t * 128:(t + 1) * 128, :]), "dma", 16)
            k.dma_flush(s)
        k.sync(r_sync)
        dk_res = k.cnt["dk"]

        with contextlib.ExitStack() as sub2:
            wo_st = sub2.enter_context(nc.sbuf_tensor("a_wost", [128, NCF, 512], F32))
            wo_bf = sub2.enter_context(nc.sbuf_tensor("a_wobf", [128, NCF, 512], BF16))
            for jb in range(4):
                pe_before_w = k.cnt["pe"]

                def wo_sync(s, jb=jb, pe_before_w=pe_before_w):
                    if jb == 0:
                        k.wait(s, "pe")
                        k.wait(s, "ve")
                        k.wait(s, "sc")
                    if jb > 0:
                        s.wait_ge(k.sems["pe"], pe_before_w)
                    k.inc(s.dma_start(
                        out=wo_st[:, :, :],
                        in_=_fm(io["woT"])[:, :, jb * 512:(jb + 1) * 512]), "dma", 16)
                    k.dma_flush(s)
                k.sync(wo_sync)
                dk_now = k.cnt["dk"]

                def wo_cast(sc, jb=jb, dk_now=dk_now, pe_before_w=pe_before_w):
                    sc.wait_ge(k.sems["dk"], dk_now)
                    if jb > 0:
                        sc.wait_ge(k.sems["pe"], pe_before_w)
                    k.inc(sc.copy(out=wo_bf.rearrange("p c m -> p (c m)"),
                                  in_=wo_st.rearrange("p c m -> p (c m)")), "sc")
                k.scalar(wo_cast)
                sc_now = k.cnt["sc"]

                for t in range(NTT):
                    si = t % 2

                    def t_wo(tn, t=t, jb=jb, si=si, sc_now=sc_now, heads_ve=heads_ve):
                        tn.wait_ge(k.sems["sc"], sc_now)
                        if t == 0 and jb == 0:
                            tn.wait_ge(k.sems["ve"], heads_ve)
                        k.slot_write_waits(tn, pS_sl[si])
                        ins = None
                        for c in range(NCF):
                            ins = tn.matmul(pS[:, si, :], ao_fm[:, c, t * 128:(t + 1) * 128],
                                            wo_bf[:, c, :],
                                            start=(c == 0), stop=(c == NCF - 1))
                        k.inc(ins, "pe")
                    k.tensor(t_wo)
                    k.slot_wrote(pS_sl[si])

                    def v_wo(v, t=t, jb=jb, si=si, dk_res=dk_res):
                        k.slot_read_waits(v, pS_sl[si])
                        v.wait_ge(k.sems["dk"], dk_res)
                        k.inc(v.tensor_tensor(
                            out=stream[:, t, jb * 512:(jb + 1) * 512],
                            in0=stream[:, t, jb * 512:(jb + 1) * 512],
                            in1=pS[:, si, :], op=ALU.add), "ve")
                    k.vector(v_wo)
                    k.slot_did_read(pS_sl[si], "ve")

            def v_fin(v):
                v.drain()
            k.vector(v_fin)

    if DEBUG in (1, 3):
        def dbg_res_sync(s):
            k.wait(s, "ve")
            for t in range(NTT):
                k.inc(s.dma_start(out=io["dbg_res"][t * 128:(t + 1) * 128, :],
                                  in_=stream[:, t, :]), "dma", 16)
            s.wait_ge(k.sems["dma"], k.cnt["dma"])
        k.sync(dbg_res_sync)

    sub0.close()

    x_fm = ctx.enter_context(nc.sbuf_tensor("x_fm", [128, NCF, SQ], BF16, side="right"))
    persist["x_fm"] = x_fm
    if True:
        with contextlib.ExitStack() as sub3:
            ln_sc = {
                "stats": sub3.enter_context(nc.sbuf_tensor("ln_stats", [128, NTT, 24], F32)),
                "mv8": sub3.enter_context(nc.sbuf_tensor("ln_mv8", [128, NTT, 2], F32)),
                "veps8": sub3.enter_context(nc.sbuf_tensor("ln_veps", [128, NTT], F32)),
                "rstd8": sub3.enter_context(nc.sbuf_tensor("ln_rstd", [128, NTT], F32)),
                "negmr8": sub3.enter_context(nc.sbuf_tensor("ln_negmr", [128, NTT], F32)),
            }
            xtokbf = sub3.enter_context(nc.sbuf_tensor("ln_tokbf", [128, NTT, 2048], BF16))
            emit_ln_and_xfm(k, stream, x_fm, ln_sc, ident_sb, pT, xtokbf)


def emit_standalone_entry(k, nc, ctx, io, persist, xname):
    stream = ctx.enter_context(nc.sbuf_tensor("stream", [128, NTT, D], F32, side="right"))
    x_fm = ctx.enter_context(nc.sbuf_tensor("x_fm", [128, NCF, SQ], BF16, side="right"))
    persist["stream"] = stream
    persist["x_fm"] = x_fm
    ident_sb, onesr_sb, onesc_sb, actc_sb = (persist["ident_sb"], persist["onesr_sb"],
                                             persist["onesc_sb"], persist["actc_sb"])

    def c_sync(s):
        k.inc(s.dma_start(out=ident_sb[:, :], in_=io["ident"][:, :]), "dma", 16)
        k.inc(s.dma_start(out=onesr_sb[:, :], in_=io["onesr"][:, :]), "dma", 16)
        k.inc(s.dma_start(out=onesc_sb[:, :], in_=io["onesc"][:, :]), "dma", 16)
        k.inc(s.dma_start(out=actc_sb[:, :], in_=io["actc"][:, :]), "dma", 16)
        for t in range(NTT):
            k.inc(s.dma_start(out=stream[:, t, :],
                              in_=io[xname][t * 128:(t + 1) * 128, :]), "dma", 16)
        k.dma_flush(s)
    k.sync(c_sync)

    with contextlib.ExitStack() as sub_in:
        stage = sub_in.enter_context(nc.sbuf_tensor("e_stage", [128, 2, 2048], F32))
        tokbf = sub_in.enter_context(nc.sbuf_tensor("e_tokbf", [128, 2, 2048], BF16))
        emit_load_cast_transpose(k, io[xname], NTT, stage, tokbf, ident_sb,
                                 persist["pT"], x_fm)


def build_moe(nc, ctx, k, io, persist):
    stream, x_fm, ident_sb, pT = (persist["stream"], persist["x_fm"],
                                  persist["ident_sb"], persist["pT"])
    NE, NI, NIT = 32, 768, 6
    NSC, NSI = 16, 128

    wg_v = io["Wg"].rearrange("e (c p) i -> e p c i", p=128)
    wu_v = io["Wu"].rearrange("e (c p) i -> e p c i", p=128)
    wd_v = io["Wd"].rearrange("e (ic p) h -> e p ic h", p=128)
    cwg_v = io["cWg"].rearrange("e (c p) i -> e p c i", p=128)
    cwu_v = io["cWu"].rearrange("e (c p) i -> e p c i", p=128)
    cwd_v = io["cWd"].rearrange("e (ic p) h -> e p ic h", p=128)

    with contextlib.ExitStack() as sub:
        sbt = lambda n, s, d: sub.enter_context(nc.sbuf_tensor(n, s, d))
        lg = sbt("m_lg", [128, NTT, 32], F32)
        ew = sbt("m_ew", [128, NTT, 32], F32)
        acc = sbt("m_acc", [128, NTT, 32], F32)
        work = sbt("m_work", [128, NTT, 32], F32)
        eqb = sbt("m_eq", [128, NTT, 32], F32)
        cw = sbt("m_cw", [128, NTT, 32], F32)
        cws = sbt("m_cws", [128, NTT, 16], F32)
        sc8 = sbt("m_sc8", [128, 5, NTT], F32)

        pG = sub.enter_context(nc.psum_tensor("m_pG", [128, 2, 512], F32))
        pU = sub.enter_context(nc.psum_tensor("m_pU", [128, 2, 512], F32))
        pY = sub.enter_context(nc.psum_tensor("m_pY", [128, 2, 512], F32))

        sl_wst = k.new_slots(2)
        sl_gu = k.new_slots(2)
        sl_y = k.new_slots(2)
        sl_a = k.new_slots(1)
        sl_silu = k.new_slots(2)
        sl_w = {"wg": k.new_slots(NIT), "wu": k.new_slots(NIT), "wd": k.new_slots(8)}

        sub_rt = contextlib.ExitStack()
        gw_st = sub_rt.enter_context(nc.sbuf_tensor("m_gwst", [128, NCF, 32], F32))
        gw_bf = sub_rt.enter_context(nc.sbuf_tensor("m_gwbf", [128, NCF, 32], BF16))

        def rt_sync(s):
            k.fence(s)
            k.inc(s.dma_start(out=gw_st[:, :, :],
                              in_=io["gate_wT"].rearrange("(c p) m -> p c m", p=128)),
                  "dma", 16)
            k.dma_flush(s)
        k.sync(rt_sync)
        dk_now = k.cnt["dk"]

        def rt_cast(sc, dk_now=dk_now):
            sc.wait_ge(k.sems["dk"], dk_now)
            k.fence(sc, sems=("pe", "ve"))
            k.inc(sc.copy(out=gw_bf.rearrange("p c m -> p (c m)"),
                          in_=gw_st.rearrange("p c m -> p (c m)")), "sc")
        k.scalar(rt_cast)
        sc_gw = k.cnt["sc"]

        for t in range(NTT):
            si = t % 2

            def rt_mm(tn, t=t, si=si, sc_gw=sc_gw):
                tn.wait_ge(k.sems["sc"], sc_gw)
                k.slot_write_waits(tn, sl_y[si])
                ins = None
                for c in range(NCF):
                    ins = tn.matmul(pY[:, si, 0:32], x_fm[:, c, t * 128:(t + 1) * 128],
                                    gw_bf[:, c, :], start=(c == 0), stop=(c == NCF - 1))
                k.inc(ins, "pe")
            k.tensor(rt_mm)
            k.slot_wrote(pY if False else sl_y[si])

            def rt_ev(v, t=t, si=si):
                k.slot_read_waits(v, sl_y[si])
                k.inc(v.tensor_copy(out=lg[:, t, :], in_=pY[:, si, 0:32]), "ve")
            k.vector(rt_ev)
            k.slot_did_read(sl_y[si], "ve")

        def rt_max(v):
            v.drain()
            last = None
            for t in range(NTT):
                last = v.reduce_max(out=sc8[:, 0, t:t + 1], in_=lg[:, t, :],
                                    axis=AX.X, negate=True)
            k.inc(last, "ve")
        k.vector(rt_max)
        ve_nm = k.cnt["ve"]

        def rt_exp(sc, ve_nm=ve_nm):
            sc.wait_ge(k.sems["ve"], ve_nm)
            last = None
            for t in range(NTT):
                last = sc.activation(out=ew[:, t, :], in_=lg[:, t, :], func=AF.Exp,
                                     bias=sc8[:, 0, t:t + 1], scale=1.0)
            k.inc(last, "sc")
        k.scalar(rt_exp)
        sc_ew = k.cnt["sc"]

        def rt_topk(v, sc_ew=sc_ew):
            for t in range(NTT):
                v.tensor_copy(out=work[:, t, :], in_=lg[:, t, :])
            v.memset(acc.rearrange("p a b -> p (a b)"), 0.0)
            v.wait_ge(k.sems["sc"], sc_ew)
            v.drain()
            for kk in range(4):
                for t in range(NTT):
                    v.reduce_max(out=sc8[:, 1, t:t + 1], in_=work[:, t, :], axis=AX.X)
                v.drain()
                for t in range(NTT):
                    v.tensor_scalar(out=eqb[:, t, :], in0=work[:, t, :],
                                    scalar1=sc8[:, 1, t:t + 1], scalar2=None,
                                    op0=ALU.is_equal)
                v.drain()
                for t in range(NTT):
                    v.scalar_tensor_tensor(out=lg[:, t, :], in0=ew[:, t, :],
                                           scalar=1.0, in1=eqb[:, t, :],
                                           op0=ALU.mult, op1=ALU.mult,
                                           accum_out=sc8[:, 2, t:t + 1])
                v.drain()
                for t in range(NTT):
                    v.scalar_tensor_tensor(out=acc[:, t, :], in0=eqb[:, t, :],
                                           scalar=sc8[:, 2, t:t + 1], in1=acc[:, t, :],
                                           op0=ALU.mult, op1=ALU.add)
                for t in range(NTT):
                    v.scalar_tensor_tensor(out=work[:, t, :], in0=eqb[:, t, :],
                                           scalar=-1e30, in1=work[:, t, :],
                                           op0=ALU.mult, op1=ALU.add)
                v.drain()
            for t in range(NTT):
                v.reduce_sum(out=sc8[:, 3, t:t + 1], in_=acc[:, t, :], axis=AX.X)
            v.drain()
            v.reciprocal(out=sc8[:, 4, :], in_=sc8[:, 3, :])
            v.drain()
            for t in range(NTT):
                v.tensor_scalar(out=cw[:, t, :], in0=acc[:, t, :],
                                scalar1=sc8[:, 4, t:t + 1], scalar2=None, op0=ALU.mult)
            v.drain()
            cwv = cw.rearrange("p t (e two) -> p t e two", two=2)
            for t in range(NTT):
                v.tensor_tensor(out=cws[:, t, :], in0=cwv[:, t, :, 0],
                                in1=cwv[:, t, :, 1], op=ALU.add)
            v.drain()
            last = v.tensor_scalar(out=cws.rearrange("p a b -> p (a b)"),
                                   in0=cws.rearrange("p a b -> p (a b)"),
                                   scalar1=0.05, scalar2=None, op0=ALU.mult)
            k.inc(last, "ve")
            v.drain()
        k.vector(rt_topk)
        ve_cw = k.cnt["ve"]
        sub_rt.close()

        wst = sbt("m_wst", [128, 2, 2048], F32)
        wg_bf = sbt("m_wg", [128, NCF, NI], BF16)
        wu_bf = sbt("m_wu", [128, NCF, NI], BF16)
        wd_bf = sbt("m_wd", [128, NIT, 2048], BF16)
        a_bf = sbt("m_a", [128, 1, NIT, SQ], BF16)
        silu_sb = sbt("m_silu", [128, 2, 512], F32)

        def expert_pass(e, small):
            n_it = 1 if small else NIT
            sa = 0
            g_view = cwg_v if small else wg_v
            u_view = cwu_v if small else wu_v
            d_view = cwd_v if small else wd_v

            marks = {}
            grains = []
            if small:
                grains.append(("wg", 0, g_view[e, :, :, :], (NCF, NSI)))
                grains.append(("wu", 0, u_view[e, :, :, :], (NCF, NSI)))
                grains.append(("wd", 0, d_view[e, :, :, :], (1, 2048)))
            else:
                for gi in range(NIT):
                    grains.append(("wg", gi, wg_v[e, :, :, gi * 128:(gi + 1) * 128],
                                   (NCF, 128)))
                for gi in range(NIT):
                    grains.append(("wu", gi, wu_v[e, :, :, gi * 128:(gi + 1) * 128],
                                   (NCF, 128)))
                for gi in range(8):
                    grains.append(("wd", gi, wd_v[e, :, :, gi * 256:(gi + 1) * 256],
                                   (NIT, 256)))
            for gn, (wname, gi, view, shp) in enumerate(grains):
                ssl = gn % 2
                nel = shp[0] * shp[1]
                first_all = (e == 0 and gn == 0 and not small)

                dwn = "dw%d" % ssl

                def g_sync(s, view=view, ssl=ssl, shp=shp, nel=nel, first_all=first_all,
                           dwn=dwn):
                    if first_all:
                        k.fence(s)
                    k.slot_write_waits(s, sl_wst[ssl])
                    k.inc(s.dma_start(
                        out=wst[:, ssl, 0:nel].rearrange("p (a b) -> p a b", a=shp[0]),
                        in_=view), dwn, 16)
                k.sync(g_sync)
                k.slot_wrote(sl_wst[ssl], "dma")
                dw_g = k.cnt[dwn]

                def g_cast(sc, wname=wname, gi=gi, ssl=ssl, shp=shp, nel=nel,
                           dw_g=dw_g, dwn=dwn, small=small):
                    sc.wait_ge(k.sems[dwn], dw_g)
                    if small and wname == "wd":
                        for sl in sl_w["wd"]:
                            k.slot_write_waits(sc, sl)
                    else:
                        k.slot_write_waits(sc, sl_w[wname][gi])
                    src = wst[:, ssl, 0:nel].rearrange("p (a b) -> p a b", a=shp[0])
                    if wname == "wg":
                        dst = wg_bf[:, :, gi * 128:gi * 128 + shp[1]]
                    elif wname == "wu":
                        dst = wu_bf[:, :, gi * 128:gi * 128 + shp[1]]
                    else:
                        dst = wd_bf[:, 0:shp[0], gi * 256:gi * 256 + shp[1]] if not small \
                            else wd_bf[:, 0:shp[0], :]
                    k.inc(sc.copy(out=dst, in_=src), "sc")
                k.scalar(g_cast)
                marks[(wname, gi)] = k.cnt["sc"]
                k.slot_did_read(sl_wst[ssl], "sc")
                k.slot_wrote(sl_w[wname][gi], "sc")

            for it in range(n_it):
                for th in range(2):
                    gu = (it * 2 + th) % 2

                    def t_gu(tn, it=it, th=th, gu=gu,
                             mg=marks[("wg", 0 if small else it)],
                             mu=marks[("wu", 0 if small else it)]):
                        tn.wait_ge(k.sems["sc"], max(mg, mu))
                        k.slot_write_waits(tn, sl_gu[gu])
                        for c in range(NCF):
                            tn.matmul(pG[:, gu, :], wg_bf[:, c, it * 128:(it + 1) * 128],
                                      x_fm[:, c, th * 512:(th + 1) * 512],
                                      start=(c == 0), stop=(c == NCF - 1))
                        ins = None
                        for c in range(NCF):
                            ins = tn.matmul(pU[:, gu, :], wu_bf[:, c, it * 128:(it + 1) * 128],
                                            x_fm[:, c, th * 512:(th + 1) * 512],
                                            start=(c == 0), stop=(c == NCF - 1))
                        k.inc(ins, "pe")
                    k.tensor(t_gu)
                    k.slot_wrote(sl_gu[gu])
                    k.slot_did_read(sl_w["wg"][0 if small else it], "pe")
                    k.slot_did_read(sl_w["wu"][0 if small else it], "pe")

                    def s_silu(sc, gu=gu):
                        k.slot_read_waits(sc, sl_gu[gu])
                        k.slot_write_waits(sc, sl_silu[gu])
                        k.inc(sc.activation(out=silu_sb[:, gu, :], in_=pG[:, gu, :],
                                            func=AF.Silu), "sc")
                    k.scalar(s_silu)
                    k.slot_did_read(sl_gu[gu], "sc")
                    k.slot_wrote(sl_silu[gu], "sc")

                    def v_a(v, it=it, th=th, gu=gu, sa=sa, first=(it == 0 and th == 0)):
                        k.slot_read_waits(v, sl_silu[gu])
                        if first:
                            v.wait_ge(k.sems["ve"], ve_cw)
                            k.slot_write_waits(v, sl_a[sa])
                        k.inc(v.tensor_tensor(out=a_bf[:, sa, it, th * 512:(th + 1) * 512],
                                              in0=silu_sb[:, gu, :], in1=pU[:, gu, :],
                                              op=ALU.mult), "ve")
                    k.vector(v_a)
                    k.slot_did_read(sl_silu[gu], "ve")
                    k.slot_did_read(sl_gu[gu], "ve")
            if small:
                k.slot_wrote(sl_a[sa], "ve")
            else:
                k.slot_wrote(sl_a[sa], "ve")
            a_done = k.cnt["ve"]

            for t in range(NTT):
                for hb in range(4):
                    y = (t * 4 + hb) % 2

                    def t_dn(tn, t=t, hb=hb, y=y, sa=sa, a_done=a_done,
                             md=marks[("wd", 0 if small else 2 * hb + 1)], n_it=n_it):
                        tn.wait_ge(k.sems["sc"], md)
                        tn.wait_ge(k.sems["ve"], a_done)
                        k.slot_write_waits(tn, sl_y[y])
                        ins = None
                        for ic in range(n_it):
                            ins = tn.matmul(pY[:, y, :], a_bf[:, sa, ic, t * 128:(t + 1) * 128],
                                            wd_bf[:, ic, hb * 512:(hb + 1) * 512],
                                            start=(ic == 0), stop=(ic == n_it - 1))
                        k.inc(ins, "pe")
                    k.tensor(t_dn)
                    k.slot_wrote(sl_y[y])
                    if small:
                        k.slot_did_read(sl_w["wd"][0], "pe")
                    else:
                        k.slot_did_read(sl_w["wd"][2 * hb], "pe")
                        k.slot_did_read(sl_w["wd"][2 * hb + 1], "pe")

                    def v_dn(v, t=t, hb=hb, y=y, e=e, small=small):
                        k.slot_read_waits(v, sl_y[y])
                        if e == 0 and not small and t == 0 and hb == 0:
                            k.wait(v, "dk")
                        col = cws[:, t, e:e + 1] if small else cw[:, t, e:e + 1]
                        k.inc(v.scalar_tensor_tensor(
                            out=stream[:, t, hb * 512:(hb + 1) * 512],
                            in0=pY[:, y, :], scalar=col,
                            in1=stream[:, t, hb * 512:(hb + 1) * 512],
                            op0=ALU.mult, op1=ALU.add), "ve")
                    k.vector(v_dn)
                    k.slot_did_read(sl_y[y], "ve")

            def v_edrain(v):
                v.drain()
            k.vector(v_edrain)
            k.slot_did_read(sl_a[sa], "pe")

        for e in range(NE):
            expert_pass(e, small=False)
        for e in range(NSC):
            expert_pass(e, small=True)

    with contextlib.ExitStack() as sub3:
        ln_sc = {
            "stats": sub3.enter_context(nc.sbuf_tensor("ln2_stats", [128, NTT, 24], F32)),
            "mv8": sub3.enter_context(nc.sbuf_tensor("ln2_mv8", [128, NTT, 2], F32)),
            "veps8": sub3.enter_context(nc.sbuf_tensor("ln2_veps", [128, NTT], F32)),
            "rstd8": sub3.enter_context(nc.sbuf_tensor("ln2_rstd", [128, NTT], F32)),
            "negmr8": sub3.enter_context(nc.sbuf_tensor("ln2_negmr", [128, NTT], F32)),
        }
        xtokbf = sub3.enter_context(nc.sbuf_tensor("ln2_tokbf", [128, NTT, 2048], BF16))
        emit_ln_and_xfm(k, stream, x_fm, ln_sc, ident_sb, pT, xtokbf)


def build_l1attn(nc, ctx, k, io, persist):
    stream, x_fm, ident_sb, pT = (persist["stream"], persist["x_fm"],
                                  persist["ident_sb"], persist["pT"])
    IB = 4

    with contextlib.ExitStack() as sub:
        sbt = lambda n, s, d: sub.enter_context(nc.sbuf_tensor(n, s, d))
        x2_all = sbt("s_x2all", [128, NKT, 2048], BF16)
        gcol = sbt("s_gcol", [128, 1], F32)

        def v_own(v):
            k.wait(v, "pe")
            last = None
            for t in range(NTT):
                last = v.tensor_copy(out=x2_all[:, t, :], in_=stream[:, t, :])
            k.inc(last, "ve")
            v.drain()
        k.vector(v_own)

        part_cast = [0] * NTT
        if "x2o" in io:
            with contextlib.ExitStack() as sub_in:
                stage = sub_in.enter_context(nc.sbuf_tensor("s_stage", [128, 2, 2048], F32))
                tokbf = sub_in.enter_context(nc.sbuf_tensor("s_tokbf", [128, 2, 2048], BF16))
                for t in range(NTT):
                    def s_ld(s, t=t):
                        if t < 2:
                            k.fence(s)
                        else:
                            s.wait_ge(k.sems["ve"], part_cast[t - 2])
                        k.inc(s.dma_start(out=stage[:, t % 2, :],
                                          in_=io["x2o"][t * 128:(t + 1) * 128, :]),
                              "dma", 16)
                        k.dma_flush(s)
                    k.sync(s_ld)
                    dk_now = k.cnt["dk"]

                    def v_pc(v, t=t, dk_now=dk_now):
                        v.wait_ge(k.sems["dk"], dk_now)
                        part_cast[t] = k.inc(v.tensor_copy(
                            out=x2_all[:, NTT + t, :], in_=stage[:, t % 2, :]), "ve")
                    k.vector(v_pc)
        else:
            cc_in = io["cc_in"]
            cc_out = io["cc_out"]

            def s_ccout(s):
                k.wait(s, "ve")
                for t in range(NTT):
                    k.inc(s.dma_start(out=cc_in[t * 128:(t + 1) * 128, :],
                                      in_=x2_all[:, t, :]), "dma", 16)
                k.dma_flush(s)
            k.sync(s_ccout)
            dk_cc = k.cnt["dk"]

            def g_cc(g, dk_cc=dk_cc):
                g.wait_ge(k.sems["dk"], dk_cc)
                k.inc(g.collective_compute(
                    "AllGather", ALU.bypass,
                    replica_groups=[[0, 1], [2, 3], [4, 5], [6, 7]],
                    ins=[cc_in.tensor.ap().opt()],
                    outs=[cc_out.tensor.ap().opt()]), "cc")
            k.gpsimd(g_cc)
            cc_done = k.cnt["cc"]

            def s_ccin(s, cc_done=cc_done):
                s.wait_ge(k.sems["cc"], cc_done)
                last = None
                for t in range(2 * NTT):
                    last = k.inc(s.dma_start(out=x2_all[:, t, :],
                                             in_=cc_out[t * 128:(t + 1) * 128, :]),
                                 "dma", 16)
                k.dma_flush(s)
            k.sync(s_ccin)

        def s_gamma(s):
            k.inc(s.dma_start(out=gcol[:, :], in_=io["gammac"][:, :]), "dma", 16)
            k.dma_flush(s)
        k.sync(s_gamma)
        x2_ready_ve = k.cnt["ve"]
        x2_ready_dma = k.cnt["dk"]

        def v_double(v):
            v.wait_ge(k.sems["ve"], x2_ready_ve)
            last = None
            for t in range(NTT):
                last = v.tensor_scalar(out=stream[:, t, :], in0=stream[:, t, :],
                                       scalar1=2.0, scalar2=None, op0=ALU.mult)
            k.inc(last, "ve")
            v.drain()
        k.vector(v_double)

        with contextlib.ExitStack() as sub2:
            erow = sub2.enter_context(nc.sbuf_tensor("s_erow", [128, 1, 2048], F32))
            attn_blk = sub2.enter_context(nc.sbuf_tensor("s_attn", [128, 4, 2048], BF16))
            attnT = sub2.enter_context(nc.sbuf_tensor("s_attnT", [128, NCF, 512], BF16))
            zsc = sub2.enter_context(nc.sbuf_tensor("s_zsc", [128, 3], F32))
            pE = sub2.enter_context(nc.psum_tensor("s_pE", [128, 2, 512], F32))
            pO2 = sub2.enter_context(nc.psum_tensor("s_pO2", [128, 2, 512], F32))
            sl_e = k.new_slots(2)
            sl_o = k.new_slots(2)
            sl_er = k.new_slots(1)
            sl_attnT = k.new_slots(1)[0]
            sl_ab = k.new_slots(1)[0]

            for ib in range(IB):
                for itl in range(4):
                    i0 = ib * 512 + itl * 128
                    esl = 0
                    for jb in range(4):
                        ps = (itl * 4 + jb) % 2

                        def t_e(tn, i0=i0, jb=jb, ps=ps, dma0=x2_ready_dma,
                                ve0=x2_ready_ve, first=(ib == 0 and itl == 0 and jb == 0)):
                            if first:
                                k.fence(tn)
                            tn.wait_ge(k.sems["ve"], ve0)
                            tn.wait_ge(k.sems["dk"], dma0)
                            k.slot_write_waits(tn, sl_e[ps])
                            ins = None
                            for sc_ in range(NKT):
                                ins = tn.matmul(pE[:, ps, :],
                                                x2_all[:, sc_, i0:i0 + 128],
                                                x2_all[:, sc_, jb * 512:(jb + 1) * 512],
                                                start=(sc_ == 0), stop=(sc_ == NKT - 1))
                            k.inc(ins, "pe")
                        k.tensor(t_e)
                        k.slot_wrote(sl_e[ps])

                        def v_ee(v, itl=itl, jb=jb, ps=ps, esl=esl):
                            k.slot_read_waits(v, sl_e[ps])
                            if jb == 0:
                                k.slot_write_waits(v, sl_er[esl])
                            k.inc(v.tensor_copy(out=erow[:, esl, jb * 512:(jb + 1) * 512],
                                                in_=pE[:, ps, :]), "ve")
                        k.vector(v_ee)
                        k.slot_did_read(sl_e[ps], "ve")
                    k.slot_wrote(sl_er[esl], "ve")

                    def v_mx(v, esl=esl):
                        k.slot_read_waits(v, sl_er[esl])
                        k.wait(v, "sc")
                        v.drain()
                        k.inc(v.reduce_max(out=zsc[:, 0:1], in_=erow[:, esl, :],
                                           axis=AX.X, negate=True), "ve")
                    k.vector(v_mx)
                    ve_mx = k.cnt["ve"]

                    def s_ex(sc, esl=esl, ve_mx=ve_mx):
                        sc.wait_ge(k.sems["ve"], ve_mx)
                        k.inc(sc.activation(out=erow[:, esl, :], in_=erow[:, esl, :],
                                            func=AF.Exp, bias=zsc[:, 0:1], scale=1.0,
                                            accum_out=zsc[:, 1:2]), "sc")
                    k.scalar(s_ex)
                    sc_ex = k.cnt["sc"]

                    def v_at(v, itl=itl, esl=esl, sc_ex=sc_ex):
                        v.wait_ge(k.sems["sc"], sc_ex)
                        v.reciprocal(out=zsc[:, 2:3], in_=zsc[:, 1:2])
                        v.drain()
                        if itl == 0:
                            k.slot_write_waits(v, sl_ab)
                        ins = v.tensor_scalar(out=attn_blk[:, itl, :],
                                              in0=erow[:, esl, :],
                                              scalar1=zsc[:, 2:3], scalar2=None,
                                              op0=ALU.mult)
                        k.inc(ins, "ve")
                    k.vector(v_at)
                    k.slot_did_read(sl_er[esl], "ve")
                    k.slot_did_read(sl_er[esl], "sc")
                k.slot_wrote(sl_ab, "ve")
                attn_ve = k.cnt["ve"]

                tpg = [0] * NCF
                evg = [0] * NCF
                for jt in range(NKT):
                    def t_tp(tn, jt=jt, attn_ve=attn_ve):
                        if jt == 0:
                            tn.wait_ge(k.sems["ve"], attn_ve)
                        if jt >= 2:
                            tn.wait_ge(k.sems["ve"], evg[jt - 2])
                        last = None
                        for itl in range(4):
                            last = tn.transpose(pT[:, jt % 2, itl * 128:(itl + 1) * 128],
                                                attn_blk[:, itl, jt * 128:(jt + 1) * 128],
                                                ident_sb[:, :])
                        tpg[jt] = k.inc(last, "pe")
                    k.tensor(t_tp)

                    def v_tp(v, jt=jt):
                        v.wait_ge(k.sems["pe"], tpg[jt])
                        if jt == 0:
                            k.slot_write_waits(v, sl_attnT)
                        evg[jt] = k.inc(v.tensor_copy(
                            out=attnT[:, jt, :],
                            in_=pT[:, jt % 2, 0:512].rearrange("p (i m) -> p i m", i=4)
                                .rearrange("p i m -> p (i m)")), "ve")
                    k.vector(v_tp)
                k.slot_wrote(sl_attnT, "ve")
                k.slot_did_read(sl_ab, "pe")
                tp_ve = k.cnt["ve"]

                for t in range(NTT):
                    osl = t % 2

                    def t_o(tn, t=t, ib=ib, osl=osl, tp_ve=tp_ve):
                        if t == 0:
                            tn.wait_ge(k.sems["ve"], tp_ve)
                        k.slot_write_waits(tn, sl_o[osl])
                        ins = None
                        for jc in range(NCF):
                            ins = tn.matmul(pO2[:, osl, :], x_fm[:, jc, t * 128:(t + 1) * 128],
                                            attnT[:, jc, :],
                                            start=(jc == 0), stop=(jc == NCF - 1))
                        k.inc(ins, "pe")
                    k.tensor(t_o)
                    k.slot_wrote(sl_o[osl])

                    def v_o(v, t=t, ib=ib, osl=osl, dma0=x2_ready_dma):
                        k.slot_read_waits(v, sl_o[osl])
                        if t == 0 and ib == 0:
                            v.wait_ge(k.sems["dk"], dma0)
                        k.inc(v.scalar_tensor_tensor(
                            out=stream[:, t, ib * 512:(ib + 1) * 512],
                            in0=pO2[:, osl, :], scalar=gcol[:, :],
                            in1=stream[:, t, ib * 512:(ib + 1) * 512],
                            op0=ALU.mult, op1=ALU.add), "ve")
                    k.vector(v_o)
                    k.slot_did_read(sl_o[osl], "ve")
                k.slot_did_read(sl_attnT, "pe")
                def v_bdrain(v):
                    v.drain()
                k.vector(v_bdrain)

    with contextlib.ExitStack() as sub3:
        ln_sc = {
            "stats": sub3.enter_context(nc.sbuf_tensor("ln3_stats", [128, NTT, 24], F32)),
            "mv8": sub3.enter_context(nc.sbuf_tensor("ln3_mv8", [128, NTT, 2], F32)),
            "veps8": sub3.enter_context(nc.sbuf_tensor("ln3_veps", [128, NTT], F32)),
            "rstd8": sub3.enter_context(nc.sbuf_tensor("ln3_rstd", [128, NTT], F32)),
            "negmr8": sub3.enter_context(nc.sbuf_tensor("ln3_negmr", [128, NTT], F32)),
        }
        xtokbf = sub3.enter_context(nc.sbuf_tensor("ln3_tokbf", [128, NTT, 2048], BF16))
        emit_ln_and_xfm(k, stream, x_fm, ln_sc, ident_sb, pT, xtokbf)


def build_l1ffn(nc, ctx, k, io, persist):
    stream, x_fm, ident_sb, onesr_sb, pT = (persist["stream"], persist["x_fm"],
                                            persist["ident_sb"], persist["onesr_sb"],
                                            persist["pT"])
    NAR = 1024
    b2_v = io["B2"].rearrange("(ac p) j -> p ac j", p=128)
    wt_v = io["WT"].rearrange("(c p) j -> p c j", p=128)

    with contextlib.ExitStack() as sub:
        sbt = lambda n, s, d: sub.enter_context(nc.sbuf_tensor(n, s, d))
        c_fm = sbt("f_cfm", [128, 8, SQ], BF16)
        pV = sub.enter_context(nc.psum_tensor("f_pV", [128, 2, 512], F32))
        sl_v = k.new_slots(2)
        subx = contextlib.ExitStack()
        vecs_sb = subx.enter_context(nc.sbuf_tensor("f_vecs", [128, NTT, NAR], F32))

        with contextlib.ExitStack() as suba:
            a2t_st = suba.enter_context(nc.sbuf_tensor("f_a2st", [128, NCF, 256], F32))
            a2t_bf = suba.enter_context(nc.sbuf_tensor("f_a2bf", [128, NCF, NAR], BF16))
            sl_a2st = k.new_slots(1)
            cast_marks = []
            for gi in range(4):
                def s_a2(s, gi=gi):
                    if gi == 0:
                        k.fence(s)
                    k.slot_write_waits(s, sl_a2st[0])
                    k.inc(s.dma_start(
                        out=a2t_st[:, :, :],
                        in_=_fm(io["A2T"])[:, :, gi * 256:(gi + 1) * 256]), "dma", 16)
                    k.dma_flush(s)
                k.sync(s_a2)
                k.slot_wrote(sl_a2st[0], "dma")
                dk_g = k.cnt["dk"]

                def sc_a2(sc, gi=gi, dk_g=dk_g):
                    sc.wait_ge(k.sems["dk"], dk_g)
                    if gi == 0:
                        k.fence(sc, sems=("pe", "ve"))
                    k.inc(sc.copy(out=a2t_bf[:, :, gi * 256:(gi + 1) * 256],
                                  in_=a2t_st[:, :, :]), "sc")
                k.scalar(sc_a2)
                cast_marks.append(k.cnt["sc"])
                k.slot_did_read(sl_a2st[0], "sc")

            for t in range(NTT):
                for vb in range(2):
                    ps = (t * 2 + vb) % 2

                    def t_v(tn, t=t, vb=vb, ps=ps, m=cast_marks[-1]):
                        tn.wait_ge(k.sems["sc"], m)
                        k.slot_write_waits(tn, sl_v[ps])
                        ins = None
                        for c in range(NCF):
                            ins = tn.matmul(pV[:, ps, :], x_fm[:, c, t * 128:(t + 1) * 128],
                                            a2t_bf[:, c, vb * 512:(vb + 1) * 512],
                                            start=(c == 0), stop=(c == NCF - 1))
                        k.inc(ins, "pe")
                    k.tensor(t_v)
                    k.slot_wrote(sl_v[ps])

                    def v_ve(v, t=t, vb=vb, ps=ps):
                        k.slot_read_waits(v, sl_v[ps])
                        k.inc(v.tensor_copy(out=vecs_sb[:, t, vb * 512:(vb + 1) * 512],
                                            in_=pV[:, ps, :]), "ve")
                    k.vector(v_ve)
                    k.slot_did_read(sl_v[ps], "ve")
        vecs_ve = k.cnt["ve"]

        subr = contextlib.ExitStack()
        sq = subr.enter_context(nc.sbuf_tensor("f_sq", [128, NTT, NAR], F32))
        arr = subr.enter_context(nc.sbuf_tensor("f_arr", [128, NTT, 64], F32))
        routes = subr.enter_context(nc.sbuf_tensor("f_routes", [128, NTT, 64], F32))
        eqb = subr.enter_context(nc.sbuf_tensor("f_eq", [128, NTT, 64], F32))
        v16 = subr.enter_context(nc.sbuf_tensor("f_v16", [128, NTT, 16], F32))
        wk8 = subr.enter_context(nc.sbuf_tensor("f_wk8", [128, NTT], F32))
        c_bf = subr.enter_context(nc.sbuf_tensor("f_cbf", [128, NTT, NAR], BF16))

        def sc_sq(sc, vecs_ve=vecs_ve):
            sc.wait_ge(k.sems["ve"], vecs_ve)
            k.fence(sc, sems=("pe", "ve"))
            last = None
            for t in range(NTT):
                last = sc.activation(out=sq[:, t, :], in_=vecs_sb[:, t, :],
                                     func=AF.Square)
            k.inc(last, "sc")
        k.scalar(sc_sq)
        sc_sq_c = k.cnt["sc"]

        def v_route(v, sc_sq_c=sc_sq_c):
            v.wait_ge(k.sems["sc"], sc_sq_c)
            sqv = sq.rearrange("p t (a r) -> p t a r", r=16)
            for width in (8, 4, 2, 1):
                for t in range(NTT):
                    v.tensor_tensor(out=sqv[:, t, :, 0:width],
                                    in0=sqv[:, t, :, 0:width],
                                    in1=sqv[:, t, :, width:2 * width], op=ALU.add)
                v.drain()
            for t in range(NTT):
                v.tensor_copy(out=arr[:, t, :], in_=sqv[:, t, :, 0])
            v.memset(routes.rearrange("p a b -> p (a b)"), 0.0)
            v.drain()
            for kk in range(4):
                for t in range(NTT):
                    v.reduce_max(out=wk8[:, t:t + 1], in_=arr[:, t, :], axis=AX.X)
                v.drain()
                for t in range(NTT):
                    v.tensor_scalar(out=eqb[:, t, :], in0=arr[:, t, :],
                                    scalar1=wk8[:, t:t + 1], scalar2=None,
                                    op0=ALU.is_equal)
                v.drain()
                for t in range(NTT):
                    v.tensor_tensor(out=routes[:, t, :], in0=routes[:, t, :],
                                    in1=eqb[:, t, :], op=ALU.add)
                for t in range(NTT):
                    v.scalar_tensor_tensor(out=arr[:, t, :], in0=eqb[:, t, :],
                                           scalar=-1e30, in1=arr[:, t, :],
                                           op0=ALU.mult, op1=ALU.add)
                v.drain()
            last = v.tensor_scalar(out=routes.rearrange("p a b -> p (a b)"),
                                   in0=routes.rearrange("p a b -> p (a b)"),
                                   scalar1=0.25, scalar2=None, op0=ALU.mult)
            v.drain()
            for t in range(NTT):
                v.tensor_tensor(out=sq[:, t, :], in0=vecs_sb[:, t, :],
                                in1=routes[:, t, :].unsqueeze(2)
                                    .broadcast_to([128, 64, 16]), op=ALU.mult)
            v.drain()
            for width in (512, 256, 128, 64, 32, 16):
                for t in range(NTT):
                    v.tensor_tensor(out=sq[:, t, 0:width], in0=sq[:, t, 0:width],
                                    in1=sq[:, t, width:2 * width], op=ALU.add)
                v.drain()
            for t in range(NTT):
                v.tensor_copy(out=v16[:, t, :], in_=sq[:, t, 0:16])
            v.drain()
            last = None
            for t in range(NTT):
                last = v.tensor_tensor(
                    out=c_bf[:, t, :],
                    in0=routes[:, t, :].unsqueeze(2).broadcast_to([128, 64, 16]),
                    in1=v16[:, t, :].unsqueeze(1).broadcast_to([128, 64, 16]),
                    op=ALU.mult)
            k.inc(last, "ve")
            v.drain()
        k.vector(v_route)
        c_ve = k.cnt["ve"]

        tpg = [0] * (NTT * 2)
        evg = [0] * (NTT * 2)
        for t in range(NTT):
            for g in range(2):
                gi = t * 2 + g

                def t_c(tn, t=t, g=g, gi=gi, c_ve=c_ve):
                    if gi == 0:
                        tn.wait_ge(k.sems["ve"], c_ve)
                    if gi >= 2:
                        tn.wait_ge(k.sems["ve"], evg[gi - 2])
                    last = None
                    for i in range(4):
                        ac = g * 4 + i
                        last = tn.transpose(pT[:, gi % 2, i * 128:(i + 1) * 128],
                                            c_bf[:, t, ac * 128:(ac + 1) * 128],
                                            ident_sb[:, :])
                    tpg[gi] = k.inc(last, "pe")
                k.tensor(t_c)

                def v_c(v, t=t, g=g, gi=gi):
                    v.wait_ge(k.sems["pe"], tpg[gi])
                    evg[gi] = k.inc(v.tensor_copy(
                        out=c_fm[:, g * 4:(g + 1) * 4, t * 128:(t + 1) * 128],
                        in_=pT[:, gi % 2, 0:512].rearrange("p (i m) -> p i m", i=4)), "ve")
                k.vector(v_c)
        cfm_ve = k.cnt["ve"]
        subr.close()
        subx.close()

        with contextlib.ExitStack() as subo:
            w_st = subo.enter_context(nc.sbuf_tensor("f_wst", [128, NCF, 512], F32))
            w_bf = subo.enter_context(nc.sbuf_tensor("f_wbf", [128, NCF, 512], BF16))
            b2_st = subo.enter_context(nc.sbuf_tensor("f_b2st", [128, 8, 512], F32))
            b2_bf = subo.enter_context(nc.sbuf_tensor("f_b2bf", [128, 8, 512], BF16))
            biasrow = subo.enter_context(nc.sbuf_tensor("f_bias", [1, 512], F32))
            biasbf = subo.enter_context(nc.sbuf_tensor("f_biasbf", [1, 512], BF16))
            sl_wb = k.new_slots(1)
            for jb in range(4):
                pe_prev = k.cnt["pe"]

                def s_w(s, jb=jb):
                    if jb == 0:
                        k.fence(s)
                    k.slot_write_waits(s, sl_wb[0])
                    k.wait(s, "sc")
                    k.inc(s.dma_start(out=w_st[:, :, :],
                                      in_=wt_v[:, :, jb * 512:(jb + 1) * 512]), "dma", 16)
                    k.inc(s.dma_start(out=b2_st[:, :, :],
                                      in_=b2_v[:, :, jb * 512:(jb + 1) * 512]), "dma", 16)
                    k.inc(s.dma_start(out=biasrow[:, :],
                                      in_=io["bias"][:, jb * 512:(jb + 1) * 512]), "dma", 16)
                    k.dma_flush(s)
                k.sync(s_w)
                dk_w = k.cnt["dk"]

                def sc_w(sc, jb=jb, dk_w=dk_w, pe_prev=pe_prev):
                    sc.wait_ge(k.sems["dk"], dk_w)
                    if jb > 0:
                        sc.wait_ge(k.sems["pe"], pe_prev)
                    sc.copy(out=w_bf.rearrange("p c m -> p (c m)"),
                            in_=w_st.rearrange("p c m -> p (c m)"))
                    sc.copy(out=b2_bf.rearrange("p c m -> p (c m)"),
                            in_=b2_st.rearrange("p c m -> p (c m)"))
                    k.inc(sc.copy(out=biasbf[:, :], in_=biasrow[:, :]), "sc")
                k.scalar(sc_w)
                sc_wm = k.cnt["sc"]

                for t in range(NTT):
                    ps = t % 2

                    def t_out(tn, t=t, jb=jb, ps=ps, sc_wm=sc_wm, cfm_ve=cfm_ve):
                        tn.wait_ge(k.sems["sc"], sc_wm)
                        if t == 0 and jb == 0:
                            tn.wait_ge(k.sems["ve"], cfm_ve)
                        k.slot_write_waits(tn, sl_v[ps])
                        for c in range(NCF):
                            tn.matmul(pV[:, ps, :], x_fm[:, c, t * 128:(t + 1) * 128],
                                      w_bf[:, c, :], start=(c == 0), stop=False)
                        for ac in range(8):
                            tn.matmul(pV[:, ps, :], c_fm[:, ac, t * 128:(t + 1) * 128],
                                      b2_bf[:, ac, :], start=False, stop=False)
                        ins = tn.matmul(pV[:, ps, :], onesr_sb[:, :],
                                        biasbf[:, :],
                                        start=False, stop=True)
                        k.inc(ins, "pe")
                    k.tensor(t_out)
                    k.slot_wrote(sl_v[ps])

                    def v_out(v, t=t, jb=jb, ps=ps):
                        k.slot_read_waits(v, sl_v[ps])
                        k.inc(v.tensor_tensor(
                            out=stream[:, t, jb * 512:(jb + 1) * 512],
                            in0=stream[:, t, jb * 512:(jb + 1) * 512],
                            in1=pV[:, ps, :], op=ALU.add), "ve")
                    k.vector(v_out)
                    k.slot_did_read(sl_v[ps], "ve")
                k.slot_did_read(sl_wb[0], "pe")
            def v_fdrain(v):
                v.drain()
            k.vector(v_fdrain)

    with contextlib.ExitStack() as sub3:
        ln_sc = {
            "stats": sub3.enter_context(nc.sbuf_tensor("ln4_stats", [128, NTT, 24], F32)),
            "mv8": sub3.enter_context(nc.sbuf_tensor("ln4_mv8", [128, NTT, 2], F32)),
            "veps8": sub3.enter_context(nc.sbuf_tensor("ln4_veps", [128, NTT], F32)),
            "rstd8": sub3.enter_context(nc.sbuf_tensor("ln4_rstd", [128, NTT], F32)),
            "negmr8": sub3.enter_context(nc.sbuf_tensor("ln4_negmr", [128, NTT], F32)),
        }
        emit_ln_apply_only(k, stream, ln_sc)


def emit_ln_apply_only(k, stream, ln_sc):
    stats, mv8, veps8, rstd8, negmr8 = (ln_sc["stats"], ln_sc["mv8"], ln_sc["veps8"],
                                        ln_sc["rstd8"], ln_sc["negmr8"])

    def v1(v):
        k.wait(v, "pe")
        for t in range(NTT):
            for g in range(4):
                v.bn_stats(out=stats[:, t, g * 6:(g + 1) * 6],
                           in_=stream[:, t, g * 512:(g + 1) * 512])
        v.drain()
        for t in range(NTT):
            v.bn_aggr(out=mv8[:, t, :], in_=stats[:, t, :])
        v.drain()
        last = None
        for t in range(NTT):
            last = v.tensor_scalar(out=veps8[:, t:t + 1], in0=mv8[:, t, 1:2],
                                   scalar1=LN_EPS, scalar2=None, op0=ALU.add)
        k.inc(last, "ve")
    k.vector(v1)
    ve_ready = k.cnt["ve"]

    def s1(sc):
        sc.wait_ge(k.sems["ve"], ve_ready)
        k.inc(sc.activation(out=rstd8[:, :], in_=veps8[:, :], func=AF.Sqrt), "sc")
    k.scalar(s1)
    sc_ready = k.cnt["sc"]

    def v2(v):
        v.wait_ge(k.sems["sc"], sc_ready)
        v.reciprocal(out=rstd8[:, :], in_=rstd8[:, :])
        v.drain()
        v.scalar_tensor_tensor(out=negmr8[:, :],
                               in0=mv8.rearrange("p t s -> p (t s)")[:, 0::2],
                               scalar=-1.0, in1=rstd8[:, :],
                               op0=ALU.mult, op1=ALU.mult)
        v.drain()
        last = None
        for t in range(NTT):
            last = v.tensor_scalar(out=stream[:, t, :], in0=stream[:, t, :],
                                   scalar1=rstd8[:, t:t + 1], scalar2=negmr8[:, t:t + 1],
                                   op0=ALU.mult, op1=ALU.add)
        k.inc(last, "ve")
        v.drain()
    k.vector(v2)


def emit_store_stream(k, stream, out_dram):
    def s_out(s):
        k.wait(s, "ve")
        for t in range(NTT):
            k.inc(s.dma_start(out=out_dram[t * 128:(t + 1) * 128, :],
                              in_=stream[:, t, :]), "dma", 16)
        k.wait(s, "dma")
    k.sync(s_out)


_PROG_CACHE = {}


def build_program(blocks):
    nc = bass.Bass("TRN2", target_bir_lowering=False)
    io = {}
    if "l0attn" in blocks:
        io["xk"] = nc.dram_tensor("xk", [SK, D], F32, kind="ExternalInput").ap()
        io["wqT"] = nc.dram_tensor("wqT", [D, D], F32, kind="ExternalInput").ap()
        io["wkT"] = nc.dram_tensor("wkT", [D, D], F32, kind="ExternalInput").ap()
        io["wvT"] = nc.dram_tensor("wvT", [D, D], F32, kind="ExternalInput").ap()
        io["woT"] = nc.dram_tensor("woT", [D, D], F32, kind="ExternalInput").ap()
        io["masks"] = nc.dram_tensor("masks", [32 * 128, 512], BF16, kind="ExternalInput").ap()
    if "moe" in blocks:
        if "l0attn" not in blocks:
            io["x1"] = nc.dram_tensor("x1", [SQ, D], F32, kind="ExternalInput").ap()
        io["gate_wT"] = nc.dram_tensor("gate_wT", [D, 32], F32, kind="ExternalInput").ap()
        io["Wg"] = nc.dram_tensor("Wg", [32, D, 768], F32, kind="ExternalInput").ap()
        io["Wu"] = nc.dram_tensor("Wu", [32, D, 768], F32, kind="ExternalInput").ap()
        io["Wd"] = nc.dram_tensor("Wd", [32, 768, D], F32, kind="ExternalInput").ap()
        io["cWg"] = nc.dram_tensor("cWg", [16, D, 128], F32, kind="ExternalInput").ap()
        io["cWu"] = nc.dram_tensor("cWu", [16, D, 128], F32, kind="ExternalInput").ap()
        io["cWd"] = nc.dram_tensor("cWd", [16, 128, D], F32, kind="ExternalInput").ap()
    if "l1attn" in blocks:
        if blocks[0] == "l1attn":
            io["x2"] = nc.dram_tensor("x2", [SQ, D], F32, kind="ExternalInput").ap()
            io["x2o"] = nc.dram_tensor("x2o", [SQ, D], F32, kind="ExternalInput").ap()
        else:
            io["cc_in"] = nc.dram_tensor("cc_in", [SQ, D], BF16).ap()
            io["cc_out"] = nc.dram_tensor("cc_out", [SK, D], BF16).ap()
        io["gammac"] = nc.dram_tensor("gammac", [128, 1], F32, kind="ExternalInput").ap()
    if "l1ffn" in blocks:
        if blocks[0] == "l1ffn":
            io["x3"] = nc.dram_tensor("x3", [SQ, D], F32, kind="ExternalInput").ap()
        io["A2T"] = nc.dram_tensor("A2T", [D, 1024], F32, kind="ExternalInput").ap()
        io["B2"] = nc.dram_tensor("B2", [1024, D], F32, kind="ExternalInput").ap()
        io["WT"] = nc.dram_tensor("WT", [D, D], F32, kind="ExternalInput").ap()
        io["bias"] = nc.dram_tensor("bias", [1, D], F32, kind="ExternalInput").ap()
    io["ident"] = nc.dram_tensor("ident", [128, 128], BF16, kind="ExternalInput").ap()
    io["actc"] = nc.dram_tensor("actc", [128, 2], F32, kind="ExternalInput").ap()
    io["onesr"] = nc.dram_tensor("onesr", [1, 128], BF16, kind="ExternalInput").ap()
    io["onesc"] = nc.dram_tensor("onesc", [128, 1], BF16, kind="ExternalInput").ap()
    out = nc.dram_tensor("out", [SQ, D], F32, kind="ExternalOutput").ap()
    if DEBUG:
        io["dbg_q"] = nc.dram_tensor("dbg_q", [128, SQ], BF16, kind="ExternalOutput").ap()
        io["dbg_k"] = nc.dram_tensor("dbg_k", [128, SK], BF16, kind="ExternalOutput").ap()
        io["dbg_v"] = nc.dram_tensor("dbg_v", [128, NKT * HD], BF16, kind="ExternalOutput").ap()
        io["dbg_exp"] = nc.dram_tensor("dbg_exp", [128, NKT * 512], BF16, kind="ExternalOutput").ap()
        io["dbg_z"] = nc.dram_tensor("dbg_z", [2, 512], F32, kind="ExternalOutput").ap()
        io["dbg_ao"] = nc.dram_tensor("dbg_ao", [128, NCF * SQ], BF16, kind="ExternalOutput").ap()
        io["dbg_res"] = nc.dram_tensor("dbg_res", [SQ, D], F32, kind="ExternalOutput").ap()
        io["dbg_xfm"] = nc.dram_tensor("dbg_xfm", [128, 2048], BF16, kind="ExternalOutput").ap()

    with contextlib.ExitStack() as ctx:
        block = ctx.enter_context(nc.Block())
        k = K(nc, ctx, block)
        persist = {
            "ident_sb": k.sb("ident_sb", [128, 128], BF16),
            "onesr_sb": k.sb("onesr_sb", [1, 128], BF16),
            "onesc_sb": k.sb("onesc_sb", [128, 1], BF16),
            "actc_sb": k.sb("actc_sb", [128, 2], F32),
            "pT": k.ps("pT", [128, 2, 1024], BF16),
        }
        for bi, b in enumerate(blocks):
            if b == "l0attn":
                build_l0_attn(nc, ctx, k, io, persist)
            elif b == "moe":
                if bi == 0:
                    emit_standalone_entry(k, nc, ctx, io, persist, "x1")
                build_moe(nc, ctx, k, io, persist)
            elif b == "l1attn":
                if bi == 0:
                    emit_standalone_entry(k, nc, ctx, io, persist, "x2")
                build_l1attn(nc, ctx, k, io, persist)
            elif b == "l1ffn":
                if bi == 0:
                    emit_standalone_entry(k, nc, ctx, io, persist, "x3")
                build_l1ffn(nc, ctx, k, io, persist)
        emit_store_stream(k, persist["stream"], out)
    return nc


def _prep_l0_inputs(inputs, core):
    b, half = core // 2, core % 2
    T0 = half * SQ
    x = np.asarray(inputs["L0_attn_x"])
    d = {}
    perm = np.concatenate([np.arange(T0, T0 + SQ), np.arange((1 - half) * SQ, (2 - half) * SQ)])
    d["xk"] = np.ascontiguousarray(x[b][perm])
    d["wqT"] = np.ascontiguousarray(np.asarray(inputs["L0_attn_wq"]).T)
    d["wkT"] = np.ascontiguousarray(np.asarray(inputs["L0_attn_wk"]).T)
    d["wvT"] = np.ascontiguousarray(np.asarray(inputs["L0_attn_wv"]).T)
    d["woT"] = np.ascontiguousarray(np.asarray(inputs["L0_attn_wo"]).T)
    kg = perm.reshape(NKT, 128)
    qj = (np.arange(2)[:, None] * 512 + np.arange(512)[None, :] + T0)
    m = (kg[None, :, :, None] - qj[:, None, None, :]) <= 128
    d["masks"] = np.ascontiguousarray(
        m.astype(bf).reshape(2 * NKT * 128, 512))
    d.update(_prep_consts())
    return d


def _prep_consts():
    return {
        "ident": np.eye(128, dtype=bf),
        "actc": np.tile(np.array([[ATT_SCALE, -EXP_SHIFT]], np.float32), (128, 1)),
        "onesr": np.ones((1, 128), dtype=bf),
        "onesc": np.ones((128, 1), dtype=bf),
    }


def _prep_l1attn_inputs(inputs, core, x2=None):
    d = {}
    b, half = core // 2, core % 2
    if x2 is not None:
        d["x2"] = np.ascontiguousarray(x2[b, half * SQ:(half + 1) * SQ])
        d["x2o"] = np.ascontiguousarray(x2[b, (1 - half) * SQ:(2 - half) * SQ])
    g = float(np.asarray(inputs["L1_attn_gamma"]).reshape(-1)[0])
    d["gammac"] = np.full((128, 1), g, np.float32)
    d.update(_prep_consts())
    return d


def _prep_l1ffn_inputs(inputs, core, x3=None):
    d = {}
    b, half = core // 2, core % 2
    if x3 is not None:
        d["x3"] = np.ascontiguousarray(x3[b, half * SQ:(half + 1) * SQ])
    A = np.asarray(inputs["L1_ffn_A"])
    d["A2T"] = np.ascontiguousarray(A.reshape(1024, 2048).T)
    B = np.asarray(inputs["L1_ffn_B"])
    d["B2"] = np.ascontiguousarray(B.transpose(0, 2, 1).reshape(1024, 2048))
    d["WT"] = np.ascontiguousarray(np.asarray(inputs["L1_ffn_W"]).T)
    d["bias"] = np.ascontiguousarray(np.asarray(inputs["L1_ffn_bias"]).reshape(1, -1))
    d.update(_prep_consts())
    return d


def _prep_moe_inputs(inputs, core, x1=None):
    d = {}
    if x1 is not None:
        b, half = core // 2, core % 2
        d["x1"] = np.ascontiguousarray(x1[b, half * SQ:(half + 1) * SQ])
    d["gate_wT"] = np.ascontiguousarray(np.asarray(inputs["L0_ffn_gate_w"]).T)
    d["Wg"] = np.ascontiguousarray(np.asarray(inputs["L0_ffn_Wg"]))
    d["Wu"] = np.ascontiguousarray(np.asarray(inputs["L0_ffn_Wu"]))
    d["Wd"] = np.ascontiguousarray(np.asarray(inputs["L0_ffn_Wd"]))
    d["cWg"] = np.ascontiguousarray(np.asarray(inputs["L0_ffn_cWg"]))
    d["cWu"] = np.ascontiguousarray(np.asarray(inputs["L0_ffn_cWu"]))
    d["cWd"] = np.ascontiguousarray(np.asarray(inputs["L0_ffn_cWd"]))
    d.update(_prep_consts())
    return d


FUSED = int(os.environ.get("KFUSED", "1"))
ALL_BLOCKS = ("l0attn", "moe", "l1attn", "l1ffn")
LAUNCHES = (("l0attn",), ("moe",), ("l1attn",), ("l1ffn",))


def _run_launch(blocks, in_maps):
    if blocks not in _PROG_CACHE:
        _PROG_CACHE[blocks] = build_program(blocks)
    res = run_bass_kernel_spmd(_PROG_CACHE[blocks], in_maps, core_ids=list(range(8)))
    out = np.empty((4, SK, D), dtype=np.float32)
    for core in range(8):
        b, half = core // 2, core % 2
        out[b, half * SQ:(half + 1) * SQ] = res.results[core]["out"]
    return out


def kernel(**inputs):
    if FUSED:
        in_maps = []
        for core in range(8):
            d = _prep_l0_inputs(inputs, core)
            d.update(_prep_moe_inputs(inputs, core))
            d.update(_prep_l1attn_inputs(inputs, core))
            d.update(_prep_l1ffn_inputs(inputs, core))
            in_maps.append(d)
        return _run_launch(ALL_BLOCKS, in_maps)
    x = None
    for blocks in LAUNCHES:
        if blocks == ("l0attn",):
            in_maps = [_prep_l0_inputs(inputs, core) for core in range(8)]
        elif blocks == ("moe",):
            in_maps = [_prep_moe_inputs(inputs, core, x1=x) for core in range(8)]
        elif blocks == ("l1attn",):
            in_maps = [_prep_l1attn_inputs(inputs, core, x2=x) for core in range(8)]
        elif blocks == ("l1ffn",):
            in_maps = [_prep_l1ffn_inputs(inputs, core, x3=x) for core in range(8)]
        x = _run_launch(blocks, in_maps)
    return x


# version 07
# speedup vs baseline: 1.2214x; 1.1280x over previous
import contextlib
import os
import numpy as np
import ml_dtypes
import concourse.bass as bass
import concourse.mybir as mybir
from concourse.bass_utils import run_bass_kernel_spmd

DEBUG = int(os.environ.get("KDBG", "0"))
F32 = mybir.dt.float32
BF16 = mybir.dt.bfloat16
AF = mybir.ActivationFunctionType
ALU = mybir.AluOpType
AX = mybir.AxisListType
bf = ml_dtypes.bfloat16

D = 2048
NH = 16
HD = 128
SQ = 1024
SK = 2048
NCF = 16
NTT = 8
NKT = 16
EXP_SHIFT = 4.0
ATT_SCALE = float(HD) ** -0.5
LN_EPS = 1e-5


class K:

    def __init__(self, nc, ctx, block):
        self.nc = nc
        self.ctx = ctx
        self.block = block
        self.sems = {}
        self.cnt = {}
        for name in ("dma", "dk", "dw0", "dw1", "pe", "ve", "sc", "gp", "cc"):
            self.sems[name] = ctx.enter_context(nc.semaphore(name))
            self.cnt[name] = 0

    def inc(self, instr, sem, k=1):
        instr.then_inc(self.sems[sem], k)
        self.cnt[sem] += k
        return self.cnt[sem]

    def wait(self, eng, sem, val=None):
        v = self.cnt[sem] if val is None else val
        if v > 0:
            eng.wait_ge(self.sems[sem], v)

    def new_slots(self, n):
        return [{"w": None, "r": []} for _ in range(n)]

    def slot_write_waits(self, eng, slot):
        for s, c in slot["r"]:
            eng.wait_ge(self.sems[s], c)

    def slot_wrote(self, slot, sem="pe"):
        slot["w"] = (sem, self.cnt[sem])
        slot["r"] = []

    def slot_read_waits(self, eng, slot):
        if slot["w"] is not None:
            s, c = slot["w"]
            eng.wait_ge(self.sems[s], c)

    def slot_did_read(self, slot, sem):
        slot["r"].append((sem, self.cnt[sem]))

    def dma_flush(self, s):
        s.wait_ge(self.sems["dma"], self.cnt["dma"])
        self.inc(s.nop(), "dk")

    def fence(self, eng, sems=("pe", "ve", "sc")):
        for s in sems:
            self.wait(eng, s)

    def sb(self, name, shape, dt):
        return self.ctx.enter_context(self.nc.sbuf_tensor(name, shape, dt))

    def ps(self, name, shape, dt=F32):
        return self.ctx.enter_context(self.nc.psum_tensor(name, shape, dt))

    def sync(self, fn):
        self.block.sync(fn)

    def tensor(self, fn):
        self.block.tensor(fn)

    def vector(self, fn):
        self.block.vector(fn)

    def scalar(self, fn):
        self.block.scalar(fn)

    def gpsimd(self, fn):
        self.block.gpsimd(fn)


def _fm(dram2d):
    return dram2d.rearrange("(c p) m -> p c m", p=128)


def emit_load_cast_transpose(k, dram_rows, n_tiles, stage, tokbf, ident_sb, pT, dst_fm,
                             dst_col0=0):
    cast_done = [0] * n_tiles
    tpg = [0] * (n_tiles * 4)
    evg = [0] * (n_tiles * 4)
    entry_ve = k.cnt["ve"]
    entry_pe = k.cnt["pe"]
    for t in range(n_tiles):
        def s_body(s, t=t):
            if t >= 2:
                s.wait_ge(k.sems["ve"], cast_done[t - 2])
            elif entry_ve > 0:
                s.wait_ge(k.sems["ve"], entry_ve)
            k.inc(s.dma_start(out=stage[:, t % 2, :],
                              in_=dram_rows[t * 128:(t + 1) * 128, :]), "dma", 16)
            k.dma_flush(s)
        k.sync(s_body)
        dk_now = k.cnt["dk"]

        def v_cast(v, t=t, dk_now=dk_now):
            v.wait_ge(k.sems["dk"], dk_now)
            if t >= 2:
                v.wait_ge(k.sems["pe"], tpg[(t - 2) * 4 + 3])
            elif entry_pe > 0:
                v.wait_ge(k.sems["pe"], entry_pe)
            cast_done[t] = k.inc(
                v.tensor_copy(out=tokbf[:, t % 2, :], in_=stage[:, t % 2, :]), "ve")
        k.vector(v_cast)

        for g in range(4):
            def t_body(tn, t=t, g=g):
                tn.wait_ge(k.sems["ve"], cast_done[t])
                gi = t * 4 + g
                if gi >= 2:
                    tn.wait_ge(k.sems["ve"], evg[gi - 2])
                elif entry_ve > 0:
                    tn.wait_ge(k.sems["ve"], entry_ve)
                last = None
                for i in range(4):
                    c = g * 4 + i
                    last = tn.transpose(pT[:, gi % 2, i * 128:(i + 1) * 128],
                                        tokbf[:, t % 2, c * 128:(c + 1) * 128],
                                        ident_sb[:, :])
                tpg[gi] = k.inc(last, "pe")
            k.tensor(t_body)

            def v_evac(v, t=t, g=g):
                gi = t * 4 + g
                v.wait_ge(k.sems["pe"], tpg[gi])
                evg[gi] = k.inc(v.tensor_copy(
                    out=dst_fm[:, g * 4:(g + 1) * 4,
                               dst_col0 + t * 128:dst_col0 + (t + 1) * 128],
                    in_=pT[:, gi % 2, 0:512].rearrange("p (i m) -> p i m", i=4)), "ve")
            k.vector(v_evac)

    def v_end(v):
        v.drain()
    k.vector(v_end)


def emit_ln_and_xfm(k, stream, x_fm, ln_sc, ident_sb, pT, xtokbf):
    stats, mv8, veps8, rstd8, negmr8 = (ln_sc["stats"], ln_sc["mv8"], ln_sc["veps8"],
                                        ln_sc["rstd8"], ln_sc["negmr8"])

    def v1(v):
        k.wait(v, "pe")
        for t in range(NTT):
            for g in range(4):
                v.bn_stats(out=stats[:, t, g * 6:(g + 1) * 6],
                           in_=stream[:, t, g * 512:(g + 1) * 512])
        v.drain()
        for t in range(NTT):
            v.bn_aggr(out=mv8[:, t, :], in_=stats[:, t, :])
        v.drain()
        last = None
        for t in range(NTT):
            last = v.tensor_scalar(out=veps8[:, t:t + 1], in0=mv8[:, t, 1:2],
                                   scalar1=LN_EPS, scalar2=None, op0=ALU.add)
        k.inc(last, "ve")
    k.vector(v1)
    ve_ready = k.cnt["ve"]

    def s1(sc):
        sc.wait_ge(k.sems["ve"], ve_ready)
        k.inc(sc.activation(out=rstd8[:, :], in_=veps8[:, :], func=AF.Sqrt), "sc")
    k.scalar(s1)
    sc_ready = k.cnt["sc"]

    def v2(v):
        v.wait_ge(k.sems["sc"], sc_ready)
        v.reciprocal(out=rstd8[:, :], in_=rstd8[:, :])
        v.drain()
        v.scalar_tensor_tensor(out=negmr8[:, :],
                               in0=mv8.rearrange("p t s -> p (t s)")[:, 0::2],
                               scalar=-1.0, in1=rstd8[:, :],
                               op0=ALU.mult, op1=ALU.mult)
        v.drain()
        for t in range(NTT):
            v.tensor_scalar(out=stream[:, t, :], in0=stream[:, t, :],
                            scalar1=rstd8[:, t:t + 1], scalar2=negmr8[:, t:t + 1],
                            op0=ALU.mult, op1=ALU.add)
        v.drain()
        for t in range(NTT):
            ins = v.tensor_copy(out=xtokbf[:, t, :], in_=stream[:, t, :])
            k.inc(ins, "ve")
    k.vector(v2)
    cast_base = k.cnt["ve"] - NTT

    tpg = [0] * (NTT * 4)
    evg = [0] * (NTT * 4)
    for t in range(NTT):
        for g in range(4):
            def t_body(tn, t=t, g=g):
                tn.wait_ge(k.sems["ve"], cast_base + t + 1)
                gi = t * 4 + g
                if gi >= 2:
                    tn.wait_ge(k.sems["ve"], evg[gi - 2])
                last = None
                for i in range(4):
                    c = g * 4 + i
                    last = tn.transpose(pT[:, gi % 2, i * 128:(i + 1) * 128],
                                        xtokbf[:, t, c * 128:(c + 1) * 128],
                                        ident_sb[:, :])
                tpg[gi] = k.inc(last, "pe")
            k.tensor(t_body)

            def v_evac(v, t=t, g=g):
                gi = t * 4 + g
                v.wait_ge(k.sems["pe"], tpg[gi])
                evg[gi] = k.inc(v.tensor_copy(
                    out=x_fm[:, g * 4:(g + 1) * 4, t * 128:(t + 1) * 128],
                    in_=pT[:, gi % 2, 0:512].rearrange("p (i m) -> p i m", i=4)), "ve")
            k.vector(v_evac)

    def v_end(v):
        v.drain()
    k.vector(v_end)


def build_l0_attn(nc, ctx, k, io, persist):
    ident_sb, onesr_sb, onesc_sb, actc_sb, pT = (
        persist["ident_sb"],
        persist["onesr_sb"], persist["onesc_sb"], persist["actc_sb"], persist["pT"])

    sub0 = contextlib.ExitStack()
    ao_fm = sub0.enter_context(nc.sbuf_tensor("a_aofm", [128, NCF, SQ], BF16))
    pS = sub0.enter_context(nc.psum_tensor("a_pS", [128, 4, 512], F32))
    pS_sl = k.new_slots(4)

    with contextlib.ExitStack() as sub:
        sbt = lambda n, s, d: sub.enter_context(nc.sbuf_tensor(n, s, d))
        xk_fm = sbt("a_xkfm", [128, NCF, SK], BF16)
        masks_sb = sbt("a_masks", [128, 32, 512], BF16)
        zrow = sbt("a_zrow", [1, 512], F32)
        zrec = sbt("a_zrec", [1, 512], F32)
        zbf = sbt("a_zbf", [1, 512], BF16)
        rzb = sbt("a_rzb", [128, 512], F32)
        xq_fm = xk_fm

        pZ = sub.enter_context(nc.psum_tensor("a_pZ", [128, 512], F32))
        pO = sub.enter_context(nc.psum_tensor("a_pO", [128, 512], F32))
        pC = pZ

        def c_sync(s):
            k.inc(s.dma_start(out=ident_sb[:, :], in_=io["ident"][:, :]), "dma", 16)
            k.inc(s.dma_start(out=onesr_sb[:, :], in_=io["onesr"][:, :]), "dma", 16)
            k.inc(s.dma_start(out=onesc_sb[:, :], in_=io["onesc"][:, :]), "dma", 16)
            k.inc(s.dma_start(out=actc_sb[:, :], in_=io["actc"][:, :]), "dma", 16)
            k.inc(s.dma_start(out=masks_sb[:, :, :],
                              in_=io["masks"].rearrange("(m p) j -> p m j", p=128)),
                  "dma", 16)
            k.dma_flush(s)
        k.sync(c_sync)
        dk_consts = k.cnt["dk"]

        with contextlib.ExitStack() as sub_in:
            stage = sub_in.enter_context(nc.sbuf_tensor("a_stage", [128, 2, 2048], F32))
            tokbf = sub_in.enter_context(nc.sbuf_tensor("a_tokbf", [128, 2, 2048], BF16))
            emit_load_cast_transpose(k, io["xk"], NKT, stage, tokbf, ident_sb, pT, xk_fm)

        input_pe = k.cnt["pe"]
        input_ve = k.cnt["ve"]
        stage_w1 = sbt("a_stw1", [128, 2, NCF, 128], F32)
        wT3 = sbt("a_wt3", [128, 2, 3, NCF, 128], BF16)
        q_h = sbt("a_qh", [128, SQ], BF16)
        k_h = sbt("a_kh", [128, SK], BF16)
        v_h = sbt("a_vh", [128, NKT, HD], BF16)
        exp_buf = sbt("a_exp", [128, NKT, 512], BF16)

        sl_stw = k.new_slots(2)
        sl_wt3 = k.new_slots(2)
        head_pe_end = [0] * (NH + 1)
        for h in range(NH):
            wb = h % 2
            for wi, wname in enumerate(("wqT", "wkT", "wvT")):
                ssl = (h * 3 + wi) % 2

                dwn = "dw%d" % ssl

                def w_sync(s, h=h, wi=wi, wname=wname, ssl=ssl, dwn=dwn):
                    k.slot_write_waits(s, sl_stw[ssl])
                    k.inc(s.dma_start(
                        out=stage_w1[:, ssl, :, :],
                        in_=io[wname][:, h * 128:(h + 1) * 128]
                            .rearrange("(c p) m -> p c m", p=128)), dwn, 16)
                k.sync(w_sync)
                k.slot_wrote(sl_stw[ssl], "dma")
                dw_w = k.cnt[dwn]

                def w_cast(v, h=h, wi=wi, wb=wb, ssl=ssl, dw_w=dw_w, dwn=dwn):
                    v.wait_ge(k.sems[dwn], dw_w)
                    if wi == 0:
                        k.slot_write_waits(v, sl_wt3[wb])
                    k.inc(v.tensor_copy(
                        out=wT3[:, wb, wi, :, :].rearrange("p c m -> p (c m)"),
                        in_=stage_w1[:, ssl, :, :].rearrange("p c m -> p (c m)")), "ve")
                k.vector(w_cast)
                k.slot_did_read(sl_stw[ssl], "ve")
            k.slot_wrote(sl_wt3[wb], "ve")
            sc_w = k.cnt["ve"]

            proj = ([("q", blk) for blk in range(2)] + [("kk", blk) for blk in range(4)]
                    + [("v", tk) for tk in range(NKT)])
            for kind, blk in proj:
                si = blk % 2

                def t_p(tn, h=h, kind=kind, blk=blk, si=si, wb=wb, sc_w=sc_w):
                    tn.wait_ge(k.sems["ve"], sc_w)
                    if h == 0:
                        tn.wait_ge(k.sems["dk"], dk_consts)
                    k.slot_write_waits(tn, pS_sl[si])
                    ins = None
                    for c in range(NCF):
                        if kind == "q":
                            ins = tn.matmul(pS[:, si, :], wT3[:, wb, 0, c, :],
                                            xq_fm[:, c, blk * 512:(blk + 1) * 512],
                                            start=(c == 0), stop=(c == NCF - 1))
                        elif kind == "kk":
                            ins = tn.matmul(pS[:, si, :], wT3[:, wb, 1, c, :],
                                            xk_fm[:, c, blk * 512:(blk + 1) * 512],
                                            start=(c == 0), stop=(c == NCF - 1))
                        else:
                            ins = tn.matmul(pS[:, si, 0:128],
                                            xk_fm[:, c, blk * 128:(blk + 1) * 128],
                                            wT3[:, wb, 2, c, :],
                                            start=(c == 0), stop=(c == NCF - 1))
                    k.inc(ins, "pe")
                k.tensor(t_p)
                k.slot_wrote(pS_sl[si])

                def v_p(v, kind=kind, blk=blk, si=si):
                    k.slot_read_waits(v, pS_sl[si])
                    if kind == "q":
                        ins = v.tensor_scalar(out=q_h[:, blk * 512:(blk + 1) * 512],
                                              in0=pS[:, si, :], scalar1=ATT_SCALE,
                                              scalar2=None, op0=ALU.mult)
                    elif kind == "kk":
                        ins = v.tensor_copy(out=k_h[:, blk * 512:(blk + 1) * 512],
                                            in_=pS[:, si, :])
                    else:
                        ins = v.tensor_copy(out=v_h[:, blk, :], in_=pS[:, si, 0:128])
                    k.inc(ins, "ve")
                k.vector(v_p)
                k.slot_did_read(pS_sl[si], "ve")
            k.slot_did_read(sl_wt3[wb], "pe")

            qkv_ve = k.cnt["ve"]

            for tqb in range(2):
                for tk2 in range(NKT // 2):
                    pg = (tk2 % 2) * 2

                    def t_s(tn, tk2=tk2, tqb=tqb, pg=pg, qkv_ve=qkv_ve, h=h):
                        if tk2 == 0 and tqb == 0:
                            tn.wait_ge(k.sems["ve"], qkv_ve)
                        k.slot_write_waits(tn, pS_sl[pg])
                        k.slot_write_waits(tn, pS_sl[pg + 1])
                        for si in range(2):
                            tk = tk2 * 2 + si
                            k.inc(tn.matmul(pS[:, pg + si, :], k_h[:, tk * 128:(tk + 1) * 128],
                                            q_h[:, tqb * 512:(tqb + 1) * 512],
                                            start=True, stop=True), "pe")
                    k.tensor(t_s)
                    k.slot_wrote(pS_sl[pg])
                    k.slot_wrote(pS_sl[pg + 1])

                    def s_exp(sc, tk2=tk2, tqb=tqb, pg=pg, h=h):
                        k.slot_read_waits(sc, pS_sl[pg])
                        k.slot_read_waits(sc, pS_sl[pg + 1])
                        if h == 0 and tqb == 0 and tk2 == 0:
                            sc.wait_ge(k.sems["dk"], dk_consts)
                        k.inc(sc.activation(
                            out=exp_buf[:, tk2 * 2:tk2 * 2 + 2, :]
                                .rearrange("p a b -> p (a b)"),
                            in_=pS[:, pg:pg + 2, :].rearrange("p a b -> p (a b)"),
                            func=AF.Exp), "sc")
                    k.scalar(s_exp)
                    k.slot_did_read(pS_sl[pg], "sc")
                    k.slot_did_read(pS_sl[pg + 1], "sc")
                sc_now = k.cnt["sc"]

                def v_mask(v, tqb=tqb, sc_now=sc_now, h=h):
                    v.wait_ge(k.sems["sc"], sc_now)
                    if h == 0 and tqb == 0:
                        v.wait_ge(k.sems["dk"], dk_consts)
                    k.inc(v.tensor_tensor(
                        out=exp_buf.rearrange("p a b -> p (a b)"),
                        in0=exp_buf.rearrange("p a b -> p (a b)"),
                        in1=masks_sb[:, tqb * 16:(tqb + 1) * 16, :]
                            .rearrange("p a b -> p (a b)"), op=ALU.mult), "ve")
                k.vector(v_mask)

                def t_zav(tn, tqb=tqb, h=h):
                    tn.wait_ge(k.sems["ve"], k.cnt["ve"])
                    for tk in range(NKT):
                        ins = tn.matmul(pZ[0:1, :], onesc_sb[:, :], exp_buf[:, tk, :],
                                        start=(tk == 0), stop=(tk == NKT - 1))
                    k.inc(ins, "pe")
                    for tk in range(NKT):
                        ins = tn.matmul(pO[:, :], v_h[:, tk, :], exp_buf[:, tk, :],
                                        start=(tk == 0), stop=(tk == NKT - 1))
                    k.inc(ins, "pe")
                k.tensor(t_zav)
                pe_zav = k.cnt["pe"]

                def v_z(v, pe_zav=pe_zav):
                    v.wait_ge(k.sems["pe"], pe_zav - 1)
                    v.tensor_copy(out=zrow[:, :], in_=pZ[0:1, :])
                    v.drain()
                    v.reciprocal(out=zrec[:, :], in_=zrow[:, :])
                    v.drain()
                    k.inc(v.tensor_copy(out=zbf[:, :], in_=zrec[:, :]), "ve")
                k.vector(v_z)
                ve_z = k.cnt["ve"]

                def t_bc(tn, ve_z=ve_z):
                    tn.wait_ge(k.sems["ve"], ve_z)
                    k.inc(tn.matmul(pC[:, :], onesr_sb[:, :], zbf[:, :],
                                    start=True, stop=True), "pe")
                k.tensor(t_bc)
                pe_bc = k.cnt["pe"]

                def v_norm(v, tqb=tqb, h=h, pe_bc=pe_bc):
                    v.wait_ge(k.sems["pe"], pe_bc)
                    v.tensor_copy(out=rzb[:, :], in_=pC[:, :])
                    v.drain()
                    k.inc(v.tensor_tensor(
                        out=ao_fm[:, h, tqb * 512:(tqb + 1) * 512],
                        in0=pO[:, :], in1=rzb[:, :], op=ALU.mult), "ve")
                k.vector(v_norm)

        heads_ve = k.cnt["ve"]

        if DEBUG in (4, 5):
            def dbg_sync2(s):
                k.wait(s, "ve")
                k.wait(s, "sc")
                if DEBUG == 4:
                    k.wait(s, "dma")
            k.sync(dbg_sync2)

        if DEBUG in (1, 2):
            def dbg_sync(s):
                k.wait(s, "ve")
                k.wait(s, "sc")
                k.inc(s.dma_start(out=io["dbg_q"][:, :], in_=q_h[:, :]), "dma", 16)
                k.inc(s.dma_start(out=io["dbg_k"][:, :], in_=k_h[:, :]), "dma", 16)
                k.inc(s.dma_start(out=io["dbg_v"][:, :],
                                  in_=v_h.rearrange("p a b -> p (a b)")), "dma", 16)
                k.inc(s.dma_start(out=io["dbg_exp"][:, :],
                                  in_=exp_buf.rearrange("p a b -> p (a b)")), "dma", 16)
                k.inc(s.dma_start(out=io["dbg_z"][0:1, :], in_=zrow[:, :]), "dma", 16)
                k.inc(s.dma_start(out=io["dbg_z"][1:2, :], in_=zrec[:, :]), "dma", 16)
                k.inc(s.dma_start(out=io["dbg_ao"][:, :],
                                  in_=ao_fm.rearrange("p a b -> p (a b)")), "dma", 16)
                s.wait_ge(k.sems["dma"], k.cnt["dma"])
            k.sync(dbg_sync)

    stream = ctx.enter_context(nc.sbuf_tensor("stream", [128, NTT, D], F32, side="right"))
    persist["stream"] = stream

    if True:
        def r_sync(s):
            k.wait(s, "pe")
            k.wait(s, "ve")
            k.wait(s, "sc")
            for t in range(NTT):
                k.inc(s.dma_start(out=stream[:, t, :],
                                  in_=io["xk"][t * 128:(t + 1) * 128, :]), "dma", 16)
            k.dma_flush(s)
        k.sync(r_sync)
        dk_res = k.cnt["dk"]

        with contextlib.ExitStack() as sub2:
            wo_st = sub2.enter_context(nc.sbuf_tensor("a_wost", [128, NCF, 512], F32))
            wo_bf = sub2.enter_context(nc.sbuf_tensor("a_wobf", [128, NCF, 512], BF16))
            for jb in range(4):
                pe_before_w = k.cnt["pe"]

                def wo_sync(s, jb=jb, pe_before_w=pe_before_w):
                    if jb == 0:
                        k.wait(s, "pe")
                        k.wait(s, "ve")
                        k.wait(s, "sc")
                    if jb > 0:
                        s.wait_ge(k.sems["pe"], pe_before_w)
                    k.inc(s.dma_start(
                        out=wo_st[:, :, :],
                        in_=_fm(io["woT"])[:, :, jb * 512:(jb + 1) * 512]), "dma", 16)
                    k.dma_flush(s)
                k.sync(wo_sync)
                dk_now = k.cnt["dk"]

                def wo_cast(sc, jb=jb, dk_now=dk_now, pe_before_w=pe_before_w):
                    sc.wait_ge(k.sems["dk"], dk_now)
                    if jb > 0:
                        sc.wait_ge(k.sems["pe"], pe_before_w)
                    k.inc(sc.copy(out=wo_bf.rearrange("p c m -> p (c m)"),
                                  in_=wo_st.rearrange("p c m -> p (c m)")), "sc")
                k.scalar(wo_cast)
                sc_now = k.cnt["sc"]

                for t in range(NTT):
                    si = t % 2

                    def t_wo(tn, t=t, jb=jb, si=si, sc_now=sc_now, heads_ve=heads_ve):
                        tn.wait_ge(k.sems["sc"], sc_now)
                        if t == 0 and jb == 0:
                            tn.wait_ge(k.sems["ve"], heads_ve)
                        k.slot_write_waits(tn, pS_sl[si])
                        ins = None
                        for c in range(NCF):
                            ins = tn.matmul(pS[:, si, :], ao_fm[:, c, t * 128:(t + 1) * 128],
                                            wo_bf[:, c, :],
                                            start=(c == 0), stop=(c == NCF - 1))
                        k.inc(ins, "pe")
                    k.tensor(t_wo)
                    k.slot_wrote(pS_sl[si])

                    def v_wo(v, t=t, jb=jb, si=si, dk_res=dk_res):
                        k.slot_read_waits(v, pS_sl[si])
                        v.wait_ge(k.sems["dk"], dk_res)
                        k.inc(v.tensor_tensor(
                            out=stream[:, t, jb * 512:(jb + 1) * 512],
                            in0=stream[:, t, jb * 512:(jb + 1) * 512],
                            in1=pS[:, si, :], op=ALU.add), "ve")
                    k.vector(v_wo)
                    k.slot_did_read(pS_sl[si], "ve")

            def v_fin(v):
                v.drain()
            k.vector(v_fin)

    if DEBUG in (1, 3):
        def dbg_res_sync(s):
            k.wait(s, "ve")
            for t in range(NTT):
                k.inc(s.dma_start(out=io["dbg_res"][t * 128:(t + 1) * 128, :],
                                  in_=stream[:, t, :]), "dma", 16)
            s.wait_ge(k.sems["dma"], k.cnt["dma"])
        k.sync(dbg_res_sync)

    sub0.close()

    x_fm = ctx.enter_context(nc.sbuf_tensor("x_fm", [128, NCF, SQ], BF16, side="right"))
    persist["x_fm"] = x_fm
    if True:
        with contextlib.ExitStack() as sub3:
            ln_sc = {
                "stats": sub3.enter_context(nc.sbuf_tensor("ln_stats", [128, NTT, 24], F32)),
                "mv8": sub3.enter_context(nc.sbuf_tensor("ln_mv8", [128, NTT, 2], F32)),
                "veps8": sub3.enter_context(nc.sbuf_tensor("ln_veps", [128, NTT], F32)),
                "rstd8": sub3.enter_context(nc.sbuf_tensor("ln_rstd", [128, NTT], F32)),
                "negmr8": sub3.enter_context(nc.sbuf_tensor("ln_negmr", [128, NTT], F32)),
            }
            xtokbf = sub3.enter_context(nc.sbuf_tensor("ln_tokbf", [128, NTT, 2048], BF16))
            emit_ln_and_xfm(k, stream, x_fm, ln_sc, ident_sb, pT, xtokbf)


def emit_standalone_entry(k, nc, ctx, io, persist, xname):
    stream = ctx.enter_context(nc.sbuf_tensor("stream", [128, NTT, D], F32, side="right"))
    x_fm = ctx.enter_context(nc.sbuf_tensor("x_fm", [128, NCF, SQ], BF16, side="right"))
    persist["stream"] = stream
    persist["x_fm"] = x_fm
    ident_sb, onesr_sb, onesc_sb, actc_sb = (persist["ident_sb"], persist["onesr_sb"],
                                             persist["onesc_sb"], persist["actc_sb"])

    def c_sync(s):
        k.inc(s.dma_start(out=ident_sb[:, :], in_=io["ident"][:, :]), "dma", 16)
        k.inc(s.dma_start(out=onesr_sb[:, :], in_=io["onesr"][:, :]), "dma", 16)
        k.inc(s.dma_start(out=onesc_sb[:, :], in_=io["onesc"][:, :]), "dma", 16)
        k.inc(s.dma_start(out=actc_sb[:, :], in_=io["actc"][:, :]), "dma", 16)
        for t in range(NTT):
            k.inc(s.dma_start(out=stream[:, t, :],
                              in_=io[xname][t * 128:(t + 1) * 128, :]), "dma", 16)
        k.dma_flush(s)
    k.sync(c_sync)

    with contextlib.ExitStack() as sub_in:
        stage = sub_in.enter_context(nc.sbuf_tensor("e_stage", [128, 2, 2048], F32))
        tokbf = sub_in.enter_context(nc.sbuf_tensor("e_tokbf", [128, 2, 2048], BF16))
        emit_load_cast_transpose(k, io[xname], NTT, stage, tokbf, ident_sb,
                                 persist["pT"], x_fm)


def build_moe(nc, ctx, k, io, persist):
    stream, x_fm, ident_sb, pT = (persist["stream"], persist["x_fm"],
                                  persist["ident_sb"], persist["pT"])
    NE, NI, NIT = 32, 768, 6
    NSC, NSI = 16, 128

    wg_v = io["Wg"].rearrange("e (c p) i -> e p c i", p=128)
    wu_v = io["Wu"].rearrange("e (c p) i -> e p c i", p=128)
    wd_v = io["Wd"].rearrange("e (ic p) h -> e p ic h", p=128)
    cwg_v = io["cWg"].rearrange("e (c p) i -> e p c i", p=128)
    cwu_v = io["cWu"].rearrange("e (c p) i -> e p c i", p=128)
    cwd_v = io["cWd"].rearrange("e (ic p) h -> e p ic h", p=128)

    with contextlib.ExitStack() as sub:
        sbt = lambda n, s, d: sub.enter_context(nc.sbuf_tensor(n, s, d))
        lg = sbt("m_lg", [128, NTT, 32], F32)
        ew = sbt("m_ew", [128, NTT, 32], F32)
        acc = sbt("m_acc", [128, NTT, 32], F32)
        work = sbt("m_work", [128, NTT, 32], F32)
        eqb = sbt("m_eq", [128, NTT, 32], F32)
        cw = sbt("m_cw", [128, NTT, 32], F32)
        cws = sbt("m_cws", [128, NTT, 16], F32)
        sc8 = sbt("m_sc8", [128, 5, NTT], F32)

        pG = sub.enter_context(nc.psum_tensor("m_pG", [128, 2, 512], F32))
        pU = sub.enter_context(nc.psum_tensor("m_pU", [128, 2, 512], F32))
        pY = sub.enter_context(nc.psum_tensor("m_pY", [128, 2, 512], F32))

        sl_wst = k.new_slots(2)
        sl_gu = k.new_slots(2)
        sl_y = k.new_slots(2)
        sl_a = k.new_slots(1)
        sl_silu = k.new_slots(2)
        sl_w = {"wg": k.new_slots(NIT), "wu": k.new_slots(NIT), "wd": k.new_slots(8)}

        sub_rt = contextlib.ExitStack()
        gw_st = sub_rt.enter_context(nc.sbuf_tensor("m_gwst", [128, NCF, 32], F32))
        gw_bf = sub_rt.enter_context(nc.sbuf_tensor("m_gwbf", [128, NCF, 32], BF16))

        def rt_sync(s):
            k.fence(s)
            k.inc(s.dma_start(out=gw_st[:, :, :],
                              in_=io["gate_wT"].rearrange("(c p) m -> p c m", p=128)),
                  "dma", 16)
            k.dma_flush(s)
        k.sync(rt_sync)
        dk_now = k.cnt["dk"]

        def rt_cast(sc, dk_now=dk_now):
            sc.wait_ge(k.sems["dk"], dk_now)
            k.fence(sc, sems=("pe", "ve"))
            k.inc(sc.copy(out=gw_bf.rearrange("p c m -> p (c m)"),
                          in_=gw_st.rearrange("p c m -> p (c m)")), "sc")
        k.scalar(rt_cast)
        sc_gw = k.cnt["sc"]

        for t in range(NTT):
            si = t % 2

            def rt_mm(tn, t=t, si=si, sc_gw=sc_gw):
                tn.wait_ge(k.sems["sc"], sc_gw)
                k.slot_write_waits(tn, sl_y[si])
                ins = None
                for c in range(NCF):
                    ins = tn.matmul(pY[:, si, 0:32], x_fm[:, c, t * 128:(t + 1) * 128],
                                    gw_bf[:, c, :], start=(c == 0), stop=(c == NCF - 1))
                k.inc(ins, "pe")
            k.tensor(rt_mm)
            k.slot_wrote(pY if False else sl_y[si])

            def rt_ev(v, t=t, si=si):
                k.slot_read_waits(v, sl_y[si])
                k.inc(v.tensor_copy(out=lg[:, t, :], in_=pY[:, si, 0:32]), "ve")
            k.vector(rt_ev)
            k.slot_did_read(sl_y[si], "ve")

        def rt_max(v):
            v.drain()
            last = None
            for t in range(NTT):
                last = v.reduce_max(out=sc8[:, 0, t:t + 1], in_=lg[:, t, :],
                                    axis=AX.X, negate=True)
            k.inc(last, "ve")
        k.vector(rt_max)
        ve_nm = k.cnt["ve"]

        def rt_exp(sc, ve_nm=ve_nm):
            sc.wait_ge(k.sems["ve"], ve_nm)
            last = None
            for t in range(NTT):
                last = sc.activation(out=ew[:, t, :], in_=lg[:, t, :], func=AF.Exp,
                                     bias=sc8[:, 0, t:t + 1], scale=1.0)
            k.inc(last, "sc")
        k.scalar(rt_exp)
        sc_ew = k.cnt["sc"]

        def rt_topk(v, sc_ew=sc_ew):
            for t in range(NTT):
                v.tensor_copy(out=work[:, t, :], in_=lg[:, t, :])
            v.memset(acc.rearrange("p a b -> p (a b)"), 0.0)
            v.wait_ge(k.sems["sc"], sc_ew)
            v.drain()
            for kk in range(4):
                for t in range(NTT):
                    v.reduce_max(out=sc8[:, 1, t:t + 1], in_=work[:, t, :], axis=AX.X)
                v.drain()
                for t in range(NTT):
                    v.tensor_scalar(out=eqb[:, t, :], in0=work[:, t, :],
                                    scalar1=sc8[:, 1, t:t + 1], scalar2=None,
                                    op0=ALU.is_equal)
                v.drain()
                for t in range(NTT):
                    v.scalar_tensor_tensor(out=lg[:, t, :], in0=ew[:, t, :],
                                           scalar=1.0, in1=eqb[:, t, :],
                                           op0=ALU.mult, op1=ALU.mult,
                                           accum_out=sc8[:, 2, t:t + 1])
                v.drain()
                for t in range(NTT):
                    v.scalar_tensor_tensor(out=acc[:, t, :], in0=eqb[:, t, :],
                                           scalar=sc8[:, 2, t:t + 1], in1=acc[:, t, :],
                                           op0=ALU.mult, op1=ALU.add)
                for t in range(NTT):
                    v.scalar_tensor_tensor(out=work[:, t, :], in0=eqb[:, t, :],
                                           scalar=-1e30, in1=work[:, t, :],
                                           op0=ALU.mult, op1=ALU.add)
                v.drain()
            for t in range(NTT):
                v.reduce_sum(out=sc8[:, 3, t:t + 1], in_=acc[:, t, :], axis=AX.X)
            v.drain()
            v.reciprocal(out=sc8[:, 4, :], in_=sc8[:, 3, :])
            v.drain()
            for t in range(NTT):
                v.tensor_scalar(out=cw[:, t, :], in0=acc[:, t, :],
                                scalar1=sc8[:, 4, t:t + 1], scalar2=None, op0=ALU.mult)
            v.drain()
            cwv = cw.rearrange("p t (e two) -> p t e two", two=2)
            for t in range(NTT):
                v.tensor_tensor(out=cws[:, t, :], in0=cwv[:, t, :, 0],
                                in1=cwv[:, t, :, 1], op=ALU.add)
            v.drain()
            last = v.tensor_scalar(out=cws.rearrange("p a b -> p (a b)"),
                                   in0=cws.rearrange("p a b -> p (a b)"),
                                   scalar1=0.05, scalar2=None, op0=ALU.mult)
            k.inc(last, "ve")
            v.drain()
        k.vector(rt_topk)
        ve_cw = k.cnt["ve"]
        sub_rt.close()

        wst = sbt("m_wst", [128, 2, 2048], F32)
        wg_bf = sbt("m_wg", [128, NCF, NI], BF16)
        wu_bf = sbt("m_wu", [128, NCF, NI], BF16)
        wd_bf = sbt("m_wd", [128, NIT, 2048], BF16)
        a_bf = sbt("m_a", [128, 1, NIT, SQ], BF16)
        silu_sb = sbt("m_silu", [128, 2, 512], F32)

        plans = [(e, False) for e in range(NE)] + [(e, True) for e in range(NSC)]
        all_marks = [dict() for _ in plans]
        grain_state = {"gn": 0}

        def make_emitters(idx):
            e, small = plans[idx]
            marks = all_marks[idx]
            grains = []
            if small:
                grains.append(("wg", 0, cwg_v[e, :, :, :], (NCF, NSI)))
                grains.append(("wu", 0, cwu_v[e, :, :, :], (NCF, NSI)))
                grains.append(("wd", 0, cwd_v[e, :, :, :], (1, 2048)))
            else:
                for gi in range(NIT):
                    grains.append(("wg", gi, wg_v[e, :, :, gi * 128:(gi + 1) * 128],
                                   (NCF, 128)))
                    grains.append(("wu", gi, wu_v[e, :, :, gi * 128:(gi + 1) * 128],
                                   (NCF, 128)))
                for gi in range(8):
                    grains.append(("wd", gi, wd_v[e, :, :, gi * 256:(gi + 1) * 256],
                                   (NIT, 256)))

            ems = []
            for en, (wname, gi, view, shp) in enumerate(grains):
                def em(wname=wname, gi=gi, view=view, shp=shp, small=small,
                       first_all=(idx == 0 and en == 0), marks=marks):
                    gn = grain_state["gn"]
                    grain_state["gn"] += 1
                    ssl = gn % 2
                    dwn = "dw%d" % ssl
                    nel = shp[0] * shp[1]

                    def g_sync(s):
                        if first_all:
                            k.fence(s)
                        k.slot_write_waits(s, sl_wst[ssl])
                        k.inc(s.dma_start(
                            out=wst[:, ssl, 0:nel].rearrange("p (a b) -> p a b", a=shp[0]),
                            in_=view), dwn, 16)
                    k.sync(g_sync)
                    k.slot_wrote(sl_wst[ssl], "dma")
                    dw_g = k.cnt[dwn]

                    def g_cast(sc):
                        sc.wait_ge(k.sems[dwn], dw_g)
                        if small and wname == "wd":
                            for sl in sl_w["wd"]:
                                k.slot_write_waits(sc, sl)
                        else:
                            k.slot_write_waits(sc, sl_w[wname][gi])
                        src = wst[:, ssl, 0:nel].rearrange("p (a b) -> p a b", a=shp[0])
                        if wname == "wg":
                            dst = wg_bf[:, :, gi * 128:gi * 128 + shp[1]]
                        elif wname == "wu":
                            dst = wu_bf[:, :, gi * 128:gi * 128 + shp[1]]
                        else:
                            dst = (wd_bf[:, 0:shp[0], gi * 256:gi * 256 + shp[1]]
                                   if not small else wd_bf[:, 0:shp[0], :])
                        k.inc(sc.copy(out=dst, in_=src), "sc")
                    k.scalar(g_cast)
                    marks[(wname, gi)] = k.cnt["sc"]
                    k.slot_did_read(sl_wst[ssl], "sc")
                    k.slot_wrote(sl_w[wname][gi], "sc")
                ems.append(em)
            return ems

        def emit_compute(idx, pre_gu, pre_wd):
            e, small = plans[idx]
            marks = all_marks[idx]
            n_it = 1 if small else NIT
            sa = 0

            for it in range(n_it):
                for th in range(2):
                    gu = (it * 2 + th) % 2

                    def t_gu(tn, it=it, th=th, gu=gu,
                             mg=marks[("wg", 0 if small else it)],
                             mu=marks[("wu", 0 if small else it)]):
                        tn.wait_ge(k.sems["sc"], max(mg, mu))
                        k.slot_write_waits(tn, sl_gu[gu])
                        for c in range(NCF):
                            tn.matmul(pG[:, gu, :], wg_bf[:, c, it * 128:(it + 1) * 128],
                                      x_fm[:, c, th * 512:(th + 1) * 512],
                                      start=(c == 0), stop=(c == NCF - 1))
                        ins = None
                        for c in range(NCF):
                            ins = tn.matmul(pU[:, gu, :], wu_bf[:, c, it * 128:(it + 1) * 128],
                                            x_fm[:, c, th * 512:(th + 1) * 512],
                                            start=(c == 0), stop=(c == NCF - 1))
                        k.inc(ins, "pe")
                    k.tensor(t_gu)
                    k.slot_wrote(sl_gu[gu])
                    k.slot_did_read(sl_w["wg"][0 if small else it], "pe")
                    k.slot_did_read(sl_w["wu"][0 if small else it], "pe")

                    def s_silu(sc, gu=gu):
                        k.slot_read_waits(sc, sl_gu[gu])
                        k.slot_write_waits(sc, sl_silu[gu])
                        k.inc(sc.activation(out=silu_sb[:, gu, :], in_=pG[:, gu, :],
                                            func=AF.Silu), "sc")
                    k.scalar(s_silu)
                    k.slot_did_read(sl_gu[gu], "sc")
                    k.slot_wrote(sl_silu[gu], "sc")

                    def v_a(v, it=it, th=th, gu=gu, sa=sa, first=(it == 0 and th == 0)):
                        k.slot_read_waits(v, sl_silu[gu])
                        if first:
                            v.wait_ge(k.sems["ve"], ve_cw)
                            k.slot_write_waits(v, sl_a[sa])
                        k.inc(v.tensor_tensor(out=a_bf[:, sa, it, th * 512:(th + 1) * 512],
                                              in0=silu_sb[:, gu, :], in1=pU[:, gu, :],
                                              op=ALU.mult), "ve")
                    k.vector(v_a)
                    k.slot_did_read(sl_silu[gu], "ve")
                    k.slot_did_read(sl_gu[gu], "ve")
                for _ in range(2):
                    if pre_gu:
                        pre_gu.pop(0)()
            while pre_gu:
                pre_gu.pop(0)()
            k.slot_wrote(sl_a[sa], "ve")
            a_done = k.cnt["ve"]

            for t in range(NTT):
                for hb in range(4):
                    y = (t * 4 + hb) % 2

                    def t_dn(tn, t=t, hb=hb, y=y, sa=sa, a_done=a_done,
                             md=marks[("wd", 0 if small else 2 * hb + 1)], n_it=n_it):
                        tn.wait_ge(k.sems["sc"], md)
                        tn.wait_ge(k.sems["ve"], a_done)
                        k.slot_write_waits(tn, sl_y[y])
                        ins = None
                        for ic in range(n_it):
                            ins = tn.matmul(pY[:, y, :], a_bf[:, sa, ic, t * 128:(t + 1) * 128],
                                            wd_bf[:, ic, hb * 512:(hb + 1) * 512],
                                            start=(ic == 0), stop=(ic == n_it - 1))
                        k.inc(ins, "pe")
                    k.tensor(t_dn)
                    k.slot_wrote(sl_y[y])
                    if small:
                        k.slot_did_read(sl_w["wd"][0], "pe")
                    else:
                        k.slot_did_read(sl_w["wd"][2 * hb], "pe")
                        k.slot_did_read(sl_w["wd"][2 * hb + 1], "pe")

                    def v_dn(v, t=t, hb=hb, y=y, e=e, small=small):
                        k.slot_read_waits(v, sl_y[y])
                        if e == 0 and not small and t == 0 and hb == 0:
                            k.wait(v, "dk")
                        col = cws[:, t, e:e + 1] if small else cw[:, t, e:e + 1]
                        k.inc(v.scalar_tensor_tensor(
                            out=stream[:, t, hb * 512:(hb + 1) * 512],
                            in0=pY[:, y, :], scalar=col,
                            in1=stream[:, t, hb * 512:(hb + 1) * 512],
                            op0=ALU.mult, op1=ALU.add), "ve")
                    k.vector(v_dn)
                    k.slot_did_read(sl_y[y], "ve")

            def v_edrain(v):
                v.drain()
            k.vector(v_edrain)
            k.slot_did_read(sl_a[sa], "pe")
            while pre_wd:
                pre_wd.pop(0)()

        for em in make_emitters(0):
            em()
        for idx in range(len(plans)):
            if idx + 1 < len(plans):
                nxt = make_emitters(idx + 1)
                n_gu = 2 * (1 if plans[idx + 1][1] else NIT)
                pre_gu, pre_wd = nxt[:n_gu], nxt[n_gu:]
            else:
                pre_gu, pre_wd = [], []
            emit_compute(idx, pre_gu, pre_wd)

    with contextlib.ExitStack() as sub3:
        ln_sc = {
            "stats": sub3.enter_context(nc.sbuf_tensor("ln2_stats", [128, NTT, 24], F32)),
            "mv8": sub3.enter_context(nc.sbuf_tensor("ln2_mv8", [128, NTT, 2], F32)),
            "veps8": sub3.enter_context(nc.sbuf_tensor("ln2_veps", [128, NTT], F32)),
            "rstd8": sub3.enter_context(nc.sbuf_tensor("ln2_rstd", [128, NTT], F32)),
            "negmr8": sub3.enter_context(nc.sbuf_tensor("ln2_negmr", [128, NTT], F32)),
        }
        xtokbf = sub3.enter_context(nc.sbuf_tensor("ln2_tokbf", [128, NTT, 2048], BF16))
        emit_ln_and_xfm(k, stream, x_fm, ln_sc, ident_sb, pT, xtokbf)


def build_l1attn(nc, ctx, k, io, persist):
    stream, x_fm, ident_sb, pT = (persist["stream"], persist["x_fm"],
                                  persist["ident_sb"], persist["pT"])
    IB = 4

    with contextlib.ExitStack() as sub:
        sbt = lambda n, s, d: sub.enter_context(nc.sbuf_tensor(n, s, d))
        x2_all = sbt("s_x2all", [128, NKT, 2048], BF16)
        gcol = sbt("s_gcol", [128, 1], F32)

        def v_own(v):
            k.wait(v, "pe")
            last = None
            for t in range(NTT):
                last = v.tensor_copy(out=x2_all[:, t, :], in_=stream[:, t, :])
            k.inc(last, "ve")
            v.drain()
        k.vector(v_own)

        part_cast = [0] * NTT
        if "x2o" in io:
            with contextlib.ExitStack() as sub_in:
                stage = sub_in.enter_context(nc.sbuf_tensor("s_stage", [128, 2, 2048], F32))
                tokbf = sub_in.enter_context(nc.sbuf_tensor("s_tokbf", [128, 2, 2048], BF16))
                for t in range(NTT):
                    def s_ld(s, t=t):
                        if t < 2:
                            k.fence(s)
                        else:
                            s.wait_ge(k.sems["ve"], part_cast[t - 2])
                        k.inc(s.dma_start(out=stage[:, t % 2, :],
                                          in_=io["x2o"][t * 128:(t + 1) * 128, :]),
                              "dma", 16)
                        k.dma_flush(s)
                    k.sync(s_ld)
                    dk_now = k.cnt["dk"]

                    def v_pc(v, t=t, dk_now=dk_now):
                        v.wait_ge(k.sems["dk"], dk_now)
                        part_cast[t] = k.inc(v.tensor_copy(
                            out=x2_all[:, NTT + t, :], in_=stage[:, t % 2, :]), "ve")
                    k.vector(v_pc)
        else:
            cc_in = io["cc_in"]
            cc_out = io["cc_out"]

            def s_ccout(s):
                k.wait(s, "ve")
                for t in range(NTT):
                    k.inc(s.dma_start(out=cc_in[t * 128:(t + 1) * 128, :],
                                      in_=x2_all[:, t, :]), "dma", 16)
                k.dma_flush(s)
            k.sync(s_ccout)
            dk_cc = k.cnt["dk"]

            def g_cc(g, dk_cc=dk_cc):
                g.wait_ge(k.sems["dk"], dk_cc)
                k.inc(g.collective_compute(
                    "AllGather", ALU.bypass,
                    replica_groups=[[0, 1], [2, 3], [4, 5], [6, 7]],
                    ins=[cc_in.tensor.ap().opt()],
                    outs=[cc_out.tensor.ap().opt()]), "cc")
            k.gpsimd(g_cc)
            cc_done = k.cnt["cc"]

            def s_ccin(s, cc_done=cc_done):
                s.wait_ge(k.sems["cc"], cc_done)
                last = None
                for t in range(2 * NTT):
                    last = k.inc(s.dma_start(out=x2_all[:, t, :],
                                             in_=cc_out[t * 128:(t + 1) * 128, :]),
                                 "dma", 16)
                k.dma_flush(s)
            k.sync(s_ccin)

        def s_gamma(s):
            k.inc(s.dma_start(out=gcol[:, :], in_=io["gammac"][:, :]), "dma", 16)
            k.dma_flush(s)
        k.sync(s_gamma)
        x2_ready_ve = k.cnt["ve"]
        x2_ready_dma = k.cnt["dk"]

        def v_double(v):
            v.wait_ge(k.sems["ve"], x2_ready_ve)
            last = None
            for t in range(NTT):
                last = v.tensor_scalar(out=stream[:, t, :], in0=stream[:, t, :],
                                       scalar1=2.0, scalar2=None, op0=ALU.mult)
            k.inc(last, "ve")
            v.drain()
        k.vector(v_double)

        with contextlib.ExitStack() as sub2:
            erow = sub2.enter_context(nc.sbuf_tensor("s_erow", [128, 1, 2048], F32))
            attn_blk = sub2.enter_context(nc.sbuf_tensor("s_attn", [128, 4, 2048], BF16))
            attnT = sub2.enter_context(nc.sbuf_tensor("s_attnT", [128, NCF, 512], BF16))
            zsc = sub2.enter_context(nc.sbuf_tensor("s_zsc", [128, 3], F32))
            pE = sub2.enter_context(nc.psum_tensor("s_pE", [128, 2, 512], F32))
            pO2 = sub2.enter_context(nc.psum_tensor("s_pO2", [128, 2, 512], F32))
            sl_e = k.new_slots(2)
            sl_o = k.new_slots(2)
            sl_er = k.new_slots(1)
            sl_attnT = k.new_slots(1)[0]
            sl_ab = k.new_slots(1)[0]

            for ib in range(IB):
                for itl in range(4):
                    i0 = ib * 512 + itl * 128
                    esl = 0
                    for jb in range(4):
                        ps = (itl * 4 + jb) % 2

                        def t_e(tn, i0=i0, jb=jb, ps=ps, dma0=x2_ready_dma,
                                ve0=x2_ready_ve, first=(ib == 0 and itl == 0 and jb == 0)):
                            if first:
                                k.fence(tn)
                            tn.wait_ge(k.sems["ve"], ve0)
                            tn.wait_ge(k.sems["dk"], dma0)
                            k.slot_write_waits(tn, sl_e[ps])
                            ins = None
                            for sc_ in range(NKT):
                                ins = tn.matmul(pE[:, ps, :],
                                                x2_all[:, sc_, i0:i0 + 128],
                                                x2_all[:, sc_, jb * 512:(jb + 1) * 512],
                                                start=(sc_ == 0), stop=(sc_ == NKT - 1))
                            k.inc(ins, "pe")
                        k.tensor(t_e)
                        k.slot_wrote(sl_e[ps])

                        def v_ee(v, itl=itl, jb=jb, ps=ps, esl=esl):
                            k.slot_read_waits(v, sl_e[ps])
                            if jb == 0:
                                k.slot_write_waits(v, sl_er[esl])
                            k.inc(v.tensor_copy(out=erow[:, esl, jb * 512:(jb + 1) * 512],
                                                in_=pE[:, ps, :]), "ve")
                        k.vector(v_ee)
                        k.slot_did_read(sl_e[ps], "ve")
                    k.slot_wrote(sl_er[esl], "ve")

                    def v_mx(v, esl=esl):
                        k.slot_read_waits(v, sl_er[esl])
                        k.wait(v, "sc")
                        v.drain()
                        k.inc(v.reduce_max(out=zsc[:, 0:1], in_=erow[:, esl, :],
                                           axis=AX.X, negate=True), "ve")
                    k.vector(v_mx)
                    ve_mx = k.cnt["ve"]

                    def s_ex(sc, esl=esl, ve_mx=ve_mx):
                        sc.wait_ge(k.sems["ve"], ve_mx)
                        k.inc(sc.activation(out=erow[:, esl, :], in_=erow[:, esl, :],
                                            func=AF.Exp, bias=zsc[:, 0:1], scale=1.0,
                                            accum_out=zsc[:, 1:2]), "sc")
                    k.scalar(s_ex)
                    sc_ex = k.cnt["sc"]

                    def v_at(v, itl=itl, esl=esl, sc_ex=sc_ex):
                        v.wait_ge(k.sems["sc"], sc_ex)
                        v.reciprocal(out=zsc[:, 2:3], in_=zsc[:, 1:2])
                        v.drain()
                        if itl == 0:
                            k.slot_write_waits(v, sl_ab)
                        ins = v.tensor_scalar(out=attn_blk[:, itl, :],
                                              in0=erow[:, esl, :],
                                              scalar1=zsc[:, 2:3], scalar2=None,
                                              op0=ALU.mult)
                        k.inc(ins, "ve")
                    k.vector(v_at)
                    k.slot_did_read(sl_er[esl], "ve")
                    k.slot_did_read(sl_er[esl], "sc")
                k.slot_wrote(sl_ab, "ve")
                attn_ve = k.cnt["ve"]

                tpg = [0] * NCF
                evg = [0] * NCF
                for jt in range(NKT):
                    def t_tp(tn, jt=jt, attn_ve=attn_ve):
                        if jt == 0:
                            tn.wait_ge(k.sems["ve"], attn_ve)
                        if jt >= 2:
                            tn.wait_ge(k.sems["ve"], evg[jt - 2])
                        last = None
                        for itl in range(4):
                            last = tn.transpose(pT[:, jt % 2, itl * 128:(itl + 1) * 128],
                                                attn_blk[:, itl, jt * 128:(jt + 1) * 128],
                                                ident_sb[:, :])
                        tpg[jt] = k.inc(last, "pe")
                    k.tensor(t_tp)

                    def v_tp(v, jt=jt):
                        v.wait_ge(k.sems["pe"], tpg[jt])
                        if jt == 0:
                            k.slot_write_waits(v, sl_attnT)
                        evg[jt] = k.inc(v.tensor_copy(
                            out=attnT[:, jt, :],
                            in_=pT[:, jt % 2, 0:512].rearrange("p (i m) -> p i m", i=4)
                                .rearrange("p i m -> p (i m)")), "ve")
                    k.vector(v_tp)
                k.slot_wrote(sl_attnT, "ve")
                k.slot_did_read(sl_ab, "pe")
                tp_ve = k.cnt["ve"]

                for t in range(NTT):
                    osl = t % 2

                    def t_o(tn, t=t, ib=ib, osl=osl, tp_ve=tp_ve):
                        if t == 0:
                            tn.wait_ge(k.sems["ve"], tp_ve)
                        k.slot_write_waits(tn, sl_o[osl])
                        ins = None
                        for jc in range(NCF):
                            ins = tn.matmul(pO2[:, osl, :], x_fm[:, jc, t * 128:(t + 1) * 128],
                                            attnT[:, jc, :],
                                            start=(jc == 0), stop=(jc == NCF - 1))
                        k.inc(ins, "pe")
                    k.tensor(t_o)
                    k.slot_wrote(sl_o[osl])

                    def v_o(v, t=t, ib=ib, osl=osl, dma0=x2_ready_dma):
                        k.slot_read_waits(v, sl_o[osl])
                        if t == 0 and ib == 0:
                            v.wait_ge(k.sems["dk"], dma0)
                        k.inc(v.scalar_tensor_tensor(
                            out=stream[:, t, ib * 512:(ib + 1) * 512],
                            in0=pO2[:, osl, :], scalar=gcol[:, :],
                            in1=stream[:, t, ib * 512:(ib + 1) * 512],
                            op0=ALU.mult, op1=ALU.add), "ve")
                    k.vector(v_o)
                    k.slot_did_read(sl_o[osl], "ve")
                k.slot_did_read(sl_attnT, "pe")
                def v_bdrain(v):
                    v.drain()
                k.vector(v_bdrain)

    with contextlib.ExitStack() as sub3:
        ln_sc = {
            "stats": sub3.enter_context(nc.sbuf_tensor("ln3_stats", [128, NTT, 24], F32)),
            "mv8": sub3.enter_context(nc.sbuf_tensor("ln3_mv8", [128, NTT, 2], F32)),
            "veps8": sub3.enter_context(nc.sbuf_tensor("ln3_veps", [128, NTT], F32)),
            "rstd8": sub3.enter_context(nc.sbuf_tensor("ln3_rstd", [128, NTT], F32)),
            "negmr8": sub3.enter_context(nc.sbuf_tensor("ln3_negmr", [128, NTT], F32)),
        }
        xtokbf = sub3.enter_context(nc.sbuf_tensor("ln3_tokbf", [128, NTT, 2048], BF16))
        emit_ln_and_xfm(k, stream, x_fm, ln_sc, ident_sb, pT, xtokbf)


def build_l1ffn(nc, ctx, k, io, persist):
    stream, x_fm, ident_sb, onesr_sb, pT = (persist["stream"], persist["x_fm"],
                                            persist["ident_sb"], persist["onesr_sb"],
                                            persist["pT"])
    NAR = 1024
    b2_v = io["B2"].rearrange("(ac p) j -> p ac j", p=128)
    wt_v = io["WT"].rearrange("(c p) j -> p c j", p=128)

    with contextlib.ExitStack() as sub:
        sbt = lambda n, s, d: sub.enter_context(nc.sbuf_tensor(n, s, d))
        c_fm = sbt("f_cfm", [128, 8, SQ], BF16)
        pV = sub.enter_context(nc.psum_tensor("f_pV", [128, 2, 512], F32))
        sl_v = k.new_slots(2)
        subx = contextlib.ExitStack()
        vecs_sb = subx.enter_context(nc.sbuf_tensor("f_vecs", [128, NTT, NAR], F32))

        with contextlib.ExitStack() as suba:
            a2t_st = suba.enter_context(nc.sbuf_tensor("f_a2st", [128, NCF, 256], F32))
            a2t_bf = suba.enter_context(nc.sbuf_tensor("f_a2bf", [128, NCF, NAR], BF16))
            sl_a2st = k.new_slots(1)
            cast_marks = []
            for gi in range(4):
                def s_a2(s, gi=gi):
                    if gi == 0:
                        k.fence(s)
                    k.slot_write_waits(s, sl_a2st[0])
                    k.inc(s.dma_start(
                        out=a2t_st[:, :, :],
                        in_=_fm(io["A2T"])[:, :, gi * 256:(gi + 1) * 256]), "dma", 16)
                    k.dma_flush(s)
                k.sync(s_a2)
                k.slot_wrote(sl_a2st[0], "dma")
                dk_g = k.cnt["dk"]

                def sc_a2(sc, gi=gi, dk_g=dk_g):
                    sc.wait_ge(k.sems["dk"], dk_g)
                    if gi == 0:
                        k.fence(sc, sems=("pe", "ve"))
                    k.inc(sc.copy(out=a2t_bf[:, :, gi * 256:(gi + 1) * 256],
                                  in_=a2t_st[:, :, :]), "sc")
                k.scalar(sc_a2)
                cast_marks.append(k.cnt["sc"])
                k.slot_did_read(sl_a2st[0], "sc")

            for t in range(NTT):
                for vb in range(2):
                    ps = (t * 2 + vb) % 2

                    def t_v(tn, t=t, vb=vb, ps=ps, m=cast_marks[-1]):
                        tn.wait_ge(k.sems["sc"], m)
                        k.slot_write_waits(tn, sl_v[ps])
                        ins = None
                        for c in range(NCF):
                            ins = tn.matmul(pV[:, ps, :], x_fm[:, c, t * 128:(t + 1) * 128],
                                            a2t_bf[:, c, vb * 512:(vb + 1) * 512],
                                            start=(c == 0), stop=(c == NCF - 1))
                        k.inc(ins, "pe")
                    k.tensor(t_v)
                    k.slot_wrote(sl_v[ps])

                    def v_ve(v, t=t, vb=vb, ps=ps):
                        k.slot_read_waits(v, sl_v[ps])
                        k.inc(v.tensor_copy(out=vecs_sb[:, t, vb * 512:(vb + 1) * 512],
                                            in_=pV[:, ps, :]), "ve")
                    k.vector(v_ve)
                    k.slot_did_read(sl_v[ps], "ve")
        vecs_ve = k.cnt["ve"]

        subr = contextlib.ExitStack()
        sq = subr.enter_context(nc.sbuf_tensor("f_sq", [128, NTT, NAR], F32))
        arr = subr.enter_context(nc.sbuf_tensor("f_arr", [128, NTT, 64], F32))
        routes = subr.enter_context(nc.sbuf_tensor("f_routes", [128, NTT, 64], F32))
        eqb = subr.enter_context(nc.sbuf_tensor("f_eq", [128, NTT, 64], F32))
        v16 = subr.enter_context(nc.sbuf_tensor("f_v16", [128, NTT, 16], F32))
        wk8 = subr.enter_context(nc.sbuf_tensor("f_wk8", [128, NTT], F32))
        c_bf = subr.enter_context(nc.sbuf_tensor("f_cbf", [128, NTT, NAR], BF16))

        def sc_sq(sc, vecs_ve=vecs_ve):
            sc.wait_ge(k.sems["ve"], vecs_ve)
            k.fence(sc, sems=("pe", "ve"))
            last = None
            for t in range(NTT):
                last = sc.activation(out=sq[:, t, :], in_=vecs_sb[:, t, :],
                                     func=AF.Square)
            k.inc(last, "sc")
        k.scalar(sc_sq)
        sc_sq_c = k.cnt["sc"]

        def v_route(v, sc_sq_c=sc_sq_c):
            v.wait_ge(k.sems["sc"], sc_sq_c)
            sqv = sq.rearrange("p t (a r) -> p t a r", r=16)
            for width in (8, 4, 2, 1):
                for t in range(NTT):
                    v.tensor_tensor(out=sqv[:, t, :, 0:width],
                                    in0=sqv[:, t, :, 0:width],
                                    in1=sqv[:, t, :, width:2 * width], op=ALU.add)
                v.drain()
            for t in range(NTT):
                v.tensor_copy(out=arr[:, t, :], in_=sqv[:, t, :, 0])
            v.memset(routes.rearrange("p a b -> p (a b)"), 0.0)
            v.drain()
            for kk in range(4):
                for t in range(NTT):
                    v.reduce_max(out=wk8[:, t:t + 1], in_=arr[:, t, :], axis=AX.X)
                v.drain()
                for t in range(NTT):
                    v.tensor_scalar(out=eqb[:, t, :], in0=arr[:, t, :],
                                    scalar1=wk8[:, t:t + 1], scalar2=None,
                                    op0=ALU.is_equal)
                v.drain()
                for t in range(NTT):
                    v.tensor_tensor(out=routes[:, t, :], in0=routes[:, t, :],
                                    in1=eqb[:, t, :], op=ALU.add)
                for t in range(NTT):
                    v.scalar_tensor_tensor(out=arr[:, t, :], in0=eqb[:, t, :],
                                           scalar=-1e30, in1=arr[:, t, :],
                                           op0=ALU.mult, op1=ALU.add)
                v.drain()
            last = v.tensor_scalar(out=routes.rearrange("p a b -> p (a b)"),
                                   in0=routes.rearrange("p a b -> p (a b)"),
                                   scalar1=0.25, scalar2=None, op0=ALU.mult)
            v.drain()
            for t in range(NTT):
                v.tensor_tensor(out=sq[:, t, :], in0=vecs_sb[:, t, :],
                                in1=routes[:, t, :].unsqueeze(2)
                                    .broadcast_to([128, 64, 16]), op=ALU.mult)
            v.drain()
            for width in (512, 256, 128, 64, 32, 16):
                for t in range(NTT):
                    v.tensor_tensor(out=sq[:, t, 0:width], in0=sq[:, t, 0:width],
                                    in1=sq[:, t, width:2 * width], op=ALU.add)
                v.drain()
            for t in range(NTT):
                v.tensor_copy(out=v16[:, t, :], in_=sq[:, t, 0:16])
            v.drain()
            last = None
            for t in range(NTT):
                last = v.tensor_tensor(
                    out=c_bf[:, t, :],
                    in0=routes[:, t, :].unsqueeze(2).broadcast_to([128, 64, 16]),
                    in1=v16[:, t, :].unsqueeze(1).broadcast_to([128, 64, 16]),
                    op=ALU.mult)
            k.inc(last, "ve")
            v.drain()
        k.vector(v_route)
        c_ve = k.cnt["ve"]

        tpg = [0] * (NTT * 2)
        evg = [0] * (NTT * 2)
        for t in range(NTT):
            for g in range(2):
                gi = t * 2 + g

                def t_c(tn, t=t, g=g, gi=gi, c_ve=c_ve):
                    if gi == 0:
                        tn.wait_ge(k.sems["ve"], c_ve)
                    if gi >= 2:
                        tn.wait_ge(k.sems["ve"], evg[gi - 2])
                    last = None
                    for i in range(4):
                        ac = g * 4 + i
                        last = tn.transpose(pT[:, gi % 2, i * 128:(i + 1) * 128],
                                            c_bf[:, t, ac * 128:(ac + 1) * 128],
                                            ident_sb[:, :])
                    tpg[gi] = k.inc(last, "pe")
                k.tensor(t_c)

                def v_c(v, t=t, g=g, gi=gi):
                    v.wait_ge(k.sems["pe"], tpg[gi])
                    evg[gi] = k.inc(v.tensor_copy(
                        out=c_fm[:, g * 4:(g + 1) * 4, t * 128:(t + 1) * 128],
                        in_=pT[:, gi % 2, 0:512].rearrange("p (i m) -> p i m", i=4)), "ve")
                k.vector(v_c)
        cfm_ve = k.cnt["ve"]
        subr.close()
        subx.close()

        with contextlib.ExitStack() as subo:
            w_st = subo.enter_context(nc.sbuf_tensor("f_wst", [128, NCF, 512], F32))
            w_bf = subo.enter_context(nc.sbuf_tensor("f_wbf", [128, NCF, 512], BF16))
            b2_st = subo.enter_context(nc.sbuf_tensor("f_b2st", [128, 8, 512], F32))
            b2_bf = subo.enter_context(nc.sbuf_tensor("f_b2bf", [128, 8, 512], BF16))
            biasrow = subo.enter_context(nc.sbuf_tensor("f_bias", [1, 512], F32))
            biasbf = subo.enter_context(nc.sbuf_tensor("f_biasbf", [1, 512], BF16))
            sl_wb = k.new_slots(1)
            for jb in range(4):
                pe_prev = k.cnt["pe"]

                def s_w(s, jb=jb):
                    if jb == 0:
                        k.fence(s)
                    k.slot_write_waits(s, sl_wb[0])
                    k.wait(s, "sc")
                    k.inc(s.dma_start(out=w_st[:, :, :],
                                      in_=wt_v[:, :, jb * 512:(jb + 1) * 512]), "dma", 16)
                    k.inc(s.dma_start(out=b2_st[:, :, :],
                                      in_=b2_v[:, :, jb * 512:(jb + 1) * 512]), "dma", 16)
                    k.inc(s.dma_start(out=biasrow[:, :],
                                      in_=io["bias"][:, jb * 512:(jb + 1) * 512]), "dma", 16)
                    k.dma_flush(s)
                k.sync(s_w)
                dk_w = k.cnt["dk"]

                def sc_w(sc, jb=jb, dk_w=dk_w, pe_prev=pe_prev):
                    sc.wait_ge(k.sems["dk"], dk_w)
                    if jb > 0:
                        sc.wait_ge(k.sems["pe"], pe_prev)
                    sc.copy(out=w_bf.rearrange("p c m -> p (c m)"),
                            in_=w_st.rearrange("p c m -> p (c m)"))
                    sc.copy(out=b2_bf.rearrange("p c m -> p (c m)"),
                            in_=b2_st.rearrange("p c m -> p (c m)"))
                    k.inc(sc.copy(out=biasbf[:, :], in_=biasrow[:, :]), "sc")
                k.scalar(sc_w)
                sc_wm = k.cnt["sc"]

                for t in range(NTT):
                    ps = t % 2

                    def t_out(tn, t=t, jb=jb, ps=ps, sc_wm=sc_wm, cfm_ve=cfm_ve):
                        tn.wait_ge(k.sems["sc"], sc_wm)
                        if t == 0 and jb == 0:
                            tn.wait_ge(k.sems["ve"], cfm_ve)
                        k.slot_write_waits(tn, sl_v[ps])
                        for c in range(NCF):
                            tn.matmul(pV[:, ps, :], x_fm[:, c, t * 128:(t + 1) * 128],
                                      w_bf[:, c, :], start=(c == 0), stop=False)
                        for ac in range(8):
                            tn.matmul(pV[:, ps, :], c_fm[:, ac, t * 128:(t + 1) * 128],
                                      b2_bf[:, ac, :], start=False, stop=False)
                        ins = tn.matmul(pV[:, ps, :], onesr_sb[:, :],
                                        biasbf[:, :],
                                        start=False, stop=True)
                        k.inc(ins, "pe")
                    k.tensor(t_out)
                    k.slot_wrote(sl_v[ps])

                    def v_out(v, t=t, jb=jb, ps=ps):
                        k.slot_read_waits(v, sl_v[ps])
                        k.inc(v.tensor_tensor(
                            out=stream[:, t, jb * 512:(jb + 1) * 512],
                            in0=stream[:, t, jb * 512:(jb + 1) * 512],
                            in1=pV[:, ps, :], op=ALU.add), "ve")
                    k.vector(v_out)
                    k.slot_did_read(sl_v[ps], "ve")
                k.slot_did_read(sl_wb[0], "pe")
            def v_fdrain(v):
                v.drain()
            k.vector(v_fdrain)

    with contextlib.ExitStack() as sub3:
        ln_sc = {
            "stats": sub3.enter_context(nc.sbuf_tensor("ln4_stats", [128, NTT, 24], F32)),
            "mv8": sub3.enter_context(nc.sbuf_tensor("ln4_mv8", [128, NTT, 2], F32)),
            "veps8": sub3.enter_context(nc.sbuf_tensor("ln4_veps", [128, NTT], F32)),
            "rstd8": sub3.enter_context(nc.sbuf_tensor("ln4_rstd", [128, NTT], F32)),
            "negmr8": sub3.enter_context(nc.sbuf_tensor("ln4_negmr", [128, NTT], F32)),
        }
        emit_ln_apply_only(k, stream, ln_sc)


def emit_ln_apply_only(k, stream, ln_sc):
    stats, mv8, veps8, rstd8, negmr8 = (ln_sc["stats"], ln_sc["mv8"], ln_sc["veps8"],
                                        ln_sc["rstd8"], ln_sc["negmr8"])

    def v1(v):
        k.wait(v, "pe")
        for t in range(NTT):
            for g in range(4):
                v.bn_stats(out=stats[:, t, g * 6:(g + 1) * 6],
                           in_=stream[:, t, g * 512:(g + 1) * 512])
        v.drain()
        for t in range(NTT):
            v.bn_aggr(out=mv8[:, t, :], in_=stats[:, t, :])
        v.drain()
        last = None
        for t in range(NTT):
            last = v.tensor_scalar(out=veps8[:, t:t + 1], in0=mv8[:, t, 1:2],
                                   scalar1=LN_EPS, scalar2=None, op0=ALU.add)
        k.inc(last, "ve")
    k.vector(v1)
    ve_ready = k.cnt["ve"]

    def s1(sc):
        sc.wait_ge(k.sems["ve"], ve_ready)
        k.inc(sc.activation(out=rstd8[:, :], in_=veps8[:, :], func=AF.Sqrt), "sc")
    k.scalar(s1)
    sc_ready = k.cnt["sc"]

    def v2(v):
        v.wait_ge(k.sems["sc"], sc_ready)
        v.reciprocal(out=rstd8[:, :], in_=rstd8[:, :])
        v.drain()
        v.scalar_tensor_tensor(out=negmr8[:, :],
                               in0=mv8.rearrange("p t s -> p (t s)")[:, 0::2],
                               scalar=-1.0, in1=rstd8[:, :],
                               op0=ALU.mult, op1=ALU.mult)
        v.drain()
        last = None
        for t in range(NTT):
            last = v.tensor_scalar(out=stream[:, t, :], in0=stream[:, t, :],
                                   scalar1=rstd8[:, t:t + 1], scalar2=negmr8[:, t:t + 1],
                                   op0=ALU.mult, op1=ALU.add)
        k.inc(last, "ve")
        v.drain()
    k.vector(v2)


def emit_store_stream(k, stream, out_dram):
    def s_out(s):
        k.wait(s, "ve")
        for t in range(NTT):
            k.inc(s.dma_start(out=out_dram[t * 128:(t + 1) * 128, :],
                              in_=stream[:, t, :]), "dma", 16)
        k.wait(s, "dma")
    k.sync(s_out)


_PROG_CACHE = {}


def build_program(blocks):
    nc = bass.Bass("TRN2", target_bir_lowering=False)
    io = {}
    if "l0attn" in blocks:
        io["xk"] = nc.dram_tensor("xk", [SK, D], F32, kind="ExternalInput").ap()
        io["wqT"] = nc.dram_tensor("wqT", [D, D], F32, kind="ExternalInput").ap()
        io["wkT"] = nc.dram_tensor("wkT", [D, D], F32, kind="ExternalInput").ap()
        io["wvT"] = nc.dram_tensor("wvT", [D, D], F32, kind="ExternalInput").ap()
        io["woT"] = nc.dram_tensor("woT", [D, D], F32, kind="ExternalInput").ap()
        io["masks"] = nc.dram_tensor("masks", [32 * 128, 512], BF16, kind="ExternalInput").ap()
    if "moe" in blocks:
        if "l0attn" not in blocks:
            io["x1"] = nc.dram_tensor("x1", [SQ, D], F32, kind="ExternalInput").ap()
        io["gate_wT"] = nc.dram_tensor("gate_wT", [D, 32], F32, kind="ExternalInput").ap()
        io["Wg"] = nc.dram_tensor("Wg", [32, D, 768], F32, kind="ExternalInput").ap()
        io["Wu"] = nc.dram_tensor("Wu", [32, D, 768], F32, kind="ExternalInput").ap()
        io["Wd"] = nc.dram_tensor("Wd", [32, 768, D], F32, kind="ExternalInput").ap()
        io["cWg"] = nc.dram_tensor("cWg", [16, D, 128], F32, kind="ExternalInput").ap()
        io["cWu"] = nc.dram_tensor("cWu", [16, D, 128], F32, kind="ExternalInput").ap()
        io["cWd"] = nc.dram_tensor("cWd", [16, 128, D], F32, kind="ExternalInput").ap()
    if "l1attn" in blocks:
        if blocks[0] == "l1attn":
            io["x2"] = nc.dram_tensor("x2", [SQ, D], F32, kind="ExternalInput").ap()
            io["x2o"] = nc.dram_tensor("x2o", [SQ, D], F32, kind="ExternalInput").ap()
        else:
            io["cc_in"] = nc.dram_tensor("cc_in", [SQ, D], BF16).ap()
            io["cc_out"] = nc.dram_tensor("cc_out", [SK, D], BF16).ap()
        io["gammac"] = nc.dram_tensor("gammac", [128, 1], F32, kind="ExternalInput").ap()
    if "l1ffn" in blocks:
        if blocks[0] == "l1ffn":
            io["x3"] = nc.dram_tensor("x3", [SQ, D], F32, kind="ExternalInput").ap()
        io["A2T"] = nc.dram_tensor("A2T", [D, 1024], F32, kind="ExternalInput").ap()
        io["B2"] = nc.dram_tensor("B2", [1024, D], F32, kind="ExternalInput").ap()
        io["WT"] = nc.dram_tensor("WT", [D, D], F32, kind="ExternalInput").ap()
        io["bias"] = nc.dram_tensor("bias", [1, D], F32, kind="ExternalInput").ap()
    io["ident"] = nc.dram_tensor("ident", [128, 128], BF16, kind="ExternalInput").ap()
    io["actc"] = nc.dram_tensor("actc", [128, 2], F32, kind="ExternalInput").ap()
    io["onesr"] = nc.dram_tensor("onesr", [1, 128], BF16, kind="ExternalInput").ap()
    io["onesc"] = nc.dram_tensor("onesc", [128, 1], BF16, kind="ExternalInput").ap()
    out = nc.dram_tensor("out", [SQ, D], F32, kind="ExternalOutput").ap()
    if DEBUG:
        io["dbg_q"] = nc.dram_tensor("dbg_q", [128, SQ], BF16, kind="ExternalOutput").ap()
        io["dbg_k"] = nc.dram_tensor("dbg_k", [128, SK], BF16, kind="ExternalOutput").ap()
        io["dbg_v"] = nc.dram_tensor("dbg_v", [128, NKT * HD], BF16, kind="ExternalOutput").ap()
        io["dbg_exp"] = nc.dram_tensor("dbg_exp", [128, NKT * 512], BF16, kind="ExternalOutput").ap()
        io["dbg_z"] = nc.dram_tensor("dbg_z", [2, 512], F32, kind="ExternalOutput").ap()
        io["dbg_ao"] = nc.dram_tensor("dbg_ao", [128, NCF * SQ], BF16, kind="ExternalOutput").ap()
        io["dbg_res"] = nc.dram_tensor("dbg_res", [SQ, D], F32, kind="ExternalOutput").ap()
        io["dbg_xfm"] = nc.dram_tensor("dbg_xfm", [128, 2048], BF16, kind="ExternalOutput").ap()

    with contextlib.ExitStack() as ctx:
        block = ctx.enter_context(nc.Block())
        k = K(nc, ctx, block)
        persist = {
            "ident_sb": k.sb("ident_sb", [128, 128], BF16),
            "onesr_sb": k.sb("onesr_sb", [1, 128], BF16),
            "onesc_sb": k.sb("onesc_sb", [128, 1], BF16),
            "actc_sb": k.sb("actc_sb", [128, 2], F32),
            "pT": k.ps("pT", [128, 2, 1024], BF16),
        }
        for bi, b in enumerate(blocks):
            if b == "l0attn":
                build_l0_attn(nc, ctx, k, io, persist)
            elif b == "moe":
                if bi == 0:
                    emit_standalone_entry(k, nc, ctx, io, persist, "x1")
                build_moe(nc, ctx, k, io, persist)
            elif b == "l1attn":
                if bi == 0:
                    emit_standalone_entry(k, nc, ctx, io, persist, "x2")
                build_l1attn(nc, ctx, k, io, persist)
            elif b == "l1ffn":
                if bi == 0:
                    emit_standalone_entry(k, nc, ctx, io, persist, "x3")
                build_l1ffn(nc, ctx, k, io, persist)
        emit_store_stream(k, persist["stream"], out)
    return nc


def _prep_l0_inputs(inputs, core):
    b, half = core // 2, core % 2
    T0 = half * SQ
    x = np.asarray(inputs["L0_attn_x"])
    d = {}
    perm = np.concatenate([np.arange(T0, T0 + SQ), np.arange((1 - half) * SQ, (2 - half) * SQ)])
    d["xk"] = np.ascontiguousarray(x[b][perm])
    d["wqT"] = np.ascontiguousarray(np.asarray(inputs["L0_attn_wq"]).T)
    d["wkT"] = np.ascontiguousarray(np.asarray(inputs["L0_attn_wk"]).T)
    d["wvT"] = np.ascontiguousarray(np.asarray(inputs["L0_attn_wv"]).T)
    d["woT"] = np.ascontiguousarray(np.asarray(inputs["L0_attn_wo"]).T)
    kg = perm.reshape(NKT, 128)
    qj = (np.arange(2)[:, None] * 512 + np.arange(512)[None, :] + T0)
    m = (kg[None, :, :, None] - qj[:, None, None, :]) <= 128
    d["masks"] = np.ascontiguousarray(
        m.astype(bf).reshape(2 * NKT * 128, 512))
    d.update(_prep_consts())
    return d


def _prep_consts():
    return {
        "ident": np.eye(128, dtype=bf),
        "actc": np.tile(np.array([[ATT_SCALE, -EXP_SHIFT]], np.float32), (128, 1)),
        "onesr": np.ones((1, 128), dtype=bf),
        "onesc": np.ones((128, 1), dtype=bf),
    }


def _prep_l1attn_inputs(inputs, core, x2=None):
    d = {}
    b, half = core // 2, core % 2
    if x2 is not None:
        d["x2"] = np.ascontiguousarray(x2[b, half * SQ:(half + 1) * SQ])
        d["x2o"] = np.ascontiguousarray(x2[b, (1 - half) * SQ:(2 - half) * SQ])
    g = float(np.asarray(inputs["L1_attn_gamma"]).reshape(-1)[0])
    d["gammac"] = np.full((128, 1), g, np.float32)
    d.update(_prep_consts())
    return d


def _prep_l1ffn_inputs(inputs, core, x3=None):
    d = {}
    b, half = core // 2, core % 2
    if x3 is not None:
        d["x3"] = np.ascontiguousarray(x3[b, half * SQ:(half + 1) * SQ])
    A = np.asarray(inputs["L1_ffn_A"])
    d["A2T"] = np.ascontiguousarray(A.reshape(1024, 2048).T)
    B = np.asarray(inputs["L1_ffn_B"])
    d["B2"] = np.ascontiguousarray(B.transpose(0, 2, 1).reshape(1024, 2048))
    d["WT"] = np.ascontiguousarray(np.asarray(inputs["L1_ffn_W"]).T)
    d["bias"] = np.ascontiguousarray(np.asarray(inputs["L1_ffn_bias"]).reshape(1, -1))
    d.update(_prep_consts())
    return d


def _prep_moe_inputs(inputs, core, x1=None):
    d = {}
    if x1 is not None:
        b, half = core // 2, core % 2
        d["x1"] = np.ascontiguousarray(x1[b, half * SQ:(half + 1) * SQ])
    d["gate_wT"] = np.ascontiguousarray(np.asarray(inputs["L0_ffn_gate_w"]).T)
    d["Wg"] = np.ascontiguousarray(np.asarray(inputs["L0_ffn_Wg"]))
    d["Wu"] = np.ascontiguousarray(np.asarray(inputs["L0_ffn_Wu"]))
    d["Wd"] = np.ascontiguousarray(np.asarray(inputs["L0_ffn_Wd"]))
    d["cWg"] = np.ascontiguousarray(np.asarray(inputs["L0_ffn_cWg"]))
    d["cWu"] = np.ascontiguousarray(np.asarray(inputs["L0_ffn_cWu"]))
    d["cWd"] = np.ascontiguousarray(np.asarray(inputs["L0_ffn_cWd"]))
    d.update(_prep_consts())
    return d


FUSED = int(os.environ.get("KFUSED", "1"))
ALL_BLOCKS = ("l0attn", "moe", "l1attn", "l1ffn")
LAUNCHES = (("l0attn",), ("moe",), ("l1attn",), ("l1ffn",))


def _run_launch(blocks, in_maps):
    if blocks not in _PROG_CACHE:
        _PROG_CACHE[blocks] = build_program(blocks)
    res = run_bass_kernel_spmd(_PROG_CACHE[blocks], in_maps, core_ids=list(range(8)))
    out = np.empty((4, SK, D), dtype=np.float32)
    for core in range(8):
        b, half = core // 2, core % 2
        out[b, half * SQ:(half + 1) * SQ] = res.results[core]["out"]
    return out


def kernel(**inputs):
    if FUSED:
        in_maps = []
        for core in range(8):
            d = _prep_l0_inputs(inputs, core)
            d.update(_prep_moe_inputs(inputs, core))
            d.update(_prep_l1attn_inputs(inputs, core))
            d.update(_prep_l1ffn_inputs(inputs, core))
            in_maps.append(d)
        return _run_launch(ALL_BLOCKS, in_maps)
    x = None
    for blocks in LAUNCHES:
        if blocks == ("l0attn",):
            in_maps = [_prep_l0_inputs(inputs, core) for core in range(8)]
        elif blocks == ("moe",):
            in_maps = [_prep_moe_inputs(inputs, core, x1=x) for core in range(8)]
        elif blocks == ("l1attn",):
            in_maps = [_prep_l1attn_inputs(inputs, core, x2=x) for core in range(8)]
        elif blocks == ("l1ffn",):
            in_maps = [_prep_l1ffn_inputs(inputs, core, x3=x) for core in range(8)]
        x = _run_launch(blocks, in_maps)
    return x
